# version 19
# baseline (speedup 1.0000x reference)
import numpy as np
import ml_dtypes
import concourse.bass as bass
import concourse.mybir as mybir
from concourse.bass_utils import run_bass_kernel_spmd
from contextlib import ExitStack

F32 = mybir.dt.float32
BF16 = mybir.dt.bfloat16
AF = mybir.ActivationFunctionType
ALU = mybir.AluOpType
AX = mybir.AxisListType

D = 1024
T = 2048
H = 16
NE = 8
NC = 4
NTT = 16
BF = ml_dtypes.bfloat16


class Sem:
    def __init__(self, ctx, nc, name):
        self.h = ctx.enter_context(nc.semaphore(name))
        self.n = 0


class Prog:

    def __init__(self, nc):
        self.nc = nc
        self.ops = {k: [] for k in ("tensor", "vector", "scalar", "gpsimd", "sync")}

    def add(self, eng, fn):
        self.ops[eng].append(fn)

    def wait(self, eng, sem, val):
        if val > 0:
            self.add(eng, lambda e, s=sem.h, v=val: e.wait_ge(s, v))

    def run(self):
        def mk(lst):
            def body(eng):
                for f in lst:
                    f(eng)
            return body
        with self.nc.Block() as blk:
            for nm, deco in (("tensor", blk.tensor), ("vector", blk.vector),
                             ("scalar", blk.scalar), ("gpsimd", blk.gpsimd),
                             ("sync", blk.sync)):
                if self.ops[nm]:
                    deco(mk(self.ops[nm]))
        self.ops = {k: [] for k in self.ops}

    def mm(self, pe, out, l, r, start, stop, tp=None, inc=False):
        if inc:
            pe.n += 1

        def f(e, out=out, l=l, r=r, start=start, stop=stop, tp=tp, inc=inc, s=pe.h):
            ins = e.matmul(out, l, r, start=start, stop=stop, tile_position=tp)
            if inc:
                ins.then_inc(s, 1)
        self.add("tensor", f)

    def trans(self, pe, out, in_, ident, inc=False):
        if inc:
            pe.n += 1

        def f(e, out=out, in_=in_, ident=ident, inc=inc, s=pe.h):
            ins = e.transpose(out, in_, ident)
            if inc:
                ins.then_inc(s, 1)
        self.add("tensor", f)

    def act(self, sm, out, in_, func, bias=0.0, scale=1.0, accum=None, inc=False):
        if inc:
            sm.n += 1

        def f(e, out=out, in_=in_, func=func, bias=bias, scale=scale, accum=accum,
              inc=inc, s=sm.h):
            ins = e.activation(out, in_, func, bias=bias, scale=scale,
                               accum_out=accum)
            if inc:
                ins.then_inc(s, 1)
        self.add("scalar", f)

    def scopy(self, sm, out, in_, inc=False):
        if inc:
            sm.n += 1

        def f(e, out=out, in_=in_, inc=inc, s=sm.h):
            ins = e.copy(out, in_)
            if inc:
                ins.then_inc(s, 1)
        self.add("scalar", f)

    def vop(self, dv, name, *args, inc=False, **kw):
        if inc:
            dv.n += 1

        def f(e, name=name, args=args, kw=kw, inc=inc, s=dv.h):
            ins = getattr(e, name)(*args, **kw)
            if inc:
                ins.then_inc(s, 1)
        self.add("vector", f)

    def dma(self, eng, sem, out, in_):
        if sem is None:
            s = self.dpool[self.dpi % len(self.dpool)]
            self.dpi += 1
            s.n += 16

            def f(e, out=out, in_=in_, sh=s.h):
                e.dma_start(out, in_).then_inc(sh, 16)
            self.add(eng, f)
            return (s, s.n)
        sem.n += 16

        def f(e, out=out, in_=in_, s=sem.h):
            e.dma_start(out, in_).then_inc(s, 16)
        self.add(eng, f)

    def waitt(self, eng, tkt):
        if tkt is None:
            return
        if isinstance(tkt, list):
            for t in tkt:
                self.waitt(eng, t)
            return
        self.wait(eng, tkt[0], tkt[1])


def fm_src(dram, rows, cols, off_elems=0, ccnt=None):
    return bass.AP(dram, off_elems,
                   [[cols, 128], [128 * cols, rows // 128], [1, ccnt or cols]])


def build_program(dumps=()):
    nc = bass.Bass("TRN2", target_bir_lowering=False)

    def din(name, shape, dt=F32):
        return nc.dram_tensor(name, list(shape), dt, kind="ExternalInput")

    xT = din("xT", (D, T))
    wqet = din("wqet", (D, 512))
    wqot = din("wqot", (D, 512))
    wket = din("wket", (D, 512))
    wkot = din("wkot", (D, 512))
    wvt = din("wvt", (D, D))
    wotq = din("wotq", (4, D, 256))
    rwp = din("rwp", (128, 64))
    w1h = din("w1h", (NE, 2, D, 512))
    w2d = din("w2d", (8, NE * D, 128))
    sguq = din("sguq", (4, D, 512))
    sdd = din("sdd", (8, D, 128))
    sgwp = din("sgwp", (128, 8))
    cosd = din("cosd", (128, T), BF16)
    sind = din("sind", (128, T), BF16)
    maskd = din("maskd", (128, 896), BF16)
    idfd = din("idfd", (128, 128))
    idbd = din("idbd", (128, 128), BF16)

    wotc = nc.dram_tensor("wotc", [4, 128, 8 * 256], BF16)
    w1c = nc.dram_tensor("w1c", [NE * 2, 128, 8 * 512], BF16)
    w2c = nc.dram_tensor("w2c", [16, 128, 32 * 128], BF16)
    sguc = nc.dram_tensor("sguc", [4, 128, 8 * 512], BF16)
    sddc = nc.dram_tensor("sddc", [8, 128, 8 * 128], BF16)
    out = nc.dram_tensor("out", [T, D], F32, kind="ExternalOutput")
    ddump = {}
    for nm, shp, ddt in (("d_rq", [128, 8 * T], BF16), ("d_rk", [128, 8 * T], BF16),
                         ("d_vaug", [128, 16 * 1040], BF16),
                         ("d_aout", [128, 8 * 512], BF16),
                         ("d_y1", [128, 8 * T], F32),
                         ("d_hg", [128, 32 * 512], BF16),
                         ("d_comb", [8, 512], BF16),
                         ("d_ctm", [128, 8], F32),
                         ("d_probs", [128, 8], F32),
                         ("d_mt1", [128, 8], F32),
                         ("d_spre", [128, 8 * 512], BF16),
                         ("d_accs", [128, 8 * 512], F32),
                         ("d_grow", [1, 512], F32),
                         ("d_z2", [128, 8 * T], F32)):
        if nm in dumps:
            ddump[nm] = nc.dram_tensor(nm, shp, ddt, kind="ExternalOutput")

    ctx = ExitStack()
    with ctx:
        ld = Sem(ctx, nc, "ld")
        ld2 = Sem(ctx, nc, "ld2")
        ld3 = Sem(ctx, nc, "ld3")
        wcs = Sem(ctx, nc, "wcs")
        pe = Sem(ctx, nc, "pe")
        acs = Sem(ctx, nc, "acs")
        dve = Sem(ctx, nc, "dve")
        st = Sem(ctx, nc, "st")

        P01 = ctx.enter_context(nc.psum_tensor("P01", [128, 1024], F32))
        P23 = ctx.enter_context(nc.psum_tensor("P23", [128, 1024], F32))
        P4_7 = [ctx.enter_context(nc.psum_tensor(f"P{i}", [128, 512], F32))
                for i in range(4, 8)]
        P = [P01[:, 0:512], P01[:, 512:1024], P23[:, 0:512], P23[:, 512:1024],
             P4_7[0][:, :], P4_7[1][:, :], P4_7[2][:, :], P4_7[3][:, :]]
        PG = [P01, P23]

        onesf = ctx.enter_context(nc.sbuf_tensor("onesf", [128, 128], F32))
        onesb = ctx.enter_context(nc.sbuf_tensor("onesb", [128, 128], BF16))
        idf = ctx.enter_context(nc.sbuf_tensor("idf", [128, 128], F32))
        idb = ctx.enter_context(nc.sbuf_tensor("idb", [128, 128], BF16))
        epsb = ctx.enter_context(nc.sbuf_tensor("epsb", [128, 1], F32))
        neg30 = ctx.enter_context(nc.sbuf_tensor("neg30", [128, 1], F32))

        p = Prog(nc)
        p.dpool = [Sem(ctx, nc, f"dp{i}") for i in range(32)]
        p.dpi = 0

        with ExitStack() as abctx:
            rq = abctx.enter_context(nc.sbuf_tensor("rq", [128, 8 * T], BF16))
            rk = abctx.enter_context(nc.sbuf_tensor("rk", [128, 8 * T], BF16))
            vaug = abctx.enter_context(nc.sbuf_tensor("vaug", [128, 16 * 1040], BF16))

            with ExitStack() as a2:
                xTbf = a2.enter_context(nc.sbuf_tensor("xTbf", [128, 8 * T], BF16))
                cosb = a2.enter_context(nc.sbuf_tensor("cosb", [128, T], BF16))
                sinb = a2.enter_context(nc.sbuf_tensor("sinb", [128, T], BF16))
                wqk = [a2.enter_context(nc.sbuf_tensor(f"wqk{i}", [128, 8 * 512], BF16))
                       for i in range(2)]
                wvb = a2.enter_context(nc.sbuf_tensor("wvb", [128, 8 * 1024], BF16))
                tmp1 = a2.enter_context(nc.sbuf_tensor("tmpa", [128, 512], BF16))
                tmp2 = a2.enter_context(nc.sbuf_tensor("tmpb", [128, 512], BF16))
                ebuf = a2.enter_context(nc.sbuf_tensor("ebuf", [128, 2 * 512], BF16))
                obuf = a2.enter_context(nc.sbuf_tensor("obuf", [128, 2 * 512], BF16))

                wqk2 = [a2.enter_context(
                    nc.sbuf_tensor(f"wqk2{i}", [128, 8 * 512], BF16))
                    for i in range(2)]
                ld_qeo = [p.dma("gpsimd", None, wqk[0][:, :], fm_src(wqet, D, 512)),
                          p.dma("gpsimd", None, wqk[1][:, :], fm_src(wqot, D, 512))]
                ld_xc = [p.dma("gpsimd", None,
                               bass.AP(xTbf.ap().tensor, cc * 512,
                                       [[8 * T, 128], [T, 8], [1, 512]]),
                               bass.AP(xT, cc * 512,
                                       [[T, 128], [128 * T, 8], [1, 512]]))
                         for cc in range(NC)]
                ld_k = [p.dma("gpsimd", None, wqk2[0][:, :], fm_src(wket, D, 512)),
                        p.dma("gpsimd", None, wqk2[1][:, :], fm_src(wkot, D, 512))]
                ld_v = p.dma("gpsimd", None, wvb[:, :], fm_src(wvt, D, D))
                ld_cs = [p.dma("sync", None, cosb[:, :], fm_src(cosd, 128, T)),
                         p.dma("sync", None, sinb[:, :], fm_src(sind, 128, T))]
                p.add("gpsimd", lambda g: g.memset(onesf[:, :], 1.0))
                p.add("gpsimd", lambda g: g.memset(onesb[:, :], 1.0))
                p.add("gpsimd", lambda g: g.memset(epsb[:, :], 1e-5))
                p.add("gpsimd", lambda g: g.memset(neg30[:, :], -1e30))
                for tt in range(NTT):
                    p.vop(dve, "memset",
                          vaug[:, tt * 1040 + 64:(tt + 1) * 1040:65], 1.0)

                it_dve_done = []
                it_evac_act = []
                itg = 0
                for wi, rT in enumerate((rq, rk)):
                    ld_pair = ld_qeo if wi == 0 else ld_k
                    wq = wqk if wi == 0 else wqk2
                    for m in range(4):
                        for c in range(NC):
                            G = PG[itg % 2]
                            if itg >= 2:
                                p.wait("tensor", acs, it_evac_act[itg - 2])
                            if m == 0 and c == 0:
                                p.waitt("tensor", ld_pair)
                            if wi == 0 and m == 0:
                                p.waitt("tensor", ld_xc[c])
                            if itg == 0:
                                p.waitt("vector", ld_cs)
                            for half in range(2):
                                for i in range(8):
                                    p.mm(pe, G[:, half * 512:half * 512 + 512],
                                         wq[half][:, i * 512 + m * 128:
                                                  i * 512 + m * 128 + 128],
                                         xTbf[:, i * T + c * 512:
                                              i * T + c * 512 + 512],
                                         start=(i == 0), stop=(i == 7),
                                         inc=(i == 7 and half == 1))
                            mm_done = pe.n
                            eb = ebuf[:, (itg % 2) * 512:(itg % 2) * 512 + 512]
                            ob = obuf[:, (itg % 2) * 512:(itg % 2) * 512 + 512]
                            p.wait("scalar", pe, mm_done)
                            if itg >= 2:
                                p.wait("scalar", dve, it_dve_done[itg - 2])
                            p.scopy(acs, eb, G[:, 0:512])
                            p.scopy(acs, ob, G[:, 512:1024], inc=True)
                            it_evac_act.append(acs.n)
                            cs = (slice(None),
                                  slice(c * 512, c * 512 + 512))
                            c0 = slice((2 * m) * T + c * 512,
                                       (2 * m) * T + c * 512 + 512)
                            c1 = slice((2 * m + 1) * T + c * 512,
                                       (2 * m + 1) * T + c * 512 + 512)
                            p.wait("vector", acs, acs.n)
                            p.vop(dve, "tensor_mul", tmp1[:, :], eb, cosb[cs])
                            p.vop(dve, "tensor_mul", tmp2[:, :], ob, sinb[cs])
                            p.vop(dve, "tensor_sub", rT[0:64, c0],
                                  tmp1[0:64, :], tmp2[0:64, :])
                            p.vop(dve, "tensor_sub", rT[0:64, c1],
                                  tmp1[64:128, :], tmp2[64:128, :])
                            p.vop(dve, "tensor_mul", tmp1[:, :], eb, sinb[cs])
                            p.vop(dve, "tensor_mul", tmp2[:, :], ob, cosb[cs])
                            p.vop(dve, "tensor_add", rT[64:128, c0],
                                  tmp1[0:64, :], tmp2[0:64, :])
                            p.vop(dve, "tensor_add", rT[64:128, c1],
                                  tmp1[64:128, :], tmp2[64:128, :],
                                  inc=True)
                            it_dve_done.append(dve.n)
                            itg += 1

                v_act = []
                vi = 0
                for tt in range(NTT):
                    for oc in range(2):
                        bk = 4 + (vi % 2)
                        if vi >= 2:
                            p.wait("tensor", acs, v_act[vi - 2])
                        if vi == 0:
                            p.waitt("tensor", ld_v)
                        for i in range(8):
                            p.mm(pe, P[bk][:, :],
                                 xTbf[:, i * T + tt * 128: i * T + tt * 128 + 128],
                                 wvb[:, i * 1024 + oc * 512:
                                     i * 1024 + oc * 512 + 512],
                                 start=(i == 0), stop=(i == 7), inc=(i == 7))
                        p.wait("scalar", pe, pe.n)
                        dst = bass.AP(vaug.ap().tensor, tt * 1040 + oc * 520,
                                      [[16 * 1040, 128], [65, 8], [1, 64]])
                        src = P[bk][:, 0:512].rearrange("p (h d) -> p h d", h=8)
                        p.scopy(acs, dst, src, inc=True)
                        v_act.append(acs.n)
                        vi += 1
                if "d_rq" in ddump or "d_rk" in ddump:
                    p.wait("sync", dve, dve.n)
                    for nm, tsr in (("d_rq", rq), ("d_rk", rk)):
                        if nm in ddump:
                            p.dma("sync", st, ddump[nm][:, :], tsr[:, :])
                if "d_vaug" in ddump:
                    p.wait("sync", acs, acs.n)
                    p.dma("sync", st, ddump["d_vaug"][:, :], vaug[:, :])
                p.run()
            a_pe, a_act, a_dve = pe.n, acs.n, dve.n

            y1f = ctx.enter_context(
                nc.sbuf_tensor("y1f", [128, 8 * T], F32, side="right"))

            with ExitStack() as bctx:
                wob = bctx.enter_context(nc.sbuf_tensor("wob", [128, 8 * 256], BF16))
                aout = bctx.enter_context(nc.sbuf_tensor("aout", [128, 8 * 512], BF16))
                exps = bctx.enter_context(nc.sbuf_tensor("exps", [128, 4 * 512], BF16))
                rrow = bctx.enter_context(nc.sbuf_tensor("rrow", [128, 512], F32))
                nbs0 = bctx.enter_context(nc.sbuf_tensor("nbs0", [128, 512], F32))
                maskb = bctx.enter_context(nc.sbuf_tensor("maskb", [128, 896], BF16))
                xf32 = bctx.enter_context(nc.sbuf_tensor("xf32", [128, 2 * 512], F32))
                z1bf = bctx.enter_context(nc.sbuf_tensor("z1bf", [128, 2 * 512], BF16))
                zsqb = bctx.enter_context(nc.sbuf_tensor("zsqb", [128, 2 * 512], BF16))
                stat = [bctx.enter_context(nc.sbuf_tensor(f"stat{i}", [128, 512], F32))
                        for i in range(3)]
                stat.append(stat[1])
                wstage = [bctx.enter_context(
                    nc.sbuf_tensor(f"wst{i}", [128, 1024], BF16)) for i in range(2)]
                rqz = bctx.enter_context(
                    nc.sbuf_tensor("rqz", [128, 2 * 1024], BF16))

                ld_msk = [p.dma("sync", None, maskb[:, :], maskd[:, :]),
                          p.dma("sync", None, idf[:, :], idfd[:, :]),
                          p.dma("sync", None, idb[:, :], idbd[:, :])]
                wo_ld = {}
                wot_st = {}
                cast_acts = {}
                xs_dve = [a_dve] * 2
                xs_ld = {}

                def x_slice_src(c, odt):
                    return bass.AP(xT, odt * 128 * T + c * 512,
                                   [[T, 128], [1, 512]])

                pieces = []
                for oh in range(4):
                    for qi in range(2):
                        pieces.append((bass.AP(wotq,
                                               oh * D * 256 + qi * 4 * 128 * 256,
                                               [[256, 128], [128 * 256, 4],
                                                [1, 256]]),
                                       wotc[oh, :, qi * 1024:(qi + 1) * 1024],
                                       "wot", oh))
                for key in range(16):
                    for qi in range(4):
                        pieces.append((bass.AP(w1h, key * D * 512 + qi * 2 * 128 * 512,
                                               [[512, 128], [128 * 512, 2], [1, 512]]),
                                       w1c[key, :, qi * 1024:(qi + 1) * 1024],
                                       "w1", key))
                for key in range(16):
                    eg_, dt_ = key // 8, key % 8
                    off0 = dt_ * (NE * D) * 128 + eg_ * 4096 * 128
                    for qi in range(4):
                        pieces.append((bass.AP(w2d, off0 + qi * 8 * 128 * 128,
                                               [[128, 128], [128 * 128, 8], [1, 128]]),
                                       w2c[key, :, qi * 1024:(qi + 1) * 1024],
                                       "w2", key))
                for key in range(4):
                    for qi in range(4):
                        pieces.append((bass.AP(sguq, key * D * 512 + qi * 2 * 128 * 512,
                                               [[512, 128], [128 * 512, 2], [1, 512]]),
                                       sguc[key, :, qi * 1024:(qi + 1) * 1024],
                                       "sgu", key))
                for key in range(8):
                    pieces.append((bass.AP(sdd, key * D * 128,
                                           [[128, 128], [128 * 128, 8], [1, 128]]),
                                   sddc[key, :, :], "sdd", key))
                pf_st = {}
                pf_slot = [None, None]
                pf_i = [0]

                def emit_pieces(n):
                    while n > 0 and pf_i[0] < len(pieces):
                        srcap, dstap, strm, key = pieces[pf_i[0]]
                        sl = pf_i[0] % 2
                        p.waitt("gpsimd", pf_slot[sl])
                        t_ld = p.dma("gpsimd", None,
                                     wstage[sl][:, 0:dstap.shape[-1]], srcap)
                        p.waitt("sync", t_ld)
                        t_st = p.dma("sync", None, dstap,
                                     wstage[sl][:, 0:dstap.shape[-1]])
                        pf_slot[sl] = t_st
                        pf_st.setdefault((strm, key), []).append(t_st)
                        pf_i[0] += 1
                        n -= 1
                exp_act = {}
                exp_rd = {}
                msk_dve = {}
                wo_free = [a_dve, a_dve]
                av_free = {0: a_dve, 1: a_dve, 2: a_dve}
                nbs_rd = [a_dve, a_dve]
                bcast_pe = [a_pe, a_pe]
                p7_dve = a_dve
                ln_dve = a_dve
                xz_dve = [a_dve] * NC
                wob_rd_pe = 0
                stats_pe = a_pe
                stats_pe_slot = [a_pe, a_pe]
                stats_q = []
                stats_pe_slot = [a_pe, a_pe]
                stats_q = []
                ip = 0
                gg = 0
                pending_norm = []
                pending_norm2 = []
                mr_dve_prev = a_dve
                gpend = [None]
                flush_av_last = None
                p.vop(dve, "memset", rqz[:, :], 0.0, inc=True)
                rqz_dve = {}
                rqz_free_pe = [a_pe, a_pe]

                def build_rqz(j):
                    if j >= 32:
                        return
                    cj, prj = j // 8, j % 8
                    sl = j % 2
                    p.wait("vector", pe, rqz_free_pe[sl])
                    base = prj * T + cj * 512
                    for hi, (r0a, r0b) in enumerate(((0, 64), (32, 96))):
                        o = sl * 1024 + hi * 512
                        p.vop(dve, "tensor_copy", rqz[r0a:r0a + 32, o:o + 512],
                              rq[r0a:r0a + 32, base:base + 512])
                        p.vop(dve, "tensor_copy", rqz[r0b:r0b + 32, o:o + 512],
                              rq[r0b:r0b + 32, base:base + 512],
                              inc=(hi == 1))
                    rqz_dve[j] = dve.n
                for c in range(NC):
                    nmt = 4 * c + 4
                    for odt in range(2):
                        p.wait("sync", dve, xs_dve[odt])
                        xs_ld[(c, odt)] = p.dma(
                            "sync", None, xf32[:, odt * 512:(odt + 1) * 512],
                            x_slice_src(c, odt))
                    if c > 0:
                        p.wait("tensor", dve, mr_dve_prev)

                    def emit_norm_s1(ent):
                        h0n, bksn, dd = ent
                        recs = [0, 0]
                        for hi in range(2):
                            avb = P4_7[bksn[hi]]
                            rr = rrow[64 + 32 * hi:65 + 32 * hi, :]
                            p.wait("scalar", pe, dd["av_done"][hi])
                            p.wait("scalar", pe, bcast_pe[hi])
                            p.act(acs, rr, avb[64:65, :], AF.Ln, inc=True)
                            p.wait("scalar", acs, acs.n)
                            p.act(acs, rr, rr, AF.Exp, scale=-1.0, inc=True)
                            recs[hi] = acs.n
                        pending_norm2.append((h0n, bksn, recs))

                    def emit_norm_s2(ent2):
                        nonlocal p7_dve
                        h0n, bksn, recs = ent2
                        un = h0n // 2
                        for hi in range(2):
                            rr = rrow[64 + 32 * hi:65 + 32 * hi, :]
                            p.wait("tensor", acs, recs[hi])
                            p.wait("tensor", dve, p7_dve)
                            p.mm(pe, P[7][64 * hi:64 * hi + 64, :],
                                 onesf[64 + 32 * hi:65 + 32 * hi, 0:64],
                                 rr, start=True, stop=True,
                                 tp=(64 + 32 * hi, 64 * hi), inc=True)
                            bcast_pe[hi] = pe.n
                        for hi in range(2):
                            avb = P4_7[bksn[hi]]
                            p.wait("vector", pe, bcast_pe[hi])
                            p.vop(dve, "tensor_copy",
                                  nbs0[hi * 64:hi * 64 + 64, :],
                                  P[7][64 * hi:64 * hi + 64, :], inc=True)
                            p.vop(dve, "tensor_mul",
                                  aout[hi * 64:hi * 64 + 64,
                                       un * 512:un * 512 + 512],
                                  avb[0:64, :],
                                  nbs0[hi * 64:hi * 64 + 64, :], inc=True)
                            av_free[bksn[hi]] = dve.n
                            nbs_rd[hi] = dve.n
                        p7_dve = dve.n
                    for pp in range(4):
                        for sub in range(2):
                            h0 = 4 * pp + 2 * sub
                            av_done = [0, 0]
                            bks = [(2 * ip) % 3, (2 * ip + 1) % 3]
                            def flush_av(pend):
                                nonlocal flush_av_last
                                pgg, pmt, pdiag, ph0, pbks, avd, pnmt = pend
                                for hi in range(2):
                                    h = ph0 + hi
                                    if pdiag:
                                        p.wait("tensor", dve, msk_dve[pgg])
                                    else:
                                        p.wait("tensor", acs, exp_act[pgg])
                                    avb = P4_7[pbks[hi]]
                                    if pmt == 0:
                                        p.wait("tensor", dve, av_free[pbks[hi]])
                                    slh = exps[:, (pgg % 2) * 1024 + hi * 512:
                                               (pgg % 2) * 1024 + hi * 512 + 512]
                                    p.mm(pe, avb[0:65, :],
                                         vaug[:, pmt * 1040 + 65 * h:
                                              pmt * 1040 + 65 * h + 65],
                                         slh, start=(pmt == 0),
                                         stop=(pmt == pnmt - 1),
                                         inc=(pmt == pnmt - 1))
                                    if pmt == pnmt - 1:
                                        avd[hi] = pe.n
                                exp_rd[pgg] = pe.n
                            flush_av_last = flush_av
                            if ip == 0:
                                build_rqz(0)
                            build_rqz(ip + 1)
                            pr = h0 // 2
                            for mt in range(nmt):
                                G = PG[gg % 2]
                                if gg >= 2:
                                    p.wait("tensor", acs, exp_act[gg - 2])
                                if mt == 0:
                                    p.wait("tensor", dve, rqz_dve[ip])
                                lo = pr * T + mt * 128
                                zsl = (ip % 2) * 1024
                                for hi in range(2):
                                    half = G[:, hi * 512:(hi + 1) * 512]
                                    p.mm(pe, half, rk[:, lo:lo + 128],
                                         rqz[:, zsl + hi * 512:
                                             zsl + hi * 512 + 512],
                                         start=True, stop=True,
                                         inc=(hi == 1))
                                sc_pe = pe.n
                                p.wait("scalar", pe, sc_pe)
                                if gg >= 2:
                                    p.wait("scalar", pe, exp_rd[gg - 2])
                                sl2 = exps[:, (gg % 2) * 1024:(gg % 2) * 1024 + 1024]
                                p.act(acs, sl2, G[:, :], AF.Exp, scale=0.125,
                                      inc=True)
                                exp_act[gg] = acs.n
                                diag = mt >= 4 * c
                                if diag:
                                    kd = mt - 4 * c
                                    p.wait("vector", acs, exp_act[gg])
                                    if gg == 0:
                                        p.waitt("vector", ld_msk)
                                    for hi in range(2):
                                        slh = exps[:, (gg % 2) * 1024 + hi * 512:
                                                   (gg % 2) * 1024 + hi * 512 + 512]
                                        p.vop(dve, "tensor_mul", slh, slh,
                                              maskb[:, 384 - 128 * kd:
                                                    384 - 128 * kd + 512],
                                              inc=(hi == 1))
                                    msk_dve[gg] = dve.n
                                if mt == 1:
                                    while pending_norm2:
                                        emit_norm_s2(pending_norm2.pop(0))
                                if gpend[0] is not None:
                                    flush_av(gpend[0])
                                    gpend[0] = None
                                if mt == 0:
                                    while pending_norm:
                                        emit_norm_s1(pending_norm.pop(0))
                                gpend[0] = (gg, mt, diag, h0, bks, av_done, nmt)
                                gg += 1
                            pending_norm.append((h0, bks, dict(av_done=av_done)))
                            emit_pieces(5)
                            rqz_free_pe[ip % 2] = sc_pe
                            ip += 1
                    if gpend[0] is not None:
                        flush_av_last(gpend[0])
                        gpend[0] = None
                    while pending_norm:
                        emit_norm_s1(pending_norm.pop(0))
                    while pending_norm2:
                        emit_norm_s2(pending_norm2.pop(0))
                    aout_dve, aout_act = dve.n, acs.n

                    for oh in range(4):
                        p.wait("sync", pe, wob_rd_pe)
                        if c == 0:
                            p.waitt("sync", pf_st[("wot", oh)])
                        wo_ld[(c, oh)] = p.dma("sync", None, wob[:, :],
                                               wotc[oh, :, :])
                        for odt in range(2 * oh, 2 * oh + 2):
                            bk = odt % 2
                            if odt == 0:
                                p.wait("tensor", dve, aout_dve)
                                p.wait("tensor", acs, aout_act)
                            if odt % 2 == 0:
                                p.waitt("tensor", wo_ld[(c, oh)])
                            if odt >= 2:
                                p.wait("tensor", dve, wo_free[bk])
                            for i in range(8):
                                p.mm(pe, P[bk][:, :],
                                     wob[:, i * 256 + (odt % 2) * 128:
                                         i * 256 + (odt % 2) * 128 + 128],
                                     aout[:, i * 512: i * 512 + 512],
                                     start=(i == 0), stop=(i == 7), inc=(i == 7))
                            wo_pe = pe.n
                            ysl = y1f[:, odt * T + c * 512: odt * T + c * 512 + 512]
                            p.wait("vector", pe, wo_pe)
                            p.waitt("vector", xs_ld[(c, odt)])
                            sx = odt % 2
                            p.vop(dve, "tensor_add", ysl, P[bk][:, :],
                                  xf32[:, sx * 512: sx * 512 + 512], inc=True)
                            wo_free[bk] = dve.n
                            xs_dve[sx] = dve.n
                            if odt < 6:
                                p.wait("sync", dve, xs_dve[sx])
                                xs_ld[(c, odt + 2)] = p.dma(
                                    "sync", None,
                                    xf32[:, sx * 512:(sx + 1) * 512],
                                    x_slice_src(c, odt + 2))
                            zb = z1bf[:, (odt % 2) * 512:(odt % 2) * 512 + 512]
                            qb = zsqb[:, (odt % 2) * 512:(odt % 2) * 512 + 512]
                            p.wait("scalar", dve, wo_free[bk])
                            p.wait("scalar", pe, stats_pe_slot[odt % 2])
                            p.scopy(acs, zb, ysl)
                            p.act(acs, qb, zb, AF.Square, inc=True)
                            cast_acts[odt] = acs.n
                            if stats_q:
                                (podt, pzb, pqb, pca) = stats_q.pop(0)
                                p.wait("tensor", acs, pca)
                                p.mm(pe, P[2][0:1, :], onesb[:, 0:1], pzb,
                                     start=(podt == 0), stop=(podt == 7))
                                p.mm(pe, P[3][0:1, :], onesb[:, 0:1], pqb,
                                     start=(podt == 0), stop=(podt == 7),
                                     inc=True)
                                stats_pe_slot[podt % 2] = pe.n
                            stats_q.append((odt, zb, qb, acs.n))
                        wob_rd_pe = pe.n
                    while stats_q:
                        (podt, pzb, pqb, pca) = stats_q.pop(0)
                        p.wait("tensor", acs, pca)
                        p.mm(pe, P[2][0:1, :], onesb[:, 0:1], pzb,
                             start=(podt == 0), stop=(podt == 7))
                        p.mm(pe, P[3][0:1, :], onesb[:, 0:1], pqb,
                             start=(podt == 0), stop=(podt == 7), inc=True)
                        stats_pe_slot[podt % 2] = pe.n
                    stats_pe = pe.n
                    while stats_q:
                        (podt, pzb, pqb, pca) = stats_q.pop(0)
                        p.wait("tensor", acs, pca)
                        p.mm(pe, P[2][0:1, :], onesb[:, 0:1], pzb,
                             start=(podt == 0), stop=(podt == 7))
                        p.mm(pe, P[3][0:1, :], onesb[:, 0:1], pqb,
                             start=(podt == 0), stop=(podt == 7), inc=True)
                        stats_pe_slot[podt % 2] = pe.n
                    stats_pe = pe.n
                    p.wait("vector", pe, stats_pe)
                    p.vop(dve, "tensor_scalar", stat[0][0:1, :], P[2][0:1, :],
                          1.0 / D, None, ALU.mult)
                    p.vop(dve, "tensor_scalar", stat[1][0:1, :], P[3][0:1, :],
                          1.0 / D, None, ALU.mult, inc=True)
                    p.wait("vector", dve, dve.n)
                    p.vop(dve, "tensor_mul", stat[2][0:1, :], stat[0][0:1, :],
                          stat[0][0:1, :], inc=True)
                    p.wait("vector", dve, dve.n)
                    p.vop(dve, "tensor_sub", stat[1][0:1, :], stat[1][0:1, :],
                          stat[2][0:1, :], inc=True)
                    var_dve = dve.n
                    p.wait("scalar", dve, var_dve)
                    p.act(acs, stat[2][0:1, :], stat[1][0:1, :], AF.Ln,
                          bias=epsb[0:1, :], scale=1.0, inc=True)
                    p.wait("scalar", acs, acs.n)
                    p.act(acs, stat[2][0:1, :], stat[2][0:1, :], AF.Exp,
                          scale=-0.5, inc=True)
                    rstd_act = acs.n
                    p.wait("vector", acs, rstd_act)
                    p.vop(dve, "tensor_mul", stat[3][0:1, :], stat[0][0:1, :],
                          stat[2][0:1, :], inc=True)
                    p.wait("vector", dve, dve.n)
                    mr_dve = dve.n
                    p.wait("tensor", dve, mr_dve)
                    p.wait("tensor", dve, av_free[0])
                    p.wait("tensor", dve, av_free[1])
                    p.mm(pe, P[4][:, :], onesf[0:1, 0:128], stat[2][0:1, :],
                         start=True, stop=True)
                    p.mm(pe, P[5][:, :], onesf[0:1, 0:128], stat[3][0:1, :],
                         start=True, stop=True, inc=True)
                    bc2_pe = pe.n
                    for odt in range(8):
                        ysl = y1f[:, odt * T + c * 512: odt * T + c * 512 + 512]
                        p.wait("vector", pe, bc2_pe)
                        p.vop(dve, "tensor_mul", ysl, ysl, P[4][:, :])
                        p.vop(dve, "tensor_sub", ysl, ysl, P[5][:, :],
                              inc=(odt == 7))
                    ln_dve = dve.n
                    av_free[0] = ln_dve
                    av_free[1] = ln_dve
                    mr_dve_prev = mr_dve
                    xz_dve[c] = dve.n
                emit_pieces(len(pieces))
                pf_state = (pf_st,)
                if "d_aout" in ddump:
                    p.wait("sync", dve, dve.n)
                    p.dma("sync", st, ddump["d_aout"][:, :], aout[:, :])
                if "d_y1" in ddump:
                    p.wait("sync", dve, dve.n)
                    p.dma("sync", st, ddump["d_y1"][:, :], y1f[:, :])
                p.run()
        b_dve, b_act, b_pe = dve.n, acs.n, pe.n

        with ExitStack() as cctx:
            y1bf = cctx.enter_context(nc.sbuf_tensor("y1bf", [128, 8 * 512], BF16))
            hgrp = cctx.enter_context(nc.sbuf_tensor("hgrp", [128, 32 * 512], BF16))
            accs = cctx.enter_context(nc.sbuf_tensor("accs", [128, 8 * 512], F32))
            w1b = [cctx.enter_context(nc.sbuf_tensor(f"w1b{i}", [128, 8 * 512], BF16))
                   for i in range(2)]
            w2b = [cctx.enter_context(nc.sbuf_tensor(f"w2b{i}", [128, 32 * 128], BF16))
                   for i in range(2)]
            sgub = [cctx.enter_context(nc.sbuf_tensor(f"sgub{i}", [128, 8 * 512], BF16))
                    for i in range(2)]
            sddb = [cctx.enter_context(nc.sbuf_tensor(f"sddb{i}", [128, 8 * 128], BF16))
                    for i in range(2)]
            sgb = cctx.enter_context(nc.sbuf_tensor("sgb", [128, 8 * 512], BF16))
            spre = cctx.enter_context(nc.sbuf_tensor("spre", [128, 8 * 512], BF16))
            hsil = cctx.enter_context(nc.sbuf_tensor("hsil", [128, 2 * 512], BF16))
            rwb = cctx.enter_context(nc.sbuf_tensor("rwb", [128, 64], BF16))
            sgwb = cctx.enter_context(nc.sbuf_tensor("sgwb", [128, 8], BF16))
            comb = [cctx.enter_context(nc.sbuf_tensor(f"comb{i}", [128, 512], BF16))
                    for i in range(8)]
            probs = cctx.enter_context(nc.sbuf_tensor("probs", [128, 8], F32))
            ctm = cctx.enter_context(nc.sbuf_tensor("ctm", [128, 8], F32))
            mt1 = cctx.enter_context(nc.sbuf_tensor("mt1", [128, 8], F32))
            lt1 = cctx.enter_context(nc.sbuf_tensor("lt1", [128, 8], F32))
            rsc = [cctx.enter_context(nc.sbuf_tensor(f"rsc{i}", [128, 1], F32))
                   for i in range(5)]
            grow = cctx.enter_context(nc.sbuf_tensor("grow", [128, 512], F32))
            tmpc = cctx.enter_context(nc.sbuf_tensor("tmpc", [128, 512], BF16))

            ld_rw = [p.dma("gpsimd", None, rwb[:, :], rwp[:, :]),
                     p.dma("gpsimd", None, sgwb[:, :], sgwp[:, :])]

            w1_free = [b_pe] * 2
            w2_free = [b_pe] * 2
            sgu_free = [b_pe] * 2
            sdd_free = [b_pe] * 2
            w1_st, w2_st, sgu_st, sdd_st = {}, {}, {}, {}
            w1_stslot = [None] * 2
            w2_stslot = [None] * 2
            sgu_stslot = [None] * 2
            sdd_stslot = [None] * 2
            y1bf_pe = b_pe
            routed_dve = [b_dve] * 4
            hs_act = [b_act] * 2
            hsil_rd = [b_dve] * 2
            cbc_rd = [b_dve] * 2
            gu_rd = [("dve", b_dve), ("dve", b_dve)]
            z2_dve = [0] * NC
            gbc_pe_prev = b_pe
            hgrp_w2_pe = b_pe

            for c in range(NC):
                pe_start = pe.n
                if c > 0:
                    p.wait("tensor", dve, z2_dve[c - 1])
                    p.wait("vector", pe, pe_start)
                    p.wait("scalar", dve, z2_dve[c - 1])
                p.wait("scalar", pe, y1bf_pe)
                for dt in range(8):
                    p.scopy(acs, y1bf[:, dt * 512: dt * 512 + 512],
                            y1f[:, dt * T + c * 512: dt * T + c * 512 + 512],
                            inc=(dt == 7))
                y1bf_act = acs.n
                p.wait("tensor", acs, y1bf_act)
                if c == 0:
                    p.waitt("tensor", ld_rw)
                p.wait("tensor", dve, routed_dve[3])
                for ts in range(4):
                    for dt in range(8):
                        p.mm(pe, P[7][:, ts * 8: ts * 8 + 8],
                             y1bf[:, dt * 512 + ts * 128: dt * 512 + ts * 128 + 128],
                             rwb[:, dt * 8: dt * 8 + 8],
                             start=(dt == 0), stop=(dt == 7), inc=(dt == 7))
                rt_pe = pe.n
                t_pe = 0
                for ts in range(4):
                    psl = P[7][:, ts * 8: ts * 8 + 8]
                    p.wait("vector", pe, rt_pe)
                    if ts > 0:
                        p.wait("vector", pe, t_pe)
                    p.vop(dve, "reduce_max", rsc[0][:, :], psl, axis=AX.X,
                          negate=True)
                    p.vop(dve, "reduce_max", rsc[1][:, :], psl, axis=AX.X,
                          inc=True)
                    p.wait("vector", dve, dve.n)
                    p.vop(dve, "tensor_single_scalar", mt1[:, :], psl,
                          rsc[1][:, :], ALU.is_ge, inc=True)
                    m1_dve = dve.n
                    p.wait("vector", dve, m1_dve)
                    p.wait("scalar", dve, m1_dve)
                    p.act(acs, probs[:, :], psl, AF.Exp, bias=rsc[0][:, :],
                          scale=1.0, accum=rsc[2][:, :], inc=True)
                    e_act = acs.n
                    p.wait("vector", acs, e_act)
                    p.vop(dve, "reciprocal", rsc[3][:, :], rsc[2][:, :], inc=True)
                    p.wait("vector", dve, dve.n)
                    p.vop(dve, "scalar_tensor_tensor", lt1[:, :], mt1[:, :],
                          neg30[:, 0:1], psl, ALU.mult, ALU.add, inc=True)
                    p.wait("vector", dve, dve.n)
                    p.vop(dve, "reduce_max", rsc[4][:, :], lt1[:, :], axis=AX.X,
                          inc=True)
                    p.wait("vector", dve, dve.n)
                    p.vop(dve, "tensor_single_scalar", mt1[:, :], psl,
                          rsc[4][:, :], ALU.is_ge, inc=True)
                    p.wait("vector", dve, dve.n)
                    p.vop(dve, "tensor_scalar", probs[:, :], probs[:, :],
                          rsc[3][:, :], None, ALU.mult, inc=True)
                    p.wait("vector", dve, dve.n)
                    p.vop(dve, "tensor_mul", ctm[:, :], probs[:, :], mt1[:, :],
                          inc=True)
                    c_dve = dve.n
                    for eb in range(2):
                        p.wait("tensor", dve, c_dve)
                        p.wait("tensor", acs, acs.n)
                        for e4 in range(4):
                            e = eb * 4 + e4
                            p.trans(pe, P[6][0:1, e4 * 128: e4 * 128 + 128],
                                    ctm[:, e:e + 1], idf[:, :], inc=(e4 == 3))
                        t_pe = pe.n
                        p.wait("scalar", pe, t_pe)
                        for e4 in range(4):
                            e = eb * 4 + e4
                            p.scopy(acs, comb[e][0:1, ts * 128: ts * 128 + 128],
                                    P[6][0:1, e4 * 128: e4 * 128 + 128],
                                    inc=(e4 == 3))
                comb_act = acs.n
                cbc_rd[0] = max(cbc_rd[0], dve.n)
                sgu_ld = {}
                sdd_ld = {}

                def sgudma(q, c=c):
                    slot = q % 2
                    p.wait("scalar", pe, sgu_free[slot])
                    if c == 0:
                        p.waitt("scalar", pf_st[("sgu", q)])
                    sgu_ld[q] = p.dma("scalar", None, sgub[slot][:, :],
                                      sguc[q, :, :])

                def sdddma(dt, c=c):
                    slot = dt % 2
                    p.wait("sync", pe, sdd_free[slot])
                    if c == 0:
                        p.waitt("sync", pf_st[("sdd", dt)])
                    sdd_ld[dt] = p.dma("sync", None, sddb[slot][:, :],
                                       sddc[dt, :, :])

                for eg in range(2):
                    w2_ld = {}

                    def w2dma(dt, eg=eg, c=c):
                        slot = dt % 2
                        key = eg * 8 + dt
                        p.wait("sync", pe, w2_free[slot])
                        if c == 0:
                            p.waitt("sync", pf_st[("w2", key)])
                        w2_ld[dt] = p.dma("sync", None, w2b[slot][:, :],
                                          w2c[key, :, :])
                    w2dma(0)
                    w2dma(1)
                    if eg == 1:
                        sgudma(0)
                        sgudma(1)
                        sdddma(0)
                        sdddma(1)
                        p.wait("vector", pe, hgrp_w2_pe)
                    for ei in range(4):
                        e = 4 * eg + ei
                        w1_lds = {}
                        for fh in range(2):
                            slot = (2 * e + fh) % 2
                            key = 2 * e + fh
                            p.wait("scalar", pe, w1_free[slot])
                            if c == 0:
                                p.waitt("scalar", pf_st[("w1", key)])
                            w1_lds[fh] = p.dma("scalar", None,
                                               w1b[slot][:, :],
                                               w1c[key, :, :])
                        cb = 6 + (e % 2)
                        p.wait("tensor", acs, comb_act)
                        p.wait("tensor", dve, cbc_rd[e % 2])
                        p.mm(pe, P[cb][:, :], onesb[0:1, 0:128],
                             comb[e][0:1, :], start=True, stop=True, inc=True)
                        cbc_mm = pe.n
                        for fh in range(2):
                            slot = (2 * e + fh) % 2
                            p.waitt("tensor", w1_lds[fh])
                            for ft in range(4):
                                f = fh * 4 + ft
                                hb = 4 + (f % 2)
                                p.wait("tensor", acs, hs_act[f % 2])
                                for dt in range(8):
                                    p.mm(pe, P[hb][:, :],
                                         w1b[slot][:, dt * 512 + ft * 128:
                                                   dt * 512 + ft * 128 + 128],
                                         y1bf[:, dt * 512: dt * 512 + 512],
                                         start=(dt == 0), stop=(dt == 7),
                                         inc=(dt == 7))
                                h_pe = pe.n
                                p.wait("scalar", pe, h_pe)
                                p.wait("scalar", dve, hsil_rd[f % 2])
                                hs = hsil[:, (f % 2) * 512:(f % 2) * 512 + 512]
                                p.act(acs, hs, P[hb][:, :], AF.Silu, inc=True)
                                hs_act[f % 2] = acs.n
                                p.wait("vector", acs, hs_act[f % 2])
                                p.wait("vector", pe, cbc_mm)
                                p.vop(dve, "tensor_mul",
                                      hgrp[:, (ei * 8 + f) * 512:
                                           (ei * 8 + f) * 512 + 512],
                                      hs, P[cb][:, :], inc=True)
                                hsil_rd[f % 2] = dve.n
                            w1_free[slot] = pe.n
                        cbc_rd[e % 2] = dve.n
                    hg_dve = dve.n
                    for dt in range(8):
                        slot = dt % 2
                        bk = dt % 4
                        p.waitt("tensor", w2_ld[dt])
                        if dt == 0:
                            p.wait("tensor", dve, hg_dve)
                        p.wait("tensor", dve, routed_dve[bk])
                        for kt in range(32):
                            p.mm(pe, P[bk][:, :],
                                 w2b[slot][:, kt * 128: kt * 128 + 128],
                                 hgrp[:, kt * 512: kt * 512 + 512],
                                 start=(kt == 0), stop=(kt == 31), inc=(kt == 31))
                        w2_free[slot] = pe.n
                        if dt + 2 < 8:
                            w2dma(dt + 2)
                        r_pe = pe.n
                        p.wait("vector", pe, r_pe)
                        asl = accs[:, dt * 512: dt * 512 + 512]
                        ysl = y1f[:, dt * T + c * 512: dt * T + c * 512 + 512]
                        if eg == 0:
                            p.vop(dve, "tensor_add", asl, P[bk][:, :], ysl,
                                  inc=True)
                        else:
                            p.vop(dve, "tensor_add", ysl, P[bk][:, :], asl,
                                  inc=True)
                        routed_dve[bk] = dve.n
                    hgrp_w2_pe = pe.n

                p.wait("tensor", dve, routed_dve[0])
                p.wait("tensor", acs, hs_act[0])
                p.wait("tensor", pe, gbc_pe_prev)
                for dt in range(8):
                    p.mm(pe, P[4][0:1, :], sgwb[:, dt:dt + 1],
                         y1bf[:, dt * 512: dt * 512 + 512],
                         start=(dt == 0), stop=(dt == 7), inc=(dt == 7))
                gw_pe = pe.n
                p.wait("scalar", pe, gw_pe)
                p.scopy(acs, grow[0:1, :], P[4][0:1, :], inc=True)
                p.wait("scalar", acs, acs.n)
                p.act(acs, grow[0:1, :], grow[0:1, :], AF.Sigmoid, inc=True)
                g_act = acs.n
                p.wait("tensor", acs, g_act)
                p.wait("tensor", acs, hs_act[1])
                p.mm(pe, P[5][:, :], onesf[0:1, 0:128], grow[0:1, :],
                     start=True, stop=True, inc=True)
                gbc_pe = pe.n
                gbc_pe_prev = gbc_pe
                sg_act = acs.n
                for q in range(4):
                    slot = q % 2
                    p.waitt("tensor", sgu_ld[q])
                    for ft2 in range(4):
                        f2 = q * 4 + ft2
                        par = f2 % 2
                        gb = 4 if par == 0 else 6
                        if f2 == 0:
                            p.wait("tensor", acs, g_act)
                        if f2 == 1:
                            p.wait("tensor", dve, cbc_rd[0])
                        kind, val = gu_rd[par]
                        if kind == "acs":
                            p.wait("tensor", acs, val)
                        else:
                            p.wait("tensor", dve, val)
                        for dt in range(8):
                            p.mm(pe, P[gb][:, :],
                                 sgub[slot][:, dt * 512 + ft2 * 128:
                                            dt * 512 + ft2 * 128 + 128],
                                 y1bf[:, dt * 512: dt * 512 + 512],
                                 start=(dt == 0), stop=(dt == 7), inc=(dt == 7))
                        gu_pe = pe.n
                        if q < 2:
                            p.wait("scalar", pe, gu_pe)
                            p.act(acs, sgb[:, f2 * 512: f2 * 512 + 512],
                                  P[gb][:, :], AF.Silu, inc=True)
                            sg_act = acs.n
                            gu_rd[par] = ("acs", acs.n)
                        else:
                            fu = f2 - 8
                            p.wait("vector", pe, gu_pe)
                            p.wait("vector", pe, gbc_pe)
                            p.wait("vector", acs, sg_act)
                            p.vop(dve, "tensor_mul", tmpc[:, :], P[gb][:, :],
                                  sgb[:, fu * 512: fu * 512 + 512])
                            p.vop(dve, "tensor_mul",
                                  spre[:, fu * 512: fu * 512 + 512],
                                  tmpc[:, :], P[5][:, :], inc=True)
                            gu_rd[par] = ("dve", dve.n)
                    sgu_free[slot] = pe.n
                    if q + 2 < 4:
                        sgudma(q + 2)
                sp_dve = dve.n
                for dt in range(8):
                    slot = dt % 2
                    bk = dt % 4
                    p.waitt("tensor", sdd_ld[dt])
                    if dt == 0:
                        p.wait("tensor", dve, sp_dve)
                    p.wait("tensor", dve, routed_dve[bk])
                    for ft in range(8):
                        p.mm(pe, P[bk][:, :],
                             sddb[slot][:, ft * 128: ft * 128 + 128],
                             spre[:, ft * 512: ft * 512 + 512],
                             start=(ft == 0), stop=(ft == 7), inc=(ft == 7))
                    sdd_free[slot] = pe.n
                    if dt + 2 < 8:
                        sdddma(dt + 2)
                    d_pe = pe.n
                    p.wait("vector", pe, d_pe)
                    ysl = y1f[:, dt * T + c * 512: dt * T + c * 512 + 512]
                    p.vop(dve, "tensor_add", ysl, P[bk][:, :], ysl, inc=True)
                    routed_dve[bk] = dve.n
                z2_dve[c] = dve.n
                y1bf_pe = pe.n
            if "d_hg" in ddump:
                p.wait("sync", dve, dve.n)
                p.dma("sync", st, ddump["d_hg"][:, :], hgrp[:, :])
            if "d_comb" in ddump:
                p.wait("sync", acs, acs.n)
                for e in range(NE):
                    p.dma("sync", st, ddump["d_comb"][e:e + 1, :], comb[e][0:1, :])
            if "d_ctm" in ddump:
                p.wait("sync", dve, dve.n)
                p.dma("sync", st, ddump["d_ctm"][:, :], ctm[:, :])
                p.dma("sync", st, ddump["d_probs"][:, :], probs[:, :])
                p.dma("sync", st, ddump["d_mt1"][:, :], mt1[:, :])
            if "d_spre" in ddump:
                p.wait("sync", dve, dve.n)
                p.dma("sync", st, ddump["d_spre"][:, :], spre[:, :])
            if "d_accs" in ddump:
                p.dma("sync", st, ddump["d_accs"][:, :], accs[:, :])
            if "d_grow" in ddump:
                p.dma("sync", st, ddump["d_grow"][:, :], grow[0:1, :])
            if "d_z2" in ddump:
                p.wait("sync", dve, dve.n)
                p.dma("sync", st, ddump["d_z2"][:, :], y1f[:, :])
            p.run()

        with ExitStack() as dctx:
            ztm = [dctx.enter_context(nc.sbuf_tensor(f"ztm{i}", [128, 1024], F32))
                   for i in range(2)]
            y2s = [dctx.enter_context(nc.sbuf_tensor(f"y2s{i}", [128, 1024], F32))
                   for i in range(2)]
            sqs = dctx.enter_context(nc.sbuf_tensor("sqs", [128, 1024], BF16))
            dS = dctx.enter_context(nc.sbuf_tensor("dS", [128, 1], F32))
            dS2 = dctx.enter_context(nc.sbuf_tensor("dS2", [128, 1], F32))
            dm = dctx.enter_context(nc.sbuf_tensor("dm", [128, 1], F32))
            dv2 = dctx.enter_context(nc.sbuf_tensor("dv2", [128, 1], F32))
            dr = dctx.enter_context(nc.sbuf_tensor("dr", [128, 1], F32))

            tile_st = [0, 0]
            tr_free = [0, 0]
            for tt in range(NTT):
                sl = tt % 2
                for dt in range(8):
                    bki = dt // 4
                    if tt >= 1 and dt % 4 == 0:
                        p.wait("tensor", acs, tr_free[bki])
                    p.trans(pe, P[4 + bki][:, (dt % 4) * 128:(dt % 4) * 128 + 128],
                            y1f[:, dt * T + tt * 128: dt * T + tt * 128 + 128],
                            idf[:, :], inc=(dt % 4 == 3))
                tr_pe = pe.n
                p.wait("scalar", pe, tr_pe)
                p.wait("scalar", st, tile_st[sl])
                p.scopy(acs, ztm[sl][:, 0:512], P[4][:, :])
                p.scopy(acs, ztm[sl][:, 512:1024], P[5][:, :], inc=True)
                tr_free[0] = acs.n
                tr_free[1] = acs.n
                zt_act = acs.n
                p.wait("vector", acs, zt_act)
                p.vop(dve, "reduce_sum", dS[:, :], ztm[sl][:, :], axis=AX.X,
                      inc=True)
                p.wait("scalar", dve, dve.n)
                p.act(acs, sqs[:, :], ztm[sl][:, :], AF.Square, accum=dS2[:, :],
                      inc=True)
                sq_act = acs.n
                p.wait("vector", acs, sq_act)
                p.vop(dve, "tensor_scalar", dm[:, :], dS[:, :], 1.0 / D, None,
                      ALU.mult)
                p.vop(dve, "tensor_scalar", dv2[:, :], dS2[:, :], 1.0 / D, None,
                      ALU.mult, inc=True)
                p.wait("vector", dve, dve.n)
                p.vop(dve, "tensor_mul", dr[:, :], dm[:, :], dm[:, :], inc=True)
                p.wait("vector", dve, dve.n)
                p.vop(dve, "tensor_sub", dv2[:, :], dv2[:, :], dr[:, :], inc=True)
                p.wait("scalar", dve, dve.n)
                p.act(acs, dr[:, :], dv2[:, :], AF.Ln, bias=epsb[:, :], scale=1.0,
                      inc=True)
                p.wait("scalar", acs, acs.n)
                p.act(acs, dr[:, :], dr[:, :], AF.Exp, scale=-0.5, inc=True)
                r_act = acs.n
                p.wait("vector", acs, r_act)
                p.vop(dve, "tensor_scalar", y2s[sl][:, :], ztm[sl][:, :],
                      dm[:, :], dr[:, :], ALU.subtract, ALU.mult, inc=True)
                y2_dve = dve.n
                p.wait("sync", dve, y2_dve)
                p.dma("sync", st, out[tt * 128:(tt + 1) * 128, :], y2s[sl][:, :])
                tile_st[sl] = st.n
            p.wait("sync", st, st.n)
            p.run()
    return nc


_cache = {}


def _host_prep(L0_attn_Wq, L0_attn_Wk, L0_attn_Wv, L0_attn_Wo, L0_ffn_router_w,
               L0_ffn_w1, L0_ffn_w2, L0_ffn_shared_gate_up, L0_ffn_shared_down,
               L0_ffn_shared_gate_w):
    f = np.float32
    perm_e = np.array([64 * h + 2 * j for h in range(H) for j in range(32)])
    perm_o = perm_e + 1
    Wq = np.asarray(L0_attn_Wq, f)
    Wk = np.asarray(L0_attn_Wk, f)
    d = {
        "wqet": np.ascontiguousarray(Wq[perm_e].T),
        "wqot": np.ascontiguousarray(Wq[perm_o].T),
        "wket": np.ascontiguousarray(Wk[perm_e].T),
        "wkot": np.ascontiguousarray(Wk[perm_o].T),
        "wvt": np.ascontiguousarray(np.asarray(L0_attn_Wv, f).T),
        "wotq": np.ascontiguousarray(
            np.asarray(L0_attn_Wo, f).T.reshape(D, 4, 256).transpose(1, 0, 2)),
        "rwp": np.ascontiguousarray(
            np.asarray(L0_ffn_router_w, f).reshape(8, 128, 8)
            .transpose(1, 0, 2).reshape(128, 64)),
        "w1h": np.ascontiguousarray(
            np.asarray(L0_ffn_w1, f).reshape(NE, D, 2, 512).transpose(0, 2, 1, 3)),
        "w2d": np.ascontiguousarray(
            np.asarray(L0_ffn_w2, f).reshape(NE * D, 8, 128).transpose(1, 0, 2)),
        "sguq": np.ascontiguousarray(
            np.asarray(L0_ffn_shared_gate_up, f).reshape(D, 4, 512)
            .transpose(1, 0, 2)),
        "sdd": np.ascontiguousarray(
            np.asarray(L0_ffn_shared_down, f).reshape(D, 8, 128)
            .transpose(1, 0, 2)),
        "sgwp": np.ascontiguousarray(
            np.asarray(L0_ffn_shared_gate_w, f).reshape(8, 128).T),
        "idfd": np.eye(128, dtype=f),
        "idbd": np.eye(128, dtype=f).astype(BF),
    }
    j = np.arange(32, dtype=np.float64)
    invf = 10000.0 ** (-j / 32.0)
    t = np.arange(T, dtype=np.float64)
    ang = t[None, :] * invf[:, None]
    d["cosd"] = np.ascontiguousarray(np.tile(np.cos(ang), (4, 1))).astype(BF)
    d["sind"] = np.ascontiguousarray(np.tile(np.sin(ang), (4, 1))).astype(BF)
    pp, jj = np.mgrid[0:128, 0:896]
    d["maskd"] = (jj >= pp + 384).astype(f).astype(BF)
    return d


def kernel(L0_attn_x, L0_attn_Wq, L0_attn_Wk, L0_attn_Wv, L0_attn_Wo,
           L0_ffn_router_w, L0_ffn_w1, L0_ffn_w2, L0_ffn_shared_gate_up,
           L0_ffn_shared_down, L0_ffn_shared_gate_w, _dumps=()):
    shared = _host_prep(L0_attn_Wq, L0_attn_Wk, L0_attn_Wv, L0_attn_Wo,
                        L0_ffn_router_w, L0_ffn_w1, L0_ffn_w2,
                        L0_ffn_shared_gate_up, L0_ffn_shared_down,
                        L0_ffn_shared_gate_w)
    x = np.asarray(L0_attn_x, np.float32)
    key = tuple(sorted(_dumps))
    if key not in _cache:
        _cache[key] = build_program(dumps=_dumps)
    nc = _cache[key]
    in_maps = []
    for cidx in range(8):
        m = dict(shared)
        m["xT"] = np.ascontiguousarray(x[cidx].T)
        in_maps.append(m)
    res = run_bass_kernel_spmd(nc, in_maps, core_ids=list(range(8)))
    outs = np.stack([np.asarray(res.results[i]["out"]) for i in range(8)], axis=0)
    if _dumps:
        return outs, res.results
    return outs


# version 20
# speedup vs baseline: 1.0733x; 1.0733x over previous
import numpy as np
import ml_dtypes
import concourse.bass as bass
import concourse.mybir as mybir
from concourse.bass_utils import run_bass_kernel_spmd
from contextlib import ExitStack

F32 = mybir.dt.float32
BF16 = mybir.dt.bfloat16
AF = mybir.ActivationFunctionType
ALU = mybir.AluOpType
AX = mybir.AxisListType

D = 1024
T = 2048
H = 16
NE = 8
NC = 4
NTT = 16
BF = ml_dtypes.bfloat16


class Sem:
    def __init__(self, ctx, nc, name):
        self.h = ctx.enter_context(nc.semaphore(name))
        self.n = 0


class Prog:

    def __init__(self, nc):
        self.nc = nc
        self.ops = {k: [] for k in ("tensor", "vector", "scalar", "gpsimd", "sync")}

    def add(self, eng, fn):
        self.ops[eng].append(fn)

    def wait(self, eng, sem, val):
        if val > 0:
            self.add(eng, lambda e, s=sem.h, v=val: e.wait_ge(s, v))

    def run(self):
        def mk(lst):
            def body(eng):
                for f in lst:
                    f(eng)
            return body
        with self.nc.Block() as blk:
            for nm, deco in (("tensor", blk.tensor), ("vector", blk.vector),
                             ("scalar", blk.scalar), ("gpsimd", blk.gpsimd),
                             ("sync", blk.sync)):
                if self.ops[nm]:
                    deco(mk(self.ops[nm]))
        self.ops = {k: [] for k in self.ops}

    def mm(self, pe, out, l, r, start, stop, tp=None, inc=False):
        if inc:
            pe.n += 1

        def f(e, out=out, l=l, r=r, start=start, stop=stop, tp=tp, inc=inc, s=pe.h):
            ins = e.matmul(out, l, r, start=start, stop=stop, tile_position=tp)
            if inc:
                ins.then_inc(s, 1)
        self.add("tensor", f)

    def trans(self, pe, out, in_, ident, inc=False):
        if inc:
            pe.n += 1

        def f(e, out=out, in_=in_, ident=ident, inc=inc, s=pe.h):
            ins = e.transpose(out, in_, ident)
            if inc:
                ins.then_inc(s, 1)
        self.add("tensor", f)

    def act(self, sm, out, in_, func, bias=0.0, scale=1.0, accum=None, inc=False):
        if inc:
            sm.n += 1

        def f(e, out=out, in_=in_, func=func, bias=bias, scale=scale, accum=accum,
              inc=inc, s=sm.h):
            ins = e.activation(out, in_, func, bias=bias, scale=scale,
                               accum_out=accum)
            if inc:
                ins.then_inc(s, 1)
        self.add("scalar", f)

    def scopy(self, sm, out, in_, inc=False):
        if inc:
            sm.n += 1

        def f(e, out=out, in_=in_, inc=inc, s=sm.h):
            ins = e.copy(out, in_)
            if inc:
                ins.then_inc(s, 1)
        self.add("scalar", f)

    def vop(self, dv, name, *args, inc=False, **kw):
        if inc:
            dv.n += 1

        def f(e, name=name, args=args, kw=kw, inc=inc, s=dv.h):
            ins = getattr(e, name)(*args, **kw)
            if inc:
                ins.then_inc(s, 1)
        self.add("vector", f)

    def dma(self, eng, sem, out, in_):
        if sem is None:
            s = self.dpool[self.dpi % len(self.dpool)]
            self.dpi += 1
            s.n += 16

            def f(e, out=out, in_=in_, sh=s.h):
                e.dma_start(out, in_).then_inc(sh, 16)
            self.add(eng, f)
            return (s, s.n)
        sem.n += 16

        def f(e, out=out, in_=in_, s=sem.h):
            e.dma_start(out, in_).then_inc(s, 16)
        self.add(eng, f)

    def waitt(self, eng, tkt):
        if tkt is None:
            return
        if isinstance(tkt, list):
            for t in tkt:
                self.waitt(eng, t)
            return
        self.wait(eng, tkt[0], tkt[1])


def fm_src(dram, rows, cols, off_elems=0, ccnt=None):
    return bass.AP(dram, off_elems,
                   [[cols, 128], [128 * cols, rows // 128], [1, ccnt or cols]])


def build_program(dumps=()):
    nc = bass.Bass("TRN2", target_bir_lowering=False)

    def din(name, shape, dt=F32):
        return nc.dram_tensor(name, list(shape), dt, kind="ExternalInput")

    xT = din("xT", (D, T))
    wqet = din("wqet", (D, 512))
    wqot = din("wqot", (D, 512))
    wket = din("wket", (D, 512))
    wkot = din("wkot", (D, 512))
    wvt = din("wvt", (D, D))
    wotq = din("wotq", (4, D, 256))
    rwp = din("rwp", (128, 64))
    w1h = din("w1h", (NE, 2, D, 512))
    w2d = din("w2d", (8, NE * D, 128))
    sguq = din("sguq", (4, D, 512))
    sdd = din("sdd", (8, D, 128))
    sgwp = din("sgwp", (128, 8))
    cosd = din("cosd", (128, T), BF16)
    sind = din("sind", (128, T), BF16)
    maskd = din("maskd", (128, 896), BF16)
    idfd = din("idfd", (128, 128))
    idbd = din("idbd", (128, 128), BF16)

    wotc = nc.dram_tensor("wotc", [4, 128, 8 * 256], BF16)
    w1c = nc.dram_tensor("w1c", [NE * 2, 128, 8 * 512], BF16)
    w2c = nc.dram_tensor("w2c", [16, 128, 32 * 128], BF16)
    sguc = nc.dram_tensor("sguc", [4, 128, 8 * 512], BF16)
    sddc = nc.dram_tensor("sddc", [8, 128, 8 * 128], BF16)
    out = nc.dram_tensor("out", [T, D], F32, kind="ExternalOutput")
    ddump = {}
    for nm, shp, ddt in (("d_rq", [128, 8 * T], BF16), ("d_rk", [128, 8 * T], BF16),
                         ("d_vaug", [128, 16 * 1040], BF16),
                         ("d_aout", [128, 8 * 512], BF16),
                         ("d_y1", [128, 8 * T], F32),
                         ("d_hg", [128, 32 * 512], BF16),
                         ("d_comb", [8, 512], BF16),
                         ("d_ctm", [128, 8], F32),
                         ("d_probs", [128, 8], F32),
                         ("d_mt1", [128, 8], F32),
                         ("d_spre", [128, 8 * 512], BF16),
                         ("d_accs", [128, 8 * 512], F32),
                         ("d_grow", [1, 512], F32),
                         ("d_z2", [128, 8 * T], F32)):
        if nm in dumps:
            ddump[nm] = nc.dram_tensor(nm, shp, ddt, kind="ExternalOutput")

    ctx = ExitStack()
    with ctx:
        ld = Sem(ctx, nc, "ld")
        ld2 = Sem(ctx, nc, "ld2")
        ld3 = Sem(ctx, nc, "ld3")
        wcs = Sem(ctx, nc, "wcs")
        pe = Sem(ctx, nc, "pe")
        acs = Sem(ctx, nc, "acs")
        dve = Sem(ctx, nc, "dve")
        st = Sem(ctx, nc, "st")

        P01 = ctx.enter_context(nc.psum_tensor("P01", [128, 1024], F32))
        P23 = ctx.enter_context(nc.psum_tensor("P23", [128, 1024], F32))
        P4_7 = [ctx.enter_context(nc.psum_tensor(f"P{i}", [128, 512], F32))
                for i in range(4, 8)]
        P = [P01[:, 0:512], P01[:, 512:1024], P23[:, 0:512], P23[:, 512:1024],
             P4_7[0][:, :], P4_7[1][:, :], P4_7[2][:, :], P4_7[3][:, :]]
        PG = [P01, P23]

        onesf = ctx.enter_context(nc.sbuf_tensor("onesf", [128, 128], F32))
        onesb = ctx.enter_context(nc.sbuf_tensor("onesb", [128, 128], BF16))
        idf = ctx.enter_context(nc.sbuf_tensor("idf", [128, 128], F32))
        idb = ctx.enter_context(nc.sbuf_tensor("idb", [128, 128], BF16))
        epsb = ctx.enter_context(nc.sbuf_tensor("epsb", [128, 1], F32))
        neg30 = ctx.enter_context(nc.sbuf_tensor("neg30", [128, 1], F32))

        p = Prog(nc)
        p.dpool = [Sem(ctx, nc, f"dp{i}") for i in range(32)]
        p.dpi = 0

        with ExitStack() as abctx:
            rq = abctx.enter_context(nc.sbuf_tensor("rq", [128, 8 * T], BF16))
            rk = abctx.enter_context(nc.sbuf_tensor("rk", [128, 8 * T], BF16))
            vaug = abctx.enter_context(nc.sbuf_tensor("vaug", [128, 16 * 1040], BF16))

            with ExitStack() as a2:
                xTbf = a2.enter_context(nc.sbuf_tensor("xTbf", [128, 8 * T], BF16))
                cosb = a2.enter_context(nc.sbuf_tensor("cosb", [128, T], BF16))
                sinb = a2.enter_context(nc.sbuf_tensor("sinb", [128, T], BF16))
                wqk = [a2.enter_context(nc.sbuf_tensor(f"wqk{i}", [128, 8 * 512], BF16))
                       for i in range(2)]
                wvb = a2.enter_context(nc.sbuf_tensor("wvb", [128, 8 * 1024], BF16))
                tmp1 = a2.enter_context(nc.sbuf_tensor("tmpa", [128, 512], BF16))
                tmp2 = a2.enter_context(nc.sbuf_tensor("tmpb", [128, 512], BF16))
                ebuf = a2.enter_context(nc.sbuf_tensor("ebuf", [128, 2 * 512], BF16))
                obuf = a2.enter_context(nc.sbuf_tensor("obuf", [128, 2 * 512], BF16))

                wqk2 = [a2.enter_context(
                    nc.sbuf_tensor(f"wqk2{i}", [128, 8 * 512], BF16))
                    for i in range(2)]
                ld_qeo = [p.dma("gpsimd", None, wqk[0][:, :], fm_src(wqet, D, 512)),
                          p.dma("gpsimd", None, wqk[1][:, :], fm_src(wqot, D, 512))]
                ld_xc = [p.dma("gpsimd", None,
                               bass.AP(xTbf.ap().tensor, cc * 512,
                                       [[8 * T, 128], [T, 8], [1, 512]]),
                               bass.AP(xT, cc * 512,
                                       [[T, 128], [128 * T, 8], [1, 512]]))
                         for cc in range(NC)]
                ld_k = [p.dma("gpsimd", None, wqk2[0][:, :], fm_src(wket, D, 512)),
                        p.dma("gpsimd", None, wqk2[1][:, :], fm_src(wkot, D, 512))]
                ld_v = p.dma("gpsimd", None, wvb[:, :], fm_src(wvt, D, D))
                ld_cs = [p.dma("sync", None, cosb[:, :], fm_src(cosd, 128, T)),
                         p.dma("sync", None, sinb[:, :], fm_src(sind, 128, T))]
                p.add("gpsimd", lambda g: g.memset(onesf[:, :], 1.0))
                p.add("gpsimd", lambda g: g.memset(onesb[:, :], 1.0))
                p.add("gpsimd", lambda g: g.memset(epsb[:, :], 1e-5))
                p.add("gpsimd", lambda g: g.memset(neg30[:, :], -1e30))
                for tt in range(NTT):
                    p.vop(dve, "memset",
                          vaug[:, tt * 1040 + 64:(tt + 1) * 1040:65], 1.0)

                it_dve_done = []
                it_evac_act = []
                itg = 0
                for wi, rT in enumerate((rq, rk)):
                    ld_pair = ld_qeo if wi == 0 else ld_k
                    wq = wqk if wi == 0 else wqk2
                    for m in range(4):
                        for c in range(NC):
                            G = PG[itg % 2]
                            if itg >= 2:
                                p.wait("tensor", acs, it_evac_act[itg - 2])
                            if m == 0 and c == 0:
                                p.waitt("tensor", ld_pair)
                            if wi == 0 and m == 0:
                                p.waitt("tensor", ld_xc[c])
                            if itg == 0:
                                p.waitt("vector", ld_cs)
                            for half in range(2):
                                for i in range(8):
                                    p.mm(pe, G[:, half * 512:half * 512 + 512],
                                         wq[half][:, i * 512 + m * 128:
                                                  i * 512 + m * 128 + 128],
                                         xTbf[:, i * T + c * 512:
                                              i * T + c * 512 + 512],
                                         start=(i == 0), stop=(i == 7),
                                         inc=(i == 7 and half == 1))
                            mm_done = pe.n
                            eb = ebuf[:, (itg % 2) * 512:(itg % 2) * 512 + 512]
                            ob = obuf[:, (itg % 2) * 512:(itg % 2) * 512 + 512]
                            p.wait("scalar", pe, mm_done)
                            if itg >= 2:
                                p.wait("scalar", dve, it_dve_done[itg - 2])
                            p.scopy(acs, eb, G[:, 0:512])
                            p.scopy(acs, ob, G[:, 512:1024], inc=True)
                            it_evac_act.append(acs.n)
                            cs = (slice(None),
                                  slice(c * 512, c * 512 + 512))
                            c0 = slice((2 * m) * T + c * 512,
                                       (2 * m) * T + c * 512 + 512)
                            c1 = slice((2 * m + 1) * T + c * 512,
                                       (2 * m + 1) * T + c * 512 + 512)
                            p.wait("vector", acs, acs.n)
                            p.vop(dve, "tensor_mul", tmp1[:, :], eb, cosb[cs])
                            p.vop(dve, "tensor_mul", tmp2[:, :], ob, sinb[cs])
                            p.vop(dve, "tensor_sub", rT[0:64, c0],
                                  tmp1[0:64, :], tmp2[0:64, :])
                            p.vop(dve, "tensor_sub", rT[0:64, c1],
                                  tmp1[64:128, :], tmp2[64:128, :])
                            p.vop(dve, "tensor_mul", tmp1[:, :], eb, sinb[cs])
                            p.vop(dve, "tensor_mul", tmp2[:, :], ob, cosb[cs])
                            p.vop(dve, "tensor_add", rT[64:128, c0],
                                  tmp1[0:64, :], tmp2[0:64, :])
                            p.vop(dve, "tensor_add", rT[64:128, c1],
                                  tmp1[64:128, :], tmp2[64:128, :],
                                  inc=True)
                            it_dve_done.append(dve.n)
                            itg += 1

                v_act = []
                vi = 0
                for tt in range(NTT):
                    for oc in range(2):
                        bk = 4 + (vi % 2)
                        if vi >= 2:
                            p.wait("tensor", acs, v_act[vi - 2])
                        if vi == 0:
                            p.waitt("tensor", ld_v)
                        for i in range(8):
                            p.mm(pe, P[bk][:, :],
                                 xTbf[:, i * T + tt * 128: i * T + tt * 128 + 128],
                                 wvb[:, i * 1024 + oc * 512:
                                     i * 1024 + oc * 512 + 512],
                                 start=(i == 0), stop=(i == 7), inc=(i == 7))
                        p.wait("scalar", pe, pe.n)
                        dst = bass.AP(vaug.ap().tensor, tt * 1040 + oc * 520,
                                      [[16 * 1040, 128], [65, 8], [1, 64]])
                        src = P[bk][:, 0:512].rearrange("p (h d) -> p h d", h=8)
                        p.scopy(acs, dst, src, inc=True)
                        v_act.append(acs.n)
                        vi += 1
                if "d_rq" in ddump or "d_rk" in ddump:
                    p.wait("sync", dve, dve.n)
                    for nm, tsr in (("d_rq", rq), ("d_rk", rk)):
                        if nm in ddump:
                            p.dma("sync", st, ddump[nm][:, :], tsr[:, :])
                if "d_vaug" in ddump:
                    p.wait("sync", acs, acs.n)
                    p.dma("sync", st, ddump["d_vaug"][:, :], vaug[:, :])
                p.run()
            a_pe, a_act, a_dve = pe.n, acs.n, dve.n

            y1f = ctx.enter_context(
                nc.sbuf_tensor("y1f", [128, 8 * T], F32, side="right"))

            with ExitStack() as bctx:
                wob = bctx.enter_context(nc.sbuf_tensor("wob", [128, 8 * 256], BF16))
                aout = bctx.enter_context(nc.sbuf_tensor("aout", [128, 8 * 512], BF16))
                exps = bctx.enter_context(nc.sbuf_tensor("exps", [128, 4 * 512], BF16))
                rrow = bctx.enter_context(nc.sbuf_tensor("rrow", [128, 512], F32))
                nbs0 = bctx.enter_context(nc.sbuf_tensor("nbs0", [128, 512], F32))
                maskb = bctx.enter_context(nc.sbuf_tensor("maskb", [128, 896], BF16))
                xf32 = bctx.enter_context(nc.sbuf_tensor("xf32", [128, 2 * 512], F32))
                z1bf = bctx.enter_context(nc.sbuf_tensor("z1bf", [128, 2 * 512], BF16))
                zsqb = bctx.enter_context(nc.sbuf_tensor("zsqb", [128, 2 * 512], BF16))
                stat = [bctx.enter_context(nc.sbuf_tensor(f"stat{i}", [128, 512], F32))
                        for i in range(3)]
                stat.append(stat[1])
                wstage = [bctx.enter_context(
                    nc.sbuf_tensor(f"wst{i}", [128, 1024], BF16)) for i in range(2)]
                rqz = bctx.enter_context(
                    nc.sbuf_tensor("rqz", [128, 2 * 1024], BF16))

                ld_msk = [p.dma("sync", None, maskb[:, :], maskd[:, :]),
                          p.dma("sync", None, idf[:, :], idfd[:, :]),
                          p.dma("sync", None, idb[:, :], idbd[:, :])]
                wo_ld = {}
                wot_st = {}
                cast_acts = {}
                xs_dve = [a_dve] * 2
                xs_ld = {}

                def x_slice_src(c, odt):
                    return bass.AP(xT, odt * 128 * T + c * 512,
                                   [[T, 128], [1, 512]])

                pieces = []
                for oh in range(4):
                    for qi in range(2):
                        pieces.append((bass.AP(wotq,
                                               oh * D * 256 + qi * 4 * 128 * 256,
                                               [[256, 128], [128 * 256, 4],
                                                [1, 256]]),
                                       wotc[oh, :, qi * 1024:(qi + 1) * 1024],
                                       "wot", oh))
                for key in range(16):
                    for qi in range(4):
                        pieces.append((bass.AP(w1h, key * D * 512 + qi * 2 * 128 * 512,
                                               [[512, 128], [128 * 512, 2], [1, 512]]),
                                       w1c[key, :, qi * 1024:(qi + 1) * 1024],
                                       "w1", key))
                for key in range(16):
                    eg_, dt_ = key // 8, key % 8
                    off0 = dt_ * (NE * D) * 128 + eg_ * 4096 * 128
                    for qi in range(4):
                        pieces.append((bass.AP(w2d, off0 + qi * 8 * 128 * 128,
                                               [[128, 128], [128 * 128, 8], [1, 128]]),
                                       w2c[key, :, qi * 1024:(qi + 1) * 1024],
                                       "w2", key))
                for key in range(4):
                    for qi in range(4):
                        pieces.append((bass.AP(sguq, key * D * 512 + qi * 2 * 128 * 512,
                                               [[512, 128], [128 * 512, 2], [1, 512]]),
                                       sguc[key, :, qi * 1024:(qi + 1) * 1024],
                                       "sgu", key))
                for key in range(8):
                    pieces.append((bass.AP(sdd, key * D * 128,
                                           [[128, 128], [128 * 128, 8], [1, 128]]),
                                   sddc[key, :, :], "sdd", key))
                pf_st = {}
                pf_slot = [None, None]
                pf_i = [0]

                def emit_pieces(n):
                    while n > 0 and pf_i[0] < len(pieces):
                        srcap, dstap, strm, key = pieces[pf_i[0]]
                        sl = pf_i[0] % 2
                        p.waitt("gpsimd", pf_slot[sl])
                        t_ld = p.dma("gpsimd", None,
                                     wstage[sl][:, 0:dstap.shape[-1]], srcap)
                        p.waitt("sync", t_ld)
                        t_st = p.dma("sync", None, dstap,
                                     wstage[sl][:, 0:dstap.shape[-1]])
                        pf_slot[sl] = t_st
                        pf_st.setdefault((strm, key), []).append(t_st)
                        pf_i[0] += 1
                        n -= 1
                exp_act = {}
                exp_rd = {}
                msk_dve = {}
                wo_free = [a_dve, a_dve]
                av_free = {0: a_dve, 1: a_dve, 2: a_dve}
                nbs_rd = [a_dve, a_dve]
                bcast_pe = [a_pe, a_pe]
                p7_dve = a_dve
                ln_dve = a_dve
                xz_dve = [a_dve] * NC
                wob_rd_pe = 0
                stats_pe = a_pe
                stats_pe_slot = [a_pe, a_pe]
                stats_q = []
                stats_pe_slot = [a_pe, a_pe]
                stats_q = []
                ip = 0
                gg = 0
                pending_norm = []
                pending_norm2 = []
                mr_dve_prev = a_dve
                gpend = [None]
                flush_av_last = None
                p.vop(dve, "memset", rqz[:, :], 0.0, inc=True)
                rqz_dve = {}
                rqz_free_pe = [a_pe, a_pe]

                def build_rqz(j):
                    if j >= 32:
                        return
                    cj, prj = j // 8, j % 8
                    sl = j % 2
                    p.wait("vector", pe, rqz_free_pe[sl])
                    base = prj * T + cj * 512
                    for hi, (r0a, r0b) in enumerate(((0, 64), (32, 96))):
                        o = sl * 1024 + hi * 512
                        p.vop(dve, "tensor_copy", rqz[r0a:r0a + 32, o:o + 512],
                              rq[r0a:r0a + 32, base:base + 512])
                        p.vop(dve, "tensor_copy", rqz[r0b:r0b + 32, o:o + 512],
                              rq[r0b:r0b + 32, base:base + 512],
                              inc=(hi == 1))
                    rqz_dve[j] = dve.n
                for c in range(NC):
                    nmt = 4 * c + 4
                    for odt in range(2):
                        p.wait("sync", dve, xs_dve[odt])
                        xs_ld[(c, odt)] = p.dma(
                            "sync", None, xf32[:, odt * 512:(odt + 1) * 512],
                            x_slice_src(c, odt))
                    if c > 0:
                        p.wait("tensor", dve, mr_dve_prev)

                    def emit_norm_s1(ent):
                        h0n, bksn, dd = ent
                        recs = [0, 0]
                        for hi in range(2):
                            avb = P4_7[bksn[hi]]
                            rr = rrow[64 + 32 * hi:65 + 32 * hi, :]
                            p.wait("scalar", pe, dd["av_done"][hi])
                            p.wait("scalar", pe, bcast_pe[hi])
                            p.act(acs, rr, avb[64:65, :], AF.Ln, inc=True)
                            p.wait("scalar", acs, acs.n)
                            p.act(acs, rr, rr, AF.Exp, scale=-1.0, inc=True)
                            recs[hi] = acs.n
                        pending_norm2.append((h0n, bksn, recs))

                    def emit_norm_s2(ent2):
                        nonlocal p7_dve
                        h0n, bksn, recs = ent2
                        un = h0n // 2
                        for hi in range(2):
                            rr = rrow[64 + 32 * hi:65 + 32 * hi, :]
                            p.wait("tensor", acs, recs[hi])
                            p.wait("tensor", dve, p7_dve)
                            p.mm(pe, P[7][64 * hi:64 * hi + 64, :],
                                 onesf[64 + 32 * hi:65 + 32 * hi, 0:64],
                                 rr, start=True, stop=True,
                                 tp=(64 + 32 * hi, 64 * hi), inc=True)
                            bcast_pe[hi] = pe.n
                        for hi in range(2):
                            avb = P4_7[bksn[hi]]
                            p.wait("vector", pe, bcast_pe[hi])
                            p.vop(dve, "tensor_copy",
                                  nbs0[hi * 64:hi * 64 + 64, :],
                                  P[7][64 * hi:64 * hi + 64, :], inc=True)
                            p.vop(dve, "tensor_mul",
                                  aout[hi * 64:hi * 64 + 64,
                                       un * 512:un * 512 + 512],
                                  avb[0:64, :],
                                  nbs0[hi * 64:hi * 64 + 64, :], inc=True)
                            av_free[bksn[hi]] = dve.n
                            nbs_rd[hi] = dve.n
                        p7_dve = dve.n
                    for pp in range(4):
                        for sub in range(2):
                            h0 = 4 * pp + 2 * sub
                            av_done = [0, 0]
                            bks = [(2 * ip) % 3, (2 * ip + 1) % 3]
                            def flush_av(pend):
                                nonlocal flush_av_last
                                pgg, pmt, pdiag, ph0, pbks, avd, pnmt = pend
                                for hi in range(2):
                                    h = ph0 + hi
                                    if pdiag:
                                        p.wait("tensor", dve, msk_dve[pgg])
                                    else:
                                        p.wait("tensor", acs, exp_act[pgg])
                                    avb = P4_7[pbks[hi]]
                                    if pmt == 0:
                                        p.wait("tensor", dve, av_free[pbks[hi]])
                                    slh = exps[:, (pgg % 2) * 1024 + hi * 512:
                                               (pgg % 2) * 1024 + hi * 512 + 512]
                                    p.mm(pe, avb[0:65, :],
                                         vaug[:, pmt * 1040 + 65 * h:
                                              pmt * 1040 + 65 * h + 65],
                                         slh, start=(pmt == 0),
                                         stop=(pmt == pnmt - 1),
                                         inc=(pmt == pnmt - 1))
                                    if pmt == pnmt - 1:
                                        avd[hi] = pe.n
                                exp_rd[pgg] = pe.n
                            flush_av_last = flush_av
                            if ip == 0:
                                build_rqz(0)
                            build_rqz(ip + 1)
                            pr = h0 // 2
                            for mt in range(nmt):
                                G = PG[gg % 2]
                                if gg >= 2:
                                    p.wait("tensor", acs, exp_act[gg - 2])
                                if mt == 0:
                                    p.wait("tensor", dve, rqz_dve[ip])
                                lo = pr * T + mt * 128
                                zsl = (ip % 2) * 1024
                                for hi in range(2):
                                    half = G[:, hi * 512:(hi + 1) * 512]
                                    p.mm(pe, half, rk[:, lo:lo + 128],
                                         rqz[:, zsl + hi * 512:
                                             zsl + hi * 512 + 512],
                                         start=True, stop=True,
                                         inc=(hi == 1))
                                sc_pe = pe.n
                                p.wait("scalar", pe, sc_pe)
                                if gg >= 2:
                                    p.wait("scalar", pe, exp_rd[gg - 2])
                                sl2 = exps[:, (gg % 2) * 1024:(gg % 2) * 1024 + 1024]
                                p.act(acs, sl2, G[:, :], AF.Exp, scale=0.125,
                                      inc=True)
                                exp_act[gg] = acs.n
                                diag = mt >= 4 * c
                                if diag:
                                    kd = mt - 4 * c
                                    p.wait("vector", acs, exp_act[gg])
                                    if gg == 0:
                                        p.waitt("vector", ld_msk)
                                    for hi in range(2):
                                        slh = exps[:, (gg % 2) * 1024 + hi * 512:
                                                   (gg % 2) * 1024 + hi * 512 + 512]
                                        p.vop(dve, "tensor_mul", slh, slh,
                                              maskb[:, 384 - 128 * kd:
                                                    384 - 128 * kd + 512],
                                              inc=(hi == 1))
                                    msk_dve[gg] = dve.n
                                if mt == 1:
                                    while pending_norm2:
                                        emit_norm_s2(pending_norm2.pop(0))
                                if gpend[0] is not None:
                                    flush_av(gpend[0])
                                    gpend[0] = None
                                if mt == 0:
                                    while pending_norm:
                                        emit_norm_s1(pending_norm.pop(0))
                                gpend[0] = (gg, mt, diag, h0, bks, av_done, nmt)
                                gg += 1
                            pending_norm.append((h0, bks, dict(av_done=av_done)))
                            emit_pieces(5)
                            rqz_free_pe[ip % 2] = sc_pe
                            ip += 1
                    if gpend[0] is not None:
                        flush_av_last(gpend[0])
                        gpend[0] = None
                    while pending_norm:
                        emit_norm_s1(pending_norm.pop(0))
                    while pending_norm2:
                        emit_norm_s2(pending_norm2.pop(0))
                    aout_dve, aout_act = dve.n, acs.n

                    for oh in range(4):
                        p.wait("sync", pe, wob_rd_pe)
                        if c == 0:
                            p.waitt("sync", pf_st[("wot", oh)])
                        wo_ld[(c, oh)] = p.dma("sync", None, wob[:, :],
                                               wotc[oh, :, :])
                        for odt in range(2 * oh, 2 * oh + 2):
                            bk = odt % 2
                            if odt == 0:
                                p.wait("tensor", dve, aout_dve)
                                p.wait("tensor", acs, aout_act)
                            if odt % 2 == 0:
                                p.waitt("tensor", wo_ld[(c, oh)])
                            if odt >= 2:
                                p.wait("tensor", dve, wo_free[bk])
                            for i in range(8):
                                p.mm(pe, P[bk][:, :],
                                     wob[:, i * 256 + (odt % 2) * 128:
                                         i * 256 + (odt % 2) * 128 + 128],
                                     aout[:, i * 512: i * 512 + 512],
                                     start=(i == 0), stop=(i == 7), inc=(i == 7))
                            wo_pe = pe.n
                            ysl = y1f[:, odt * T + c * 512: odt * T + c * 512 + 512]
                            p.wait("vector", pe, wo_pe)
                            p.waitt("vector", xs_ld[(c, odt)])
                            sx = odt % 2
                            p.vop(dve, "tensor_add", ysl, P[bk][:, :],
                                  xf32[:, sx * 512: sx * 512 + 512], inc=True)
                            wo_free[bk] = dve.n
                            xs_dve[sx] = dve.n
                            if odt < 6:
                                p.wait("sync", dve, xs_dve[sx])
                                xs_ld[(c, odt + 2)] = p.dma(
                                    "sync", None,
                                    xf32[:, sx * 512:(sx + 1) * 512],
                                    x_slice_src(c, odt + 2))
                            zb = z1bf[:, (odt % 2) * 512:(odt % 2) * 512 + 512]
                            qb = zsqb[:, (odt % 2) * 512:(odt % 2) * 512 + 512]
                            p.wait("scalar", dve, wo_free[bk])
                            p.wait("scalar", pe, stats_pe_slot[odt % 2])
                            p.scopy(acs, zb, ysl)
                            p.act(acs, qb, zb, AF.Square, inc=True)
                            cast_acts[odt] = acs.n
                            if stats_q:
                                (podt, pzb, pqb, pca) = stats_q.pop(0)
                                p.wait("tensor", acs, pca)
                                p.mm(pe, P[2][0:1, :], onesb[:, 0:1], pzb,
                                     start=(podt == 0), stop=(podt == 7))
                                p.mm(pe, P[3][0:1, :], onesb[:, 0:1], pqb,
                                     start=(podt == 0), stop=(podt == 7),
                                     inc=True)
                                stats_pe_slot[podt % 2] = pe.n
                            stats_q.append((odt, zb, qb, acs.n))
                        wob_rd_pe = pe.n
                    while stats_q:
                        (podt, pzb, pqb, pca) = stats_q.pop(0)
                        p.wait("tensor", acs, pca)
                        p.mm(pe, P[2][0:1, :], onesb[:, 0:1], pzb,
                             start=(podt == 0), stop=(podt == 7))
                        p.mm(pe, P[3][0:1, :], onesb[:, 0:1], pqb,
                             start=(podt == 0), stop=(podt == 7), inc=True)
                        stats_pe_slot[podt % 2] = pe.n
                    stats_pe = pe.n
                    while stats_q:
                        (podt, pzb, pqb, pca) = stats_q.pop(0)
                        p.wait("tensor", acs, pca)
                        p.mm(pe, P[2][0:1, :], onesb[:, 0:1], pzb,
                             start=(podt == 0), stop=(podt == 7))
                        p.mm(pe, P[3][0:1, :], onesb[:, 0:1], pqb,
                             start=(podt == 0), stop=(podt == 7), inc=True)
                        stats_pe_slot[podt % 2] = pe.n
                    stats_pe = pe.n
                    p.wait("vector", pe, stats_pe)
                    p.vop(dve, "tensor_scalar", stat[0][0:1, :], P[2][0:1, :],
                          1.0 / D, None, ALU.mult)
                    p.vop(dve, "tensor_scalar", stat[1][0:1, :], P[3][0:1, :],
                          1.0 / D, None, ALU.mult, inc=True)
                    p.wait("vector", dve, dve.n)
                    p.vop(dve, "tensor_mul", stat[2][0:1, :], stat[0][0:1, :],
                          stat[0][0:1, :], inc=True)
                    p.wait("vector", dve, dve.n)
                    p.vop(dve, "tensor_sub", stat[1][0:1, :], stat[1][0:1, :],
                          stat[2][0:1, :], inc=True)
                    var_dve = dve.n
                    p.wait("scalar", dve, var_dve)
                    p.act(acs, stat[2][0:1, :], stat[1][0:1, :], AF.Ln,
                          bias=epsb[0:1, :], scale=1.0, inc=True)
                    p.wait("scalar", acs, acs.n)
                    p.act(acs, stat[2][0:1, :], stat[2][0:1, :], AF.Exp,
                          scale=-0.5, inc=True)
                    rstd_act = acs.n
                    p.wait("vector", acs, rstd_act)
                    p.vop(dve, "tensor_mul", stat[3][0:1, :], stat[0][0:1, :],
                          stat[2][0:1, :], inc=True)
                    p.wait("vector", dve, dve.n)
                    mr_dve = dve.n
                    p.wait("tensor", dve, mr_dve)
                    p.wait("tensor", dve, av_free[0])
                    p.wait("tensor", dve, av_free[1])
                    p.mm(pe, P[4][:, :], onesf[0:1, 0:128], stat[2][0:1, :],
                         start=True, stop=True)
                    p.mm(pe, P[5][:, :], onesf[0:1, 0:128], stat[3][0:1, :],
                         start=True, stop=True, inc=True)
                    bc2_pe = pe.n
                    for odt in range(8):
                        ysl = y1f[:, odt * T + c * 512: odt * T + c * 512 + 512]
                        p.wait("vector", pe, bc2_pe)
                        p.vop(dve, "tensor_mul", ysl, ysl, P[4][:, :])
                        p.vop(dve, "tensor_sub", ysl, ysl, P[5][:, :],
                              inc=(odt == 7))
                    ln_dve = dve.n
                    av_free[0] = ln_dve
                    av_free[1] = ln_dve
                    mr_dve_prev = mr_dve
                    xz_dve[c] = dve.n
                emit_pieces(len(pieces))
                pf_state = (pf_st,)
                if "d_aout" in ddump:
                    p.wait("sync", dve, dve.n)
                    p.dma("sync", st, ddump["d_aout"][:, :], aout[:, :])
                if "d_y1" in ddump:
                    p.wait("sync", dve, dve.n)
                    p.dma("sync", st, ddump["d_y1"][:, :], y1f[:, :])
                p.run()
        b_dve, b_act, b_pe = dve.n, acs.n, pe.n

        with ExitStack() as cctx:
            y1bf = cctx.enter_context(nc.sbuf_tensor("y1bf", [128, 8 * 512], BF16))
            hgrp = cctx.enter_context(nc.sbuf_tensor("hgrp", [128, 32 * 512], BF16))
            accs = cctx.enter_context(nc.sbuf_tensor("accs", [128, 8 * 512], F32))
            w1b = [cctx.enter_context(nc.sbuf_tensor(f"w1b{i}", [128, 8 * 512], BF16))
                   for i in range(2)]
            w2b = [cctx.enter_context(nc.sbuf_tensor(f"w2b{i}", [128, 32 * 128], BF16))
                   for i in range(2)]
            sgub = [cctx.enter_context(nc.sbuf_tensor(f"sgub{i}", [128, 8 * 512], BF16))
                    for i in range(2)]
            sddb = [cctx.enter_context(nc.sbuf_tensor(f"sddb{i}", [128, 8 * 128], BF16))
                    for i in range(2)]
            sgb = cctx.enter_context(nc.sbuf_tensor("sgb", [128, 8 * 512], BF16))
            spre = cctx.enter_context(nc.sbuf_tensor("spre", [128, 8 * 512], BF16))
            hsil = cctx.enter_context(nc.sbuf_tensor("hsil", [128, 2 * 512], BF16))
            rwb = cctx.enter_context(nc.sbuf_tensor("rwb", [128, 64], BF16))
            sgwb = cctx.enter_context(nc.sbuf_tensor("sgwb", [128, 8], BF16))
            comb = [cctx.enter_context(nc.sbuf_tensor(f"comb{i}", [128, 512], BF16))
                    for i in range(8)]
            probs = cctx.enter_context(nc.sbuf_tensor("probs", [128, 8], F32))
            ctm = cctx.enter_context(nc.sbuf_tensor("ctm", [128, 8], F32))
            mt1 = cctx.enter_context(nc.sbuf_tensor("mt1", [128, 8], F32))
            lt1 = cctx.enter_context(nc.sbuf_tensor("lt1", [128, 8], F32))
            rsc = [cctx.enter_context(nc.sbuf_tensor(f"rsc{i}", [128, 1], F32))
                   for i in range(5)]
            grow = cctx.enter_context(nc.sbuf_tensor("grow", [128, 512], F32))
            tmpc = cctx.enter_context(nc.sbuf_tensor("tmpc", [128, 512], BF16))

            ld_rw = [p.dma("gpsimd", None, rwb[:, :], rwp[:, :]),
                     p.dma("gpsimd", None, sgwb[:, :], sgwp[:, :])]

            w1_free = [b_pe] * 2
            w2_free = [b_pe] * 2
            sgu_free = [b_pe] * 2
            sdd_free = [b_pe] * 2
            w1_st, w2_st, sgu_st, sdd_st = {}, {}, {}, {}
            w1_stslot = [None] * 2
            w2_stslot = [None] * 2
            sgu_stslot = [None] * 2
            sdd_stslot = [None] * 2
            y1bf_pe = b_pe
            routed_dve = [b_dve] * 4
            hs_act = [b_act] * 2
            hsil_rd = [b_dve] * 2
            cbc_rd = [b_dve] * 2
            gu_rd = [("dve", b_dve), ("dve", b_dve)]
            z2_dve = [0] * NC
            gbc_pe_prev = b_pe
            hgrp_w2_pe = b_pe

            for c in range(NC):
                pe_start = pe.n
                if c > 0:
                    p.wait("tensor", dve, z2_dve[c - 1])
                    p.wait("vector", pe, pe_start)
                    p.wait("scalar", dve, z2_dve[c - 1])
                p.wait("scalar", pe, y1bf_pe)
                for dt in range(8):
                    p.scopy(acs, y1bf[:, dt * 512: dt * 512 + 512],
                            y1f[:, dt * T + c * 512: dt * T + c * 512 + 512],
                            inc=(dt == 7))
                y1bf_act = acs.n
                p.wait("tensor", acs, y1bf_act)
                if c == 0:
                    p.waitt("tensor", ld_rw)
                p.wait("tensor", dve, routed_dve[3])
                for ts in range(4):
                    for dt in range(8):
                        p.mm(pe, P[7][:, ts * 8: ts * 8 + 8],
                             y1bf[:, dt * 512 + ts * 128: dt * 512 + ts * 128 + 128],
                             rwb[:, dt * 8: dt * 8 + 8],
                             start=(dt == 0), stop=(dt == 7), inc=(dt == 7))
                rt_pe = pe.n
                t_pe = 0
                for ts in range(4):
                    psl = P[7][:, ts * 8: ts * 8 + 8]
                    p.wait("vector", pe, rt_pe)
                    if ts > 0:
                        p.wait("vector", pe, t_pe)
                    p.vop(dve, "reduce_max", rsc[0][:, :], psl, axis=AX.X,
                          negate=True)
                    p.vop(dve, "reduce_max", rsc[1][:, :], psl, axis=AX.X,
                          inc=True)
                    p.wait("vector", dve, dve.n)
                    p.vop(dve, "tensor_single_scalar", mt1[:, :], psl,
                          rsc[1][:, :], ALU.is_ge, inc=True)
                    m1_dve = dve.n
                    p.wait("vector", dve, m1_dve)
                    p.wait("scalar", dve, m1_dve)
                    p.act(acs, probs[:, :], psl, AF.Exp, bias=rsc[0][:, :],
                          scale=1.0, accum=rsc[2][:, :], inc=True)
                    e_act = acs.n
                    p.wait("vector", acs, e_act)
                    p.vop(dve, "reciprocal", rsc[3][:, :], rsc[2][:, :], inc=True)
                    p.wait("vector", dve, dve.n)
                    p.vop(dve, "scalar_tensor_tensor", lt1[:, :], mt1[:, :],
                          neg30[:, 0:1], psl, ALU.mult, ALU.add, inc=True)
                    p.wait("vector", dve, dve.n)
                    p.vop(dve, "reduce_max", rsc[4][:, :], lt1[:, :], axis=AX.X,
                          inc=True)
                    p.wait("vector", dve, dve.n)
                    p.vop(dve, "tensor_single_scalar", mt1[:, :], psl,
                          rsc[4][:, :], ALU.is_ge, inc=True)
                    p.wait("vector", dve, dve.n)
                    p.vop(dve, "tensor_scalar", probs[:, :], probs[:, :],
                          rsc[3][:, :], None, ALU.mult, inc=True)
                    p.wait("vector", dve, dve.n)
                    p.vop(dve, "tensor_mul", ctm[:, :], probs[:, :], mt1[:, :],
                          inc=True)
                    c_dve = dve.n
                    for eb in range(2):
                        p.wait("tensor", dve, c_dve)
                        p.wait("tensor", acs, acs.n)
                        for e4 in range(4):
                            e = eb * 4 + e4
                            p.trans(pe, P[6][0:1, e4 * 128: e4 * 128 + 128],
                                    ctm[:, e:e + 1], idf[:, :], inc=(e4 == 3))
                        t_pe = pe.n
                        p.wait("scalar", pe, t_pe)
                        for e4 in range(4):
                            e = eb * 4 + e4
                            p.scopy(acs, comb[e][0:1, ts * 128: ts * 128 + 128],
                                    P[6][0:1, e4 * 128: e4 * 128 + 128],
                                    inc=(e4 == 3))
                comb_act = acs.n
                cbc_rd[0] = max(cbc_rd[0], dve.n)
                sgu_ld = {}
                sdd_ld = {}
                w1_lds2 = {}

                def w1load(s, c=c):
                    slot = s % 2
                    p.wait("scalar", pe, w1_free[slot])
                    if c == 0:
                        p.waitt("scalar", pf_st[("w1", s)])
                    w1_lds2[s] = p.dma("scalar", None, w1b[slot][:, :],
                                       w1c[s, :, :])
                w1load(0)
                w1load(1)

                def sgudma(q, c=c):
                    slot = q % 2
                    p.wait("scalar", pe, sgu_free[slot])
                    if c == 0:
                        p.waitt("scalar", pf_st[("sgu", q)])
                    sgu_ld[q] = p.dma("scalar", None, sgub[slot][:, :],
                                      sguc[q, :, :])

                def sdddma(dt, c=c):
                    slot = dt % 2
                    p.wait("sync", pe, sdd_free[slot])
                    if c == 0:
                        p.waitt("sync", pf_st[("sdd", dt)])
                    sdd_ld[dt] = p.dma("sync", None, sddb[slot][:, :],
                                       sddc[dt, :, :])

                for eg in range(2):
                    w2_ld = {}

                    def w2dma(dt, eg=eg, c=c):
                        slot = dt % 2
                        key = eg * 8 + dt
                        p.wait("sync", pe, w2_free[slot])
                        if c == 0:
                            p.waitt("sync", pf_st[("w2", key)])
                        w2_ld[dt] = p.dma("sync", None, w2b[slot][:, :],
                                          w2c[key, :, :])
                    w2dma(0)
                    w2dma(1)
                    if eg == 1:
                        sgudma(0)
                        sgudma(1)
                        sdddma(0)
                        sdddma(1)
                        p.wait("vector", pe, hgrp_w2_pe)
                    for ei in range(4):
                        e = 4 * eg + ei
                        cb = 6 + (e % 2)
                        p.wait("tensor", acs, comb_act)
                        p.wait("tensor", dve, cbc_rd[e % 2])
                        p.mm(pe, P[cb][:, :], onesb[0:1, 0:128],
                             comb[e][0:1, :], start=True, stop=True, inc=True)
                        cbc_mm = pe.n
                        for fh in range(2):
                            slot = (2 * e + fh) % 2
                            p.waitt("tensor", w1_lds2[2 * e + fh])
                            for ft in range(4):
                                f = fh * 4 + ft
                                hb = 4 + (f % 2)
                                p.wait("tensor", acs, hs_act[f % 2])
                                for dt in range(8):
                                    p.mm(pe, P[hb][:, :],
                                         w1b[slot][:, dt * 512 + ft * 128:
                                                   dt * 512 + ft * 128 + 128],
                                         y1bf[:, dt * 512: dt * 512 + 512],
                                         start=(dt == 0), stop=(dt == 7),
                                         inc=(dt == 7))
                                h_pe = pe.n
                                p.wait("scalar", pe, h_pe)
                                p.wait("scalar", dve, hsil_rd[f % 2])
                                hs = hsil[:, (f % 2) * 512:(f % 2) * 512 + 512]
                                p.act(acs, hs, P[hb][:, :], AF.Silu, inc=True)
                                hs_act[f % 2] = acs.n
                                p.wait("vector", acs, hs_act[f % 2])
                                p.wait("vector", pe, cbc_mm)
                                p.vop(dve, "tensor_mul",
                                      hgrp[:, (ei * 8 + f) * 512:
                                           (ei * 8 + f) * 512 + 512],
                                      hs, P[cb][:, :], inc=True)
                                hsil_rd[f % 2] = dve.n
                            w1_free[slot] = pe.n
                            if 2 * e + fh + 2 < 16:
                                w1load(2 * e + fh + 2)
                        cbc_rd[e % 2] = dve.n
                    hg_dve = dve.n
                    for dt in range(8):
                        slot = dt % 2
                        bk = dt % 4
                        p.waitt("tensor", w2_ld[dt])
                        if dt == 0:
                            p.wait("tensor", dve, hg_dve)
                        p.wait("tensor", dve, routed_dve[bk])
                        for kt in range(32):
                            p.mm(pe, P[bk][:, :],
                                 w2b[slot][:, kt * 128: kt * 128 + 128],
                                 hgrp[:, kt * 512: kt * 512 + 512],
                                 start=(kt == 0), stop=(kt == 31), inc=(kt == 31))
                        w2_free[slot] = pe.n
                        if dt + 2 < 8:
                            w2dma(dt + 2)
                        r_pe = pe.n
                        p.wait("vector", pe, r_pe)
                        asl = accs[:, dt * 512: dt * 512 + 512]
                        ysl = y1f[:, dt * T + c * 512: dt * T + c * 512 + 512]
                        if eg == 0:
                            p.vop(dve, "tensor_add", asl, P[bk][:, :], ysl,
                                  inc=True)
                        else:
                            p.vop(dve, "tensor_add", ysl, P[bk][:, :], asl,
                                  inc=True)
                        routed_dve[bk] = dve.n
                    hgrp_w2_pe = pe.n

                p.wait("tensor", dve, routed_dve[0])
                p.wait("tensor", acs, hs_act[0])
                p.wait("tensor", pe, gbc_pe_prev)
                for dt in range(8):
                    p.mm(pe, P[4][0:1, :], sgwb[:, dt:dt + 1],
                         y1bf[:, dt * 512: dt * 512 + 512],
                         start=(dt == 0), stop=(dt == 7), inc=(dt == 7))
                gw_pe = pe.n
                p.wait("scalar", pe, gw_pe)
                p.scopy(acs, grow[0:1, :], P[4][0:1, :], inc=True)
                p.wait("scalar", acs, acs.n)
                p.act(acs, grow[0:1, :], grow[0:1, :], AF.Sigmoid, inc=True)
                g_act = acs.n
                p.wait("tensor", acs, g_act)
                p.wait("tensor", acs, hs_act[1])
                p.mm(pe, P[5][:, :], onesf[0:1, 0:128], grow[0:1, :],
                     start=True, stop=True, inc=True)
                gbc_pe = pe.n
                gbc_pe_prev = gbc_pe
                sg_act = acs.n
                for q in range(4):
                    slot = q % 2
                    p.waitt("tensor", sgu_ld[q])
                    for ft2 in range(4):
                        f2 = q * 4 + ft2
                        par = f2 % 2
                        gb = 4 if par == 0 else 6
                        if f2 == 0:
                            p.wait("tensor", acs, g_act)
                        if f2 == 1:
                            p.wait("tensor", dve, cbc_rd[0])
                        kind, val = gu_rd[par]
                        if kind == "acs":
                            p.wait("tensor", acs, val)
                        else:
                            p.wait("tensor", dve, val)
                        for dt in range(8):
                            p.mm(pe, P[gb][:, :],
                                 sgub[slot][:, dt * 512 + ft2 * 128:
                                            dt * 512 + ft2 * 128 + 128],
                                 y1bf[:, dt * 512: dt * 512 + 512],
                                 start=(dt == 0), stop=(dt == 7), inc=(dt == 7))
                        gu_pe = pe.n
                        if q < 2:
                            p.wait("scalar", pe, gu_pe)
                            p.act(acs, sgb[:, f2 * 512: f2 * 512 + 512],
                                  P[gb][:, :], AF.Silu, inc=True)
                            sg_act = acs.n
                            gu_rd[par] = ("acs", acs.n)
                        else:
                            fu = f2 - 8
                            p.wait("vector", pe, gu_pe)
                            p.wait("vector", pe, gbc_pe)
                            p.wait("vector", acs, sg_act)
                            p.vop(dve, "tensor_mul", tmpc[:, :], P[gb][:, :],
                                  sgb[:, fu * 512: fu * 512 + 512])
                            p.vop(dve, "tensor_mul",
                                  spre[:, fu * 512: fu * 512 + 512],
                                  tmpc[:, :], P[5][:, :], inc=True)
                            gu_rd[par] = ("dve", dve.n)
                    sgu_free[slot] = pe.n
                    if q + 2 < 4:
                        sgudma(q + 2)
                sp_dve = dve.n
                for dt in range(8):
                    slot = dt % 2
                    bk = dt % 4
                    p.waitt("tensor", sdd_ld[dt])
                    if dt == 0:
                        p.wait("tensor", dve, sp_dve)
                    p.wait("tensor", dve, routed_dve[bk])
                    for ft in range(8):
                        p.mm(pe, P[bk][:, :],
                             sddb[slot][:, ft * 128: ft * 128 + 128],
                             spre[:, ft * 512: ft * 512 + 512],
                             start=(ft == 0), stop=(ft == 7), inc=(ft == 7))
                    sdd_free[slot] = pe.n
                    if dt + 2 < 8:
                        sdddma(dt + 2)
                    d_pe = pe.n
                    p.wait("vector", pe, d_pe)
                    ysl = y1f[:, dt * T + c * 512: dt * T + c * 512 + 512]
                    p.vop(dve, "tensor_add", ysl, P[bk][:, :], ysl, inc=True)
                    routed_dve[bk] = dve.n
                z2_dve[c] = dve.n
                y1bf_pe = pe.n
            if "d_hg" in ddump:
                p.wait("sync", dve, dve.n)
                p.dma("sync", st, ddump["d_hg"][:, :], hgrp[:, :])
            if "d_comb" in ddump:
                p.wait("sync", acs, acs.n)
                for e in range(NE):
                    p.dma("sync", st, ddump["d_comb"][e:e + 1, :], comb[e][0:1, :])
            if "d_ctm" in ddump:
                p.wait("sync", dve, dve.n)
                p.dma("sync", st, ddump["d_ctm"][:, :], ctm[:, :])
                p.dma("sync", st, ddump["d_probs"][:, :], probs[:, :])
                p.dma("sync", st, ddump["d_mt1"][:, :], mt1[:, :])
            if "d_spre" in ddump:
                p.wait("sync", dve, dve.n)
                p.dma("sync", st, ddump["d_spre"][:, :], spre[:, :])
            if "d_accs" in ddump:
                p.dma("sync", st, ddump["d_accs"][:, :], accs[:, :])
            if "d_grow" in ddump:
                p.dma("sync", st, ddump["d_grow"][:, :], grow[0:1, :])
            if "d_z2" in ddump:
                p.wait("sync", dve, dve.n)
                p.dma("sync", st, ddump["d_z2"][:, :], y1f[:, :])
            p.run()

        with ExitStack() as dctx:
            ztm = [dctx.enter_context(nc.sbuf_tensor(f"ztm{i}", [128, 1024], F32))
                   for i in range(2)]
            y2s = [dctx.enter_context(nc.sbuf_tensor(f"y2s{i}", [128, 1024], F32))
                   for i in range(2)]
            sqs = dctx.enter_context(nc.sbuf_tensor("sqs", [128, 1024], BF16))
            dS = dctx.enter_context(nc.sbuf_tensor("dS", [128, 1], F32))
            dS2 = dctx.enter_context(nc.sbuf_tensor("dS2", [128, 1], F32))
            dm = dctx.enter_context(nc.sbuf_tensor("dm", [128, 1], F32))
            dv2 = dctx.enter_context(nc.sbuf_tensor("dv2", [128, 1], F32))
            dr = dctx.enter_context(nc.sbuf_tensor("dr", [128, 1], F32))

            tile_st = [0, 0]
            tr_free = [0, 0]
            for tt in range(NTT):
                sl = tt % 2
                for dt in range(8):
                    bki = dt // 4
                    if tt >= 1 and dt % 4 == 0:
                        p.wait("tensor", acs, tr_free[bki])
                    p.trans(pe, P[4 + bki][:, (dt % 4) * 128:(dt % 4) * 128 + 128],
                            y1f[:, dt * T + tt * 128: dt * T + tt * 128 + 128],
                            idf[:, :], inc=(dt % 4 == 3))
                tr_pe = pe.n
                p.wait("scalar", pe, tr_pe)
                p.wait("scalar", st, tile_st[sl])
                p.scopy(acs, ztm[sl][:, 0:512], P[4][:, :])
                p.scopy(acs, ztm[sl][:, 512:1024], P[5][:, :], inc=True)
                tr_free[0] = acs.n
                tr_free[1] = acs.n
                zt_act = acs.n
                p.wait("vector", acs, zt_act)
                p.vop(dve, "reduce_sum", dS[:, :], ztm[sl][:, :], axis=AX.X,
                      inc=True)
                p.wait("scalar", dve, dve.n)
                p.act(acs, sqs[:, :], ztm[sl][:, :], AF.Square, accum=dS2[:, :],
                      inc=True)
                sq_act = acs.n
                p.wait("vector", acs, sq_act)
                p.vop(dve, "tensor_scalar", dm[:, :], dS[:, :], 1.0 / D, None,
                      ALU.mult)
                p.vop(dve, "tensor_scalar", dv2[:, :], dS2[:, :], 1.0 / D, None,
                      ALU.mult, inc=True)
                p.wait("vector", dve, dve.n)
                p.vop(dve, "tensor_mul", dr[:, :], dm[:, :], dm[:, :], inc=True)
                p.wait("vector", dve, dve.n)
                p.vop(dve, "tensor_sub", dv2[:, :], dv2[:, :], dr[:, :], inc=True)
                p.wait("scalar", dve, dve.n)
                p.act(acs, dr[:, :], dv2[:, :], AF.Ln, bias=epsb[:, :], scale=1.0,
                      inc=True)
                p.wait("scalar", acs, acs.n)
                p.act(acs, dr[:, :], dr[:, :], AF.Exp, scale=-0.5, inc=True)
                r_act = acs.n
                p.wait("vector", acs, r_act)
                p.vop(dve, "tensor_scalar", y2s[sl][:, :], ztm[sl][:, :],
                      dm[:, :], dr[:, :], ALU.subtract, ALU.mult, inc=True)
                y2_dve = dve.n
                p.wait("sync", dve, y2_dve)
                p.dma("sync", st, out[tt * 128:(tt + 1) * 128, :], y2s[sl][:, :])
                tile_st[sl] = st.n
            p.wait("sync", st, st.n)
            p.run()
    return nc


_cache = {}


def _host_prep(L0_attn_Wq, L0_attn_Wk, L0_attn_Wv, L0_attn_Wo, L0_ffn_router_w,
               L0_ffn_w1, L0_ffn_w2, L0_ffn_shared_gate_up, L0_ffn_shared_down,
               L0_ffn_shared_gate_w):
    f = np.float32
    perm_e = np.array([64 * h + 2 * j for h in range(H) for j in range(32)])
    perm_o = perm_e + 1
    Wq = np.asarray(L0_attn_Wq, f)
    Wk = np.asarray(L0_attn_Wk, f)
    d = {
        "wqet": np.ascontiguousarray(Wq[perm_e].T),
        "wqot": np.ascontiguousarray(Wq[perm_o].T),
        "wket": np.ascontiguousarray(Wk[perm_e].T),
        "wkot": np.ascontiguousarray(Wk[perm_o].T),
        "wvt": np.ascontiguousarray(np.asarray(L0_attn_Wv, f).T),
        "wotq": np.ascontiguousarray(
            np.asarray(L0_attn_Wo, f).T.reshape(D, 4, 256).transpose(1, 0, 2)),
        "rwp": np.ascontiguousarray(
            np.asarray(L0_ffn_router_w, f).reshape(8, 128, 8)
            .transpose(1, 0, 2).reshape(128, 64)),
        "w1h": np.ascontiguousarray(
            np.asarray(L0_ffn_w1, f).reshape(NE, D, 2, 512).transpose(0, 2, 1, 3)),
        "w2d": np.ascontiguousarray(
            np.asarray(L0_ffn_w2, f).reshape(NE * D, 8, 128).transpose(1, 0, 2)),
        "sguq": np.ascontiguousarray(
            np.asarray(L0_ffn_shared_gate_up, f).reshape(D, 4, 512)
            .transpose(1, 0, 2)),
        "sdd": np.ascontiguousarray(
            np.asarray(L0_ffn_shared_down, f).reshape(D, 8, 128)
            .transpose(1, 0, 2)),
        "sgwp": np.ascontiguousarray(
            np.asarray(L0_ffn_shared_gate_w, f).reshape(8, 128).T),
        "idfd": np.eye(128, dtype=f),
        "idbd": np.eye(128, dtype=f).astype(BF),
    }
    j = np.arange(32, dtype=np.float64)
    invf = 10000.0 ** (-j / 32.0)
    t = np.arange(T, dtype=np.float64)
    ang = t[None, :] * invf[:, None]
    d["cosd"] = np.ascontiguousarray(np.tile(np.cos(ang), (4, 1))).astype(BF)
    d["sind"] = np.ascontiguousarray(np.tile(np.sin(ang), (4, 1))).astype(BF)
    pp, jj = np.mgrid[0:128, 0:896]
    d["maskd"] = (jj >= pp + 384).astype(f).astype(BF)
    return d


def kernel(L0_attn_x, L0_attn_Wq, L0_attn_Wk, L0_attn_Wv, L0_attn_Wo,
           L0_ffn_router_w, L0_ffn_w1, L0_ffn_w2, L0_ffn_shared_gate_up,
           L0_ffn_shared_down, L0_ffn_shared_gate_w, _dumps=()):
    shared = _host_prep(L0_attn_Wq, L0_attn_Wk, L0_attn_Wv, L0_attn_Wo,
                        L0_ffn_router_w, L0_ffn_w1, L0_ffn_w2,
                        L0_ffn_shared_gate_up, L0_ffn_shared_down,
                        L0_ffn_shared_gate_w)
    x = np.asarray(L0_attn_x, np.float32)
    key = tuple(sorted(_dumps))
    if key not in _cache:
        _cache[key] = build_program(dumps=_dumps)
    nc = _cache[key]
    in_maps = []
    for cidx in range(8):
        m = dict(shared)
        m["xT"] = np.ascontiguousarray(x[cidx].T)
        in_maps.append(m)
    res = run_bass_kernel_spmd(nc, in_maps, core_ids=list(range(8)))
    outs = np.stack([np.asarray(res.results[i]["out"]) for i in range(8)], axis=0)
    if _dumps:
        return outs, res.results
    return outs


# version 22
# speedup vs baseline: 1.1060x; 1.0304x over previous
import numpy as np
import ml_dtypes
import concourse.bass as bass
import concourse.mybir as mybir
from concourse.bass_utils import run_bass_kernel_spmd
from contextlib import ExitStack

F32 = mybir.dt.float32
BF16 = mybir.dt.bfloat16
AF = mybir.ActivationFunctionType
ALU = mybir.AluOpType
AX = mybir.AxisListType

D = 1024
T = 2048
H = 16
NE = 8
NC = 4
NTT = 16
BF = ml_dtypes.bfloat16


class Sem:
    def __init__(self, ctx, nc, name):
        self.h = ctx.enter_context(nc.semaphore(name))
        self.n = 0


class Prog:

    def __init__(self, nc):
        self.nc = nc
        self.ops = {k: [] for k in ("tensor", "vector", "scalar", "gpsimd", "sync")}

    def add(self, eng, fn):
        self.ops[eng].append(fn)

    def wait(self, eng, sem, val):
        if val > 0:
            self.add(eng, lambda e, s=sem.h, v=val: e.wait_ge(s, v))

    def run(self):
        def mk(lst):
            def body(eng):
                for f in lst:
                    f(eng)
            return body
        with self.nc.Block() as blk:
            for nm, deco in (("tensor", blk.tensor), ("vector", blk.vector),
                             ("scalar", blk.scalar), ("gpsimd", blk.gpsimd),
                             ("sync", blk.sync)):
                if self.ops[nm]:
                    deco(mk(self.ops[nm]))
        self.ops = {k: [] for k in self.ops}

    def mm(self, pe, out, l, r, start, stop, tp=None, inc=False):
        if inc:
            pe.n += 1

        def f(e, out=out, l=l, r=r, start=start, stop=stop, tp=tp, inc=inc, s=pe.h):
            ins = e.matmul(out, l, r, start=start, stop=stop, tile_position=tp)
            if inc:
                ins.then_inc(s, 1)
        self.add("tensor", f)

    def trans(self, pe, out, in_, ident, inc=False):
        if inc:
            pe.n += 1

        def f(e, out=out, in_=in_, ident=ident, inc=inc, s=pe.h):
            ins = e.transpose(out, in_, ident)
            if inc:
                ins.then_inc(s, 1)
        self.add("tensor", f)

    def act(self, sm, out, in_, func, bias=0.0, scale=1.0, accum=None, inc=False):
        if inc:
            sm.n += 1

        def f(e, out=out, in_=in_, func=func, bias=bias, scale=scale, accum=accum,
              inc=inc, s=sm.h):
            ins = e.activation(out, in_, func, bias=bias, scale=scale,
                               accum_out=accum)
            if inc:
                ins.then_inc(s, 1)
        self.add("scalar", f)

    def scopy(self, sm, out, in_, inc=False):
        if inc:
            sm.n += 1

        def f(e, out=out, in_=in_, inc=inc, s=sm.h):
            ins = e.copy(out, in_)
            if inc:
                ins.then_inc(s, 1)
        self.add("scalar", f)

    def vop(self, dv, name, *args, inc=False, **kw):
        if inc:
            dv.n += 1

        def f(e, name=name, args=args, kw=kw, inc=inc, s=dv.h):
            ins = getattr(e, name)(*args, **kw)
            if inc:
                ins.then_inc(s, 1)
        self.add("vector", f)

    def dma(self, eng, sem, out, in_):
        if sem is None:
            s = self.dpool[self.dpi % len(self.dpool)]
            self.dpi += 1
            s.n += 16

            def f(e, out=out, in_=in_, sh=s.h):
                e.dma_start(out, in_).then_inc(sh, 16)
            self.add(eng, f)
            return (s, s.n)
        sem.n += 16

        def f(e, out=out, in_=in_, s=sem.h):
            e.dma_start(out, in_).then_inc(s, 16)
        self.add(eng, f)

    def waitt(self, eng, tkt):
        if tkt is None:
            return
        if isinstance(tkt, list):
            for t in tkt:
                self.waitt(eng, t)
            return
        self.wait(eng, tkt[0], tkt[1])


def fm_src(dram, rows, cols, off_elems=0, ccnt=None):
    return bass.AP(dram, off_elems,
                   [[cols, 128], [128 * cols, rows // 128], [1, ccnt or cols]])


def build_program(dumps=()):
    nc = bass.Bass("TRN2", target_bir_lowering=False)

    def din(name, shape, dt=F32):
        return nc.dram_tensor(name, list(shape), dt, kind="ExternalInput")

    xT = din("xT", (D, T))
    wqet = din("wqet", (D, 512))
    wqot = din("wqot", (D, 512))
    wket = din("wket", (D, 512))
    wkot = din("wkot", (D, 512))
    wvt = din("wvt", (D, D))
    wotq = din("wotq", (4, D, 256))
    rwp = din("rwp", (128, 64))
    w1h = din("w1h", (NE, 2, D, 512))
    w2d = din("w2d", (8, NE * D, 128))
    sguq = din("sguq", (4, D, 512))
    sdd = din("sdd", (8, D, 128))
    sgwp = din("sgwp", (128, 8))
    cosd = din("cosd", (128, T), BF16)
    sind = din("sind", (128, T), BF16)
    maskd = din("maskd", (128, 896), BF16)
    idfd = din("idfd", (128, 128))
    idbd = din("idbd", (128, 128), BF16)

    wotc = nc.dram_tensor("wotc", [4, 128, 8 * 256], BF16)
    w1c = nc.dram_tensor("w1c", [NE * 2, 128, 8 * 512], BF16)
    w2c = nc.dram_tensor("w2c", [16, 128, 32 * 128], BF16)
    sguc = nc.dram_tensor("sguc", [4, 128, 8 * 512], BF16)
    sddc = nc.dram_tensor("sddc", [8, 128, 8 * 128], BF16)
    out = nc.dram_tensor("out", [T, D], F32, kind="ExternalOutput")
    ddump = {}
    for nm, shp, ddt in (("d_rq", [128, 8 * T], BF16), ("d_rk", [128, 8 * T], BF16),
                         ("d_vaug", [128, 16 * 1040], BF16),
                         ("d_aout", [128, 8 * 512], BF16),
                         ("d_y1", [128, 8 * T], F32),
                         ("d_hg", [128, 32 * 512], BF16),
                         ("d_comb", [8, 512], BF16),
                         ("d_ctm", [128, 8], F32),
                         ("d_probs", [128, 8], F32),
                         ("d_mt1", [128, 8], F32),
                         ("d_spre", [128, 8 * 512], BF16),
                         ("d_accs", [128, 8 * 512], F32),
                         ("d_grow", [1, 512], F32),
                         ("d_z2", [128, 8 * T], F32)):
        if nm in dumps:
            ddump[nm] = nc.dram_tensor(nm, shp, ddt, kind="ExternalOutput")

    ctx = ExitStack()
    with ctx:
        ld = Sem(ctx, nc, "ld")
        ld2 = Sem(ctx, nc, "ld2")
        ld3 = Sem(ctx, nc, "ld3")
        wcs = Sem(ctx, nc, "wcs")
        pe = Sem(ctx, nc, "pe")
        acs = Sem(ctx, nc, "acs")
        dve = Sem(ctx, nc, "dve")
        st = Sem(ctx, nc, "st")

        P01 = ctx.enter_context(nc.psum_tensor("P01", [128, 1024], F32))
        P23 = ctx.enter_context(nc.psum_tensor("P23", [128, 1024], F32))
        P4_7 = [ctx.enter_context(nc.psum_tensor(f"P{i}", [128, 512], F32))
                for i in range(4, 8)]
        P = [P01[:, 0:512], P01[:, 512:1024], P23[:, 0:512], P23[:, 512:1024],
             P4_7[0][:, :], P4_7[1][:, :], P4_7[2][:, :], P4_7[3][:, :]]
        PG = [P01, P23]

        onesf = ctx.enter_context(nc.sbuf_tensor("onesf", [128, 128], F32))
        onesb = ctx.enter_context(nc.sbuf_tensor("onesb", [128, 128], BF16))
        idf = ctx.enter_context(nc.sbuf_tensor("idf", [128, 128], F32))
        idb = ctx.enter_context(nc.sbuf_tensor("idb", [128, 128], BF16))
        epsb = ctx.enter_context(nc.sbuf_tensor("epsb", [128, 1], F32))
        neg30 = ctx.enter_context(nc.sbuf_tensor("neg30", [128, 1], F32))

        p = Prog(nc)
        p.dpool = [Sem(ctx, nc, f"dp{i}") for i in range(32)]
        p.dpi = 0

        with ExitStack() as abctx:
            rq = abctx.enter_context(nc.sbuf_tensor("rq", [128, 8 * T], BF16))
            rk = abctx.enter_context(nc.sbuf_tensor("rk", [128, 8 * T], BF16))
            vaug = abctx.enter_context(nc.sbuf_tensor("vaug", [128, 16 * 1040], BF16))

            with ExitStack() as a2:
                xTbf = a2.enter_context(nc.sbuf_tensor("xTbf", [128, 8 * T], BF16))
                cosb = a2.enter_context(nc.sbuf_tensor("cosb", [128, T], BF16))
                sinb = a2.enter_context(nc.sbuf_tensor("sinb", [128, T], BF16))
                wqk = [a2.enter_context(nc.sbuf_tensor(f"wqk{i}", [128, 8 * 512], BF16))
                       for i in range(2)]
                wvb = a2.enter_context(nc.sbuf_tensor("wvb", [128, 8 * 1024], BF16))
                tmp1 = a2.enter_context(nc.sbuf_tensor("tmpa", [128, 512], BF16))
                tmp2 = a2.enter_context(nc.sbuf_tensor("tmpb", [128, 512], BF16))
                ebuf = a2.enter_context(nc.sbuf_tensor("ebuf", [128, 2 * 512], BF16))
                obuf = a2.enter_context(nc.sbuf_tensor("obuf", [128, 2 * 512], BF16))

                wqk2 = [a2.enter_context(
                    nc.sbuf_tensor(f"wqk2{i}", [128, 8 * 512], BF16))
                    for i in range(2)]
                ld_qeo = [p.dma("gpsimd", None, wqk[0][:, :], fm_src(wqet, D, 512)),
                          p.dma("gpsimd", None, wqk[1][:, :], fm_src(wqot, D, 512))]
                ld_xc = [p.dma("gpsimd", None,
                               bass.AP(xTbf.ap().tensor, cc * 512,
                                       [[8 * T, 128], [T, 8], [1, 512]]),
                               bass.AP(xT, cc * 512,
                                       [[T, 128], [128 * T, 8], [1, 512]]))
                         for cc in range(NC)]
                ld_k = [p.dma("gpsimd", None, wqk2[0][:, :], fm_src(wket, D, 512)),
                        p.dma("gpsimd", None, wqk2[1][:, :], fm_src(wkot, D, 512))]
                ld_v = p.dma("gpsimd", None, wvb[:, :], fm_src(wvt, D, D))
                ld_cs = [p.dma("sync", None, cosb[:, :], fm_src(cosd, 128, T)),
                         p.dma("sync", None, sinb[:, :], fm_src(sind, 128, T))]
                p.add("gpsimd", lambda g: g.memset(onesf[:, :], 1.0))
                p.add("gpsimd", lambda g: g.memset(onesb[:, :], 1.0))
                p.add("gpsimd", lambda g: g.memset(epsb[:, :], 1e-5))
                p.add("gpsimd", lambda g: g.memset(neg30[:, :], -1e30))
                for tt in range(NTT):
                    p.vop(dve, "memset",
                          vaug[:, tt * 1040 + 64:(tt + 1) * 1040:65], 1.0)

                it_dve_done = []
                it_evac_act = []
                itg = 0
                for wi, rT in enumerate((rq, rk)):
                    ld_pair = ld_qeo if wi == 0 else ld_k
                    wq = wqk if wi == 0 else wqk2
                    for m in range(4):
                        for c in range(NC):
                            G = PG[itg % 2]
                            if itg >= 2:
                                p.wait("tensor", acs, it_evac_act[itg - 2])
                            if m == 0 and c == 0:
                                p.waitt("tensor", ld_pair)
                            if wi == 0 and m == 0:
                                p.waitt("tensor", ld_xc[c])
                            if itg == 0:
                                p.waitt("vector", ld_cs)
                            for half in range(2):
                                for i in range(8):
                                    p.mm(pe, G[:, half * 512:half * 512 + 512],
                                         wq[half][:, i * 512 + m * 128:
                                                  i * 512 + m * 128 + 128],
                                         xTbf[:, i * T + c * 512:
                                              i * T + c * 512 + 512],
                                         start=(i == 0), stop=(i == 7),
                                         inc=(i == 7 and half == 1))
                            mm_done = pe.n
                            eb = ebuf[:, (itg % 2) * 512:(itg % 2) * 512 + 512]
                            ob = obuf[:, (itg % 2) * 512:(itg % 2) * 512 + 512]
                            p.wait("scalar", pe, mm_done)
                            if itg >= 2:
                                p.wait("scalar", dve, it_dve_done[itg - 2])
                            p.scopy(acs, eb, G[:, 0:512])
                            p.scopy(acs, ob, G[:, 512:1024], inc=True)
                            it_evac_act.append(acs.n)
                            cs = (slice(None),
                                  slice(c * 512, c * 512 + 512))
                            c0 = slice((2 * m) * T + c * 512,
                                       (2 * m) * T + c * 512 + 512)
                            c1 = slice((2 * m + 1) * T + c * 512,
                                       (2 * m + 1) * T + c * 512 + 512)
                            p.wait("vector", acs, acs.n)
                            p.vop(dve, "tensor_mul", tmp1[:, :], eb, cosb[cs])
                            p.vop(dve, "tensor_mul", tmp2[:, :], ob, sinb[cs])
                            p.vop(dve, "tensor_sub", rT[0:64, c0],
                                  tmp1[0:64, :], tmp2[0:64, :])
                            p.vop(dve, "tensor_sub", rT[0:64, c1],
                                  tmp1[64:128, :], tmp2[64:128, :])
                            p.vop(dve, "tensor_mul", tmp1[:, :], eb, sinb[cs])
                            p.vop(dve, "tensor_mul", tmp2[:, :], ob, cosb[cs])
                            p.vop(dve, "tensor_add", rT[64:128, c0],
                                  tmp1[0:64, :], tmp2[0:64, :])
                            p.vop(dve, "tensor_add", rT[64:128, c1],
                                  tmp1[64:128, :], tmp2[64:128, :],
                                  inc=True)
                            it_dve_done.append(dve.n)
                            itg += 1

                v_act = []
                vi = 0
                for tt in range(NTT):
                    for oc in range(2):
                        bk = 4 + (vi % 2)
                        if vi >= 2:
                            p.wait("tensor", acs, v_act[vi - 2])
                        if vi == 0:
                            p.waitt("tensor", ld_v)
                        for i in range(8):
                            p.mm(pe, P[bk][:, :],
                                 xTbf[:, i * T + tt * 128: i * T + tt * 128 + 128],
                                 wvb[:, i * 1024 + oc * 512:
                                     i * 1024 + oc * 512 + 512],
                                 start=(i == 0), stop=(i == 7), inc=(i == 7))
                        p.wait("scalar", pe, pe.n)
                        dst = bass.AP(vaug.ap().tensor, tt * 1040 + oc * 520,
                                      [[16 * 1040, 128], [65, 8], [1, 64]])
                        src = P[bk][:, 0:512].rearrange("p (h d) -> p h d", h=8)
                        p.scopy(acs, dst, src, inc=True)
                        v_act.append(acs.n)
                        vi += 1
                if "d_rq" in ddump or "d_rk" in ddump:
                    p.wait("sync", dve, dve.n)
                    for nm, tsr in (("d_rq", rq), ("d_rk", rk)):
                        if nm in ddump:
                            p.dma("sync", st, ddump[nm][:, :], tsr[:, :])
                if "d_vaug" in ddump:
                    p.wait("sync", acs, acs.n)
                    p.dma("sync", st, ddump["d_vaug"][:, :], vaug[:, :])
                p.run()
            a_pe, a_act, a_dve = pe.n, acs.n, dve.n

            y1f = ctx.enter_context(
                nc.sbuf_tensor("y1f", [128, 8 * T], F32, side="right"))

            with ExitStack() as bctx:
                wob = bctx.enter_context(nc.sbuf_tensor("wob", [128, 8 * 256], BF16))
                aout = bctx.enter_context(nc.sbuf_tensor("aout", [128, 8 * 512], BF16))
                exps = bctx.enter_context(nc.sbuf_tensor("exps", [128, 4 * 512], BF16))
                rrow = bctx.enter_context(nc.sbuf_tensor("rrow", [128, 512], F32))
                nbs0 = bctx.enter_context(nc.sbuf_tensor("nbs0", [128, 512], F32))
                maskb = bctx.enter_context(nc.sbuf_tensor("maskb", [128, 896], BF16))
                xf32 = bctx.enter_context(nc.sbuf_tensor("xf32", [128, 2 * 512], F32))
                z1bf = bctx.enter_context(nc.sbuf_tensor("z1bf", [128, 2 * 512], BF16))
                zsqb = bctx.enter_context(nc.sbuf_tensor("zsqb", [128, 2 * 512], BF16))
                stat = [bctx.enter_context(nc.sbuf_tensor(f"stat{i}", [128, 512], F32))
                        for i in range(3)]
                stat.append(stat[1])
                wstage = [bctx.enter_context(
                    nc.sbuf_tensor(f"wst{i}", [128, 1024], BF16)) for i in range(2)]
                rqz = bctx.enter_context(
                    nc.sbuf_tensor("rqz", [128, 2 * 1024], BF16))

                ld_msk = [p.dma("sync", None, maskb[:, :], maskd[:, :]),
                          p.dma("sync", None, idf[:, :], idfd[:, :]),
                          p.dma("sync", None, idb[:, :], idbd[:, :])]
                wo_ld = {}
                wot_st = {}
                cast_acts = {}
                xs_dve = [a_dve] * 2
                xs_ld = {}

                def x_slice_src(c, odt):
                    return bass.AP(xT, odt * 128 * T + c * 512,
                                   [[T, 128], [1, 512]])

                pieces = []
                for oh in range(4):
                    for qi in range(2):
                        pieces.append((bass.AP(wotq,
                                               oh * D * 256 + qi * 4 * 128 * 256,
                                               [[256, 128], [128 * 256, 4],
                                                [1, 256]]),
                                       wotc[oh, :, qi * 1024:(qi + 1) * 1024],
                                       "wot", oh))
                for key in range(16):
                    for qi in range(4):
                        pieces.append((bass.AP(w1h, key * D * 512 + qi * 2 * 128 * 512,
                                               [[512, 128], [128 * 512, 2], [1, 512]]),
                                       w1c[key, :, qi * 1024:(qi + 1) * 1024],
                                       "w1", key))
                for key in range(16):
                    eg_, dt_ = key // 8, key % 8
                    off0 = dt_ * (NE * D) * 128 + eg_ * 4096 * 128
                    for qi in range(4):
                        pieces.append((bass.AP(w2d, off0 + qi * 8 * 128 * 128,
                                               [[128, 128], [128 * 128, 8], [1, 128]]),
                                       w2c[key, :, qi * 1024:(qi + 1) * 1024],
                                       "w2", key))
                for key in range(4):
                    for qi in range(4):
                        pieces.append((bass.AP(sguq, key * D * 512 + qi * 2 * 128 * 512,
                                               [[512, 128], [128 * 512, 2], [1, 512]]),
                                       sguc[key, :, qi * 1024:(qi + 1) * 1024],
                                       "sgu", key))
                for key in range(8):
                    pieces.append((bass.AP(sdd, key * D * 128,
                                           [[128, 128], [128 * 128, 8], [1, 128]]),
                                   sddc[key, :, :], "sdd", key))
                pf_st = {}
                pf_slot = [None, None]
                pf_i = [0]

                def emit_pieces(n):
                    while n > 0 and pf_i[0] < len(pieces):
                        srcap, dstap, strm, key = pieces[pf_i[0]]
                        sl = pf_i[0] % 2
                        p.waitt("gpsimd", pf_slot[sl])
                        t_ld = p.dma("gpsimd", None,
                                     wstage[sl][:, 0:dstap.shape[-1]], srcap)
                        p.waitt("sync", t_ld)
                        t_st = p.dma("sync", None, dstap,
                                     wstage[sl][:, 0:dstap.shape[-1]])
                        pf_slot[sl] = t_st
                        pf_st.setdefault((strm, key), []).append(t_st)
                        pf_i[0] += 1
                        n -= 1
                exp_act = {}
                exp_rd = {}
                msk_dve = {}
                wo_free = [a_dve, a_dve]
                av_free = {0: a_dve, 1: a_dve, 2: a_dve}
                nbs_rd = [a_dve, a_dve]
                bcast_pe = [a_pe, a_pe]
                p7_dve = a_dve
                ln_dve = a_dve
                xz_dve = [a_dve] * NC
                wob_rd_pe = 0
                stats_pe = a_pe
                stats_pe_slot = [a_pe, a_pe]
                stats_q = []
                stats_pe_slot = [a_pe, a_pe]
                stats_q = []
                ip = 0
                gg = 0
                pending_norm = []
                pending_norm2 = []
                mr_dve_prev = a_dve
                gpend = [None]
                flush_av_last = None
                p.vop(dve, "memset", rqz[:, :], 0.0, inc=True)
                rqz_dve = {}
                rqz_free_pe = [a_pe, a_pe]

                def build_rqz(j):
                    if j >= 32:
                        return
                    cj, prj = j // 8, j % 8
                    sl = j % 2
                    p.wait("vector", pe, rqz_free_pe[sl])
                    base = prj * T + cj * 512
                    for hi, (r0a, r0b) in enumerate(((0, 64), (32, 96))):
                        o = sl * 1024 + hi * 512
                        p.vop(dve, "tensor_copy", rqz[r0a:r0a + 32, o:o + 512],
                              rq[r0a:r0a + 32, base:base + 512])
                        p.vop(dve, "tensor_copy", rqz[r0b:r0b + 32, o:o + 512],
                              rq[r0b:r0b + 32, base:base + 512],
                              inc=(hi == 1))
                    rqz_dve[j] = dve.n
                for c in range(NC):
                    nmt = 4 * c + 4
                    for odt in range(2):
                        p.wait("scalar", dve, xs_dve[odt])
                        xs_ld[(c, odt)] = p.dma(
                            "scalar", None, xf32[:, odt * 512:(odt + 1) * 512],
                            x_slice_src(c, odt))
                    if c > 0:
                        p.wait("tensor", dve, mr_dve_prev)

                    def emit_norm_s1(ent):
                        h0n, bksn, dd = ent
                        recs = [0, 0]
                        for hi in range(2):
                            avb = P4_7[bksn[hi]]
                            rr = rrow[64 + 32 * hi:65 + 32 * hi, :]
                            p.wait("scalar", pe, dd["av_done"][hi])
                            p.wait("scalar", pe, bcast_pe[hi])
                            p.act(acs, rr, avb[64:65, :], AF.Ln, inc=True)
                            p.wait("scalar", acs, acs.n)
                            p.act(acs, rr, rr, AF.Exp, scale=-1.0, inc=True)
                            recs[hi] = acs.n
                        pending_norm2.append((h0n, bksn, recs))

                    def emit_norm_s2(ent2):
                        nonlocal p7_dve
                        h0n, bksn, recs = ent2
                        un = h0n // 2
                        for hi in range(2):
                            rr = rrow[64 + 32 * hi:65 + 32 * hi, :]
                            p.wait("tensor", acs, recs[hi])
                            p.wait("tensor", dve, p7_dve)
                            p.mm(pe, P[7][64 * hi:64 * hi + 64, :],
                                 onesf[64 + 32 * hi:65 + 32 * hi, 0:64],
                                 rr, start=True, stop=True,
                                 tp=(64 + 32 * hi, 64 * hi), inc=True)
                            bcast_pe[hi] = pe.n
                        for hi in range(2):
                            avb = P4_7[bksn[hi]]
                            p.wait("vector", pe, bcast_pe[hi])
                            p.vop(dve, "tensor_copy",
                                  nbs0[hi * 64:hi * 64 + 64, :],
                                  P[7][64 * hi:64 * hi + 64, :], inc=True)
                            p.vop(dve, "tensor_mul",
                                  aout[hi * 64:hi * 64 + 64,
                                       un * 512:un * 512 + 512],
                                  avb[0:64, :],
                                  nbs0[hi * 64:hi * 64 + 64, :], inc=True)
                            av_free[bksn[hi]] = dve.n
                            nbs_rd[hi] = dve.n
                        p7_dve = dve.n
                    for pp in range(4):
                        for sub in range(2):
                            h0 = 4 * pp + 2 * sub
                            av_done = [0, 0]
                            bks = [(2 * ip) % 3, (2 * ip + 1) % 3]
                            def flush_av(pend):
                                nonlocal flush_av_last
                                pgg, pmt, pdiag, ph0, pbks, avd, pnmt = pend
                                for hi in range(2):
                                    h = ph0 + hi
                                    if pdiag:
                                        p.wait("tensor", dve, msk_dve[pgg])
                                    else:
                                        p.wait("tensor", acs, exp_act[pgg])
                                    avb = P4_7[pbks[hi]]
                                    if pmt == 0:
                                        p.wait("tensor", dve, av_free[pbks[hi]])
                                    slh = exps[:, (pgg % 2) * 1024 + hi * 512:
                                               (pgg % 2) * 1024 + hi * 512 + 512]
                                    p.mm(pe, avb[0:65, :],
                                         vaug[:, pmt * 1040 + 65 * h:
                                              pmt * 1040 + 65 * h + 65],
                                         slh, start=(pmt == 0),
                                         stop=(pmt == pnmt - 1),
                                         inc=(pmt == pnmt - 1))
                                    if pmt == pnmt - 1:
                                        avd[hi] = pe.n
                                exp_rd[pgg] = pe.n
                            flush_av_last = flush_av
                            if ip == 0:
                                build_rqz(0)
                            build_rqz(ip + 1)
                            pr = h0 // 2
                            for mt in range(nmt):
                                G = PG[gg % 2]
                                if gg >= 2:
                                    p.wait("tensor", acs, exp_act[gg - 2])
                                if mt == 0:
                                    p.wait("tensor", dve, rqz_dve[ip])
                                lo = pr * T + mt * 128
                                zsl = (ip % 2) * 1024
                                for hi in range(2):
                                    half = G[:, hi * 512:(hi + 1) * 512]
                                    p.mm(pe, half, rk[:, lo:lo + 128],
                                         rqz[:, zsl + hi * 512:
                                             zsl + hi * 512 + 512],
                                         start=True, stop=True,
                                         inc=(hi == 1))
                                sc_pe = pe.n
                                p.wait("scalar", pe, sc_pe)
                                if gg >= 2:
                                    p.wait("scalar", pe, exp_rd[gg - 2])
                                sl2 = exps[:, (gg % 2) * 1024:(gg % 2) * 1024 + 1024]
                                p.act(acs, sl2, G[:, :], AF.Exp, scale=0.125,
                                      inc=True)
                                exp_act[gg] = acs.n
                                diag = mt >= 4 * c
                                if diag:
                                    kd = mt - 4 * c
                                    p.wait("vector", acs, exp_act[gg])
                                    if gg == 0:
                                        p.waitt("vector", ld_msk)
                                    for hi in range(2):
                                        slh = exps[:, (gg % 2) * 1024 + hi * 512:
                                                   (gg % 2) * 1024 + hi * 512 + 512]
                                        p.vop(dve, "tensor_mul", slh, slh,
                                              maskb[:, 384 - 128 * kd:
                                                    384 - 128 * kd + 512],
                                              inc=(hi == 1))
                                    msk_dve[gg] = dve.n
                                if mt == 1:
                                    while pending_norm2:
                                        emit_norm_s2(pending_norm2.pop(0))
                                if gpend[0] is not None:
                                    flush_av(gpend[0])
                                    gpend[0] = None
                                if mt == 0:
                                    while pending_norm:
                                        emit_norm_s1(pending_norm.pop(0))
                                gpend[0] = (gg, mt, diag, h0, bks, av_done, nmt)
                                gg += 1
                            pending_norm.append((h0, bks, dict(av_done=av_done)))
                            emit_pieces(5)
                            rqz_free_pe[ip % 2] = sc_pe
                            ip += 1
                    if gpend[0] is not None:
                        flush_av_last(gpend[0])
                        gpend[0] = None
                    while pending_norm:
                        emit_norm_s1(pending_norm.pop(0))
                    while pending_norm2:
                        emit_norm_s2(pending_norm2.pop(0))
                    aout_dve, aout_act = dve.n, acs.n

                    for oh in range(4):
                        p.wait("scalar", pe, wob_rd_pe)
                        if c == 0:
                            p.waitt("scalar", pf_st[("wot", oh)])
                        wo_ld[(c, oh)] = p.dma("scalar", None, wob[:, :],
                                               wotc[oh, :, :])
                        for odt in range(2 * oh, 2 * oh + 2):
                            bk = odt % 2
                            if odt == 0:
                                p.wait("tensor", dve, aout_dve)
                                p.wait("tensor", acs, aout_act)
                            if odt % 2 == 0:
                                p.waitt("tensor", wo_ld[(c, oh)])
                            if odt >= 2:
                                p.wait("tensor", dve, wo_free[bk])
                            for i in range(8):
                                p.mm(pe, P[bk][:, :],
                                     wob[:, i * 256 + (odt % 2) * 128:
                                         i * 256 + (odt % 2) * 128 + 128],
                                     aout[:, i * 512: i * 512 + 512],
                                     start=(i == 0), stop=(i == 7), inc=(i == 7))
                            wo_pe = pe.n
                            ysl = y1f[:, odt * T + c * 512: odt * T + c * 512 + 512]
                            p.wait("vector", pe, wo_pe)
                            p.waitt("vector", xs_ld[(c, odt)])
                            sx = odt % 2
                            p.vop(dve, "tensor_add", ysl, P[bk][:, :],
                                  xf32[:, sx * 512: sx * 512 + 512], inc=True)
                            wo_free[bk] = dve.n
                            xs_dve[sx] = dve.n
                            if odt < 6:
                                p.wait("scalar", dve, xs_dve[sx])
                                xs_ld[(c, odt + 2)] = p.dma(
                                    "scalar", None,
                                    xf32[:, sx * 512:(sx + 1) * 512],
                                    x_slice_src(c, odt + 2))
                            zb = z1bf[:, (odt % 2) * 512:(odt % 2) * 512 + 512]
                            qb = zsqb[:, (odt % 2) * 512:(odt % 2) * 512 + 512]
                            p.wait("scalar", dve, wo_free[bk])
                            p.wait("scalar", pe, stats_pe_slot[odt % 2])
                            p.scopy(acs, zb, ysl)
                            p.act(acs, qb, zb, AF.Square, inc=True)
                            cast_acts[odt] = acs.n
                            if stats_q:
                                (podt, pzb, pqb, pca) = stats_q.pop(0)
                                p.wait("tensor", acs, pca)
                                p.mm(pe, P[2][0:1, :], onesb[:, 0:1], pzb,
                                     start=(podt == 0), stop=(podt == 7))
                                p.mm(pe, P[3][0:1, :], onesb[:, 0:1], pqb,
                                     start=(podt == 0), stop=(podt == 7),
                                     inc=True)
                                stats_pe_slot[podt % 2] = pe.n
                            stats_q.append((odt, zb, qb, acs.n))
                        wob_rd_pe = pe.n
                    while stats_q:
                        (podt, pzb, pqb, pca) = stats_q.pop(0)
                        p.wait("tensor", acs, pca)
                        p.mm(pe, P[2][0:1, :], onesb[:, 0:1], pzb,
                             start=(podt == 0), stop=(podt == 7))
                        p.mm(pe, P[3][0:1, :], onesb[:, 0:1], pqb,
                             start=(podt == 0), stop=(podt == 7), inc=True)
                        stats_pe_slot[podt % 2] = pe.n
                    stats_pe = pe.n
                    while stats_q:
                        (podt, pzb, pqb, pca) = stats_q.pop(0)
                        p.wait("tensor", acs, pca)
                        p.mm(pe, P[2][0:1, :], onesb[:, 0:1], pzb,
                             start=(podt == 0), stop=(podt == 7))
                        p.mm(pe, P[3][0:1, :], onesb[:, 0:1], pqb,
                             start=(podt == 0), stop=(podt == 7), inc=True)
                        stats_pe_slot[podt % 2] = pe.n
                    stats_pe = pe.n
                    p.wait("vector", pe, stats_pe)
                    p.vop(dve, "tensor_scalar", stat[0][0:1, :], P[2][0:1, :],
                          1.0 / D, None, ALU.mult)
                    p.vop(dve, "tensor_scalar", stat[1][0:1, :], P[3][0:1, :],
                          1.0 / D, None, ALU.mult, inc=True)
                    p.wait("vector", dve, dve.n)
                    p.vop(dve, "tensor_mul", stat[2][0:1, :], stat[0][0:1, :],
                          stat[0][0:1, :], inc=True)
                    p.wait("vector", dve, dve.n)
                    p.vop(dve, "tensor_sub", stat[1][0:1, :], stat[1][0:1, :],
                          stat[2][0:1, :], inc=True)
                    var_dve = dve.n
                    p.wait("scalar", dve, var_dve)
                    p.act(acs, stat[2][0:1, :], stat[1][0:1, :], AF.Ln,
                          bias=epsb[0:1, :], scale=1.0, inc=True)
                    p.wait("scalar", acs, acs.n)
                    p.act(acs, stat[2][0:1, :], stat[2][0:1, :], AF.Exp,
                          scale=-0.5, inc=True)
                    rstd_act = acs.n
                    p.wait("vector", acs, rstd_act)
                    p.vop(dve, "tensor_mul", stat[3][0:1, :], stat[0][0:1, :],
                          stat[2][0:1, :], inc=True)
                    p.wait("vector", dve, dve.n)
                    mr_dve = dve.n
                    p.wait("tensor", dve, mr_dve)
                    p.wait("tensor", dve, av_free[0])
                    p.wait("tensor", dve, av_free[1])
                    p.mm(pe, P[4][:, :], onesf[0:1, 0:128], stat[2][0:1, :],
                         start=True, stop=True)
                    p.mm(pe, P[5][:, :], onesf[0:1, 0:128], stat[3][0:1, :],
                         start=True, stop=True, inc=True)
                    bc2_pe = pe.n
                    for odt in range(8):
                        ysl = y1f[:, odt * T + c * 512: odt * T + c * 512 + 512]
                        p.wait("vector", pe, bc2_pe)
                        p.vop(dve, "tensor_mul", ysl, ysl, P[4][:, :])
                        p.vop(dve, "tensor_sub", ysl, ysl, P[5][:, :],
                              inc=(odt == 7))
                    ln_dve = dve.n
                    av_free[0] = ln_dve
                    av_free[1] = ln_dve
                    mr_dve_prev = mr_dve
                    xz_dve[c] = dve.n
                emit_pieces(len(pieces))
                pf_state = (pf_st,)
                if "d_aout" in ddump:
                    p.wait("sync", dve, dve.n)
                    p.dma("sync", st, ddump["d_aout"][:, :], aout[:, :])
                if "d_y1" in ddump:
                    p.wait("sync", dve, dve.n)
                    p.dma("sync", st, ddump["d_y1"][:, :], y1f[:, :])
                p.run()
        b_dve, b_act, b_pe = dve.n, acs.n, pe.n

        with ExitStack() as cctx:
            y1bf = cctx.enter_context(nc.sbuf_tensor("y1bf", [128, 8 * 512], BF16))
            hgrp = cctx.enter_context(nc.sbuf_tensor("hgrp", [128, 32 * 512], BF16))
            accs = cctx.enter_context(nc.sbuf_tensor("accs", [128, 8 * 512], F32))
            w1b = [cctx.enter_context(nc.sbuf_tensor(f"w1b{i}", [128, 8 * 512], BF16))
                   for i in range(2)]
            w2b = [cctx.enter_context(nc.sbuf_tensor(f"w2b{i}", [128, 32 * 128], BF16))
                   for i in range(2)]
            sgub = [cctx.enter_context(nc.sbuf_tensor(f"sgub{i}", [128, 8 * 512], BF16))
                    for i in range(2)]
            sddb = [cctx.enter_context(nc.sbuf_tensor(f"sddb{i}", [128, 8 * 128], BF16))
                    for i in range(2)]
            sgb = cctx.enter_context(nc.sbuf_tensor("sgb", [128, 8 * 512], BF16))
            spre = cctx.enter_context(nc.sbuf_tensor("spre", [128, 8 * 512], BF16))
            hsil = cctx.enter_context(nc.sbuf_tensor("hsil", [128, 2 * 512], BF16))
            rwb = cctx.enter_context(nc.sbuf_tensor("rwb", [128, 64], BF16))
            sgwb = cctx.enter_context(nc.sbuf_tensor("sgwb", [128, 8], BF16))
            comb = [cctx.enter_context(nc.sbuf_tensor(f"comb{i}", [128, 512], BF16))
                    for i in range(8)]
            probs = cctx.enter_context(nc.sbuf_tensor("probs", [128, 8], F32))
            ctm = cctx.enter_context(nc.sbuf_tensor("ctm", [128, 8], F32))
            mt1 = cctx.enter_context(nc.sbuf_tensor("mt1", [128, 8], F32))
            lt1 = cctx.enter_context(nc.sbuf_tensor("lt1", [128, 8], F32))
            rsc = [cctx.enter_context(nc.sbuf_tensor(f"rsc{i}", [128, 1], F32))
                   for i in range(5)]
            grow = cctx.enter_context(nc.sbuf_tensor("grow", [128, 512], F32))
            tmpc = cctx.enter_context(nc.sbuf_tensor("tmpc", [128, 512], BF16))

            ld_rw = [p.dma("gpsimd", None, rwb[:, :], rwp[:, :]),
                     p.dma("gpsimd", None, sgwb[:, :], sgwp[:, :])]

            w1_free = [b_pe] * 2
            w2_free = [b_pe] * 2
            sgu_free = [b_pe] * 2
            sdd_free = [b_pe] * 2
            w1_st, w2_st, sgu_st, sdd_st = {}, {}, {}, {}
            w1_stslot = [None] * 2
            w2_stslot = [None] * 2
            sgu_stslot = [None] * 2
            sdd_stslot = [None] * 2
            y1bf_pe = b_pe
            routed_dve = [b_dve] * 4
            hs_act = [b_act] * 2
            hsil_rd = [b_dve] * 2
            cbc_rd = [b_dve] * 2
            gu_rd = [("dve", b_dve), ("dve", b_dve)]
            z2_dve = [0] * NC
            gbc_pe_prev = b_pe
            hgrp_w2_pe = b_pe

            for c in range(NC):
                pe_start = pe.n
                if c > 0:
                    p.wait("tensor", dve, z2_dve[c - 1])
                    p.wait("vector", pe, pe_start)
                    p.wait("scalar", dve, z2_dve[c - 1])
                p.wait("scalar", pe, y1bf_pe)
                for dt in range(8):
                    p.scopy(acs, y1bf[:, dt * 512: dt * 512 + 512],
                            y1f[:, dt * T + c * 512: dt * T + c * 512 + 512],
                            inc=(dt == 7))
                y1bf_act = acs.n
                p.wait("tensor", acs, y1bf_act)
                if c == 0:
                    p.waitt("tensor", ld_rw)
                p.wait("tensor", dve, routed_dve[3])
                for ts in range(4):
                    for dt in range(8):
                        p.mm(pe, P[7][:, ts * 8: ts * 8 + 8],
                             y1bf[:, dt * 512 + ts * 128: dt * 512 + ts * 128 + 128],
                             rwb[:, dt * 8: dt * 8 + 8],
                             start=(dt == 0), stop=(dt == 7), inc=(dt == 7))
                rt_pe = pe.n
                t_pe = 0
                for ts in range(4):
                    psl = P[7][:, ts * 8: ts * 8 + 8]
                    p.wait("vector", pe, rt_pe)
                    if ts > 0:
                        p.wait("vector", pe, t_pe)
                    p.vop(dve, "reduce_max", rsc[0][:, :], psl, axis=AX.X,
                          negate=True)
                    p.vop(dve, "reduce_max", rsc[1][:, :], psl, axis=AX.X,
                          inc=True)
                    p.wait("vector", dve, dve.n)
                    p.vop(dve, "tensor_single_scalar", mt1[:, :], psl,
                          rsc[1][:, :], ALU.is_ge, inc=True)
                    m1_dve = dve.n
                    p.wait("vector", dve, m1_dve)
                    p.wait("scalar", dve, m1_dve)
                    p.act(acs, probs[:, :], psl, AF.Exp, bias=rsc[0][:, :],
                          scale=1.0, accum=rsc[2][:, :], inc=True)
                    e_act = acs.n
                    p.wait("vector", acs, e_act)
                    p.vop(dve, "reciprocal", rsc[3][:, :], rsc[2][:, :], inc=True)
                    p.wait("vector", dve, dve.n)
                    p.vop(dve, "scalar_tensor_tensor", lt1[:, :], mt1[:, :],
                          neg30[:, 0:1], psl, ALU.mult, ALU.add, inc=True)
                    p.wait("vector", dve, dve.n)
                    p.vop(dve, "reduce_max", rsc[4][:, :], lt1[:, :], axis=AX.X,
                          inc=True)
                    p.wait("vector", dve, dve.n)
                    p.vop(dve, "tensor_single_scalar", mt1[:, :], psl,
                          rsc[4][:, :], ALU.is_ge, inc=True)
                    p.wait("vector", dve, dve.n)
                    p.vop(dve, "tensor_scalar", probs[:, :], probs[:, :],
                          rsc[3][:, :], None, ALU.mult, inc=True)
                    p.wait("vector", dve, dve.n)
                    p.vop(dve, "tensor_mul", ctm[:, :], probs[:, :], mt1[:, :],
                          inc=True)
                    c_dve = dve.n
                    for eb in range(2):
                        p.wait("tensor", dve, c_dve)
                        p.wait("tensor", acs, acs.n)
                        for e4 in range(4):
                            e = eb * 4 + e4
                            p.trans(pe, P[6][0:1, e4 * 128: e4 * 128 + 128],
                                    ctm[:, e:e + 1], idf[:, :], inc=(e4 == 3))
                        t_pe = pe.n
                        p.wait("scalar", pe, t_pe)
                        for e4 in range(4):
                            e = eb * 4 + e4
                            p.scopy(acs, comb[e][0:1, ts * 128: ts * 128 + 128],
                                    P[6][0:1, e4 * 128: e4 * 128 + 128],
                                    inc=(e4 == 3))
                comb_act = acs.n
                cbc_rd[0] = max(cbc_rd[0], dve.n)
                sgu_ld = {}
                sdd_ld = {}
                w1_lds2 = {}

                def w1load(s, c=c):
                    slot = s % 2
                    p.wait("scalar", pe, w1_free[slot])
                    if c == 0:
                        p.waitt("scalar", pf_st[("w1", s)])
                    w1_lds2[s] = p.dma("scalar", None, w1b[slot][:, :],
                                       w1c[s, :, :])
                w1load(0)
                w1load(1)

                def sgudma(q, c=c):
                    slot = q % 2
                    p.wait("scalar", pe, sgu_free[slot])
                    if c == 0:
                        p.waitt("scalar", pf_st[("sgu", q)])
                    sgu_ld[q] = p.dma("scalar", None, sgub[slot][:, :],
                                      sguc[q, :, :])

                def sdddma(dt, c=c):
                    slot = dt % 2
                    p.wait("sync", pe, sdd_free[slot])
                    if c == 0:
                        p.waitt("sync", pf_st[("sdd", dt)])
                    sdd_ld[dt] = p.dma("sync", None, sddb[slot][:, :],
                                       sddc[dt, :, :])

                for eg in range(2):
                    w2_ld = {}

                    def w2dma(dt, eg=eg, c=c):
                        slot = dt % 2
                        key = eg * 8 + dt
                        p.wait("sync", pe, w2_free[slot])
                        if c == 0:
                            p.waitt("sync", pf_st[("w2", key)])
                        w2_ld[dt] = p.dma("sync", None, w2b[slot][:, :],
                                          w2c[key, :, :])
                    w2dma(0)
                    w2dma(1)
                    if eg == 1:
                        sgudma(0)
                        sgudma(1)
                        sdddma(0)
                        sdddma(1)
                        p.wait("vector", pe, hgrp_w2_pe)
                    for ei in range(4):
                        e = 4 * eg + ei
                        cb = 6 + (e % 2)
                        p.wait("tensor", acs, comb_act)
                        p.wait("tensor", dve, cbc_rd[e % 2])
                        p.mm(pe, P[cb][:, :], onesb[0:1, 0:128],
                             comb[e][0:1, :], start=True, stop=True, inc=True)
                        cbc_mm = pe.n
                        for fh in range(2):
                            slot = (2 * e + fh) % 2
                            p.waitt("tensor", w1_lds2[2 * e + fh])
                            for ft in range(4):
                                f = fh * 4 + ft
                                hb = 4 + (f % 2)
                                p.wait("tensor", acs, hs_act[f % 2])
                                for dt in range(8):
                                    p.mm(pe, P[hb][:, :],
                                         w1b[slot][:, dt * 512 + ft * 128:
                                                   dt * 512 + ft * 128 + 128],
                                         y1bf[:, dt * 512: dt * 512 + 512],
                                         start=(dt == 0), stop=(dt == 7),
                                         inc=(dt == 7))
                                h_pe = pe.n
                                p.wait("scalar", pe, h_pe)
                                p.wait("scalar", dve, hsil_rd[f % 2])
                                hs = hsil[:, (f % 2) * 512:(f % 2) * 512 + 512]
                                p.act(acs, hs, P[hb][:, :], AF.Silu, inc=True)
                                hs_act[f % 2] = acs.n
                                p.wait("vector", acs, hs_act[f % 2])
                                p.wait("vector", pe, cbc_mm)
                                p.vop(dve, "tensor_mul",
                                      hgrp[:, (ei * 8 + f) * 512:
                                           (ei * 8 + f) * 512 + 512],
                                      hs, P[cb][:, :], inc=True)
                                hsil_rd[f % 2] = dve.n
                            w1_free[slot] = pe.n
                            if 2 * e + fh + 2 < 16:
                                w1load(2 * e + fh + 2)
                        cbc_rd[e % 2] = dve.n
                    hg_dve = dve.n
                    for dt in range(8):
                        slot = dt % 2
                        bk = dt % 4
                        p.waitt("tensor", w2_ld[dt])
                        if dt == 0:
                            p.wait("tensor", dve, hg_dve)
                        p.wait("tensor", dve, routed_dve[bk])
                        for kt in range(32):
                            p.mm(pe, P[bk][:, :],
                                 w2b[slot][:, kt * 128: kt * 128 + 128],
                                 hgrp[:, kt * 512: kt * 512 + 512],
                                 start=(kt == 0), stop=(kt == 31), inc=(kt == 31))
                        w2_free[slot] = pe.n
                        if dt + 2 < 8:
                            w2dma(dt + 2)
                        r_pe = pe.n
                        p.wait("vector", pe, r_pe)
                        asl = accs[:, dt * 512: dt * 512 + 512]
                        ysl = y1f[:, dt * T + c * 512: dt * T + c * 512 + 512]
                        if eg == 0:
                            p.vop(dve, "tensor_add", asl, P[bk][:, :], ysl,
                                  inc=True)
                        else:
                            p.vop(dve, "tensor_add", ysl, P[bk][:, :], asl,
                                  inc=True)
                        routed_dve[bk] = dve.n
                    hgrp_w2_pe = pe.n

                p.wait("tensor", dve, routed_dve[0])
                p.wait("tensor", acs, hs_act[0])
                p.wait("tensor", pe, gbc_pe_prev)
                for dt in range(8):
                    p.mm(pe, P[4][0:1, :], sgwb[:, dt:dt + 1],
                         y1bf[:, dt * 512: dt * 512 + 512],
                         start=(dt == 0), stop=(dt == 7), inc=(dt == 7))
                gw_pe = pe.n
                p.wait("scalar", pe, gw_pe)
                p.scopy(acs, grow[0:1, :], P[4][0:1, :], inc=True)
                p.wait("scalar", acs, acs.n)
                p.act(acs, grow[0:1, :], grow[0:1, :], AF.Sigmoid, inc=True)
                g_act = acs.n
                p.wait("tensor", acs, g_act)
                p.wait("tensor", acs, hs_act[1])
                p.mm(pe, P[5][:, :], onesf[0:1, 0:128], grow[0:1, :],
                     start=True, stop=True, inc=True)
                gbc_pe = pe.n
                gbc_pe_prev = gbc_pe
                sg_act = acs.n
                for q in range(4):
                    slot = q % 2
                    p.waitt("tensor", sgu_ld[q])
                    for ft2 in range(4):
                        f2 = q * 4 + ft2
                        par = f2 % 2
                        gb = 4 if par == 0 else 6
                        if f2 == 0:
                            p.wait("tensor", acs, g_act)
                        if f2 == 1:
                            p.wait("tensor", dve, cbc_rd[0])
                        kind, val = gu_rd[par]
                        if kind == "acs":
                            p.wait("tensor", acs, val)
                        else:
                            p.wait("tensor", dve, val)
                        for dt in range(8):
                            p.mm(pe, P[gb][:, :],
                                 sgub[slot][:, dt * 512 + ft2 * 128:
                                            dt * 512 + ft2 * 128 + 128],
                                 y1bf[:, dt * 512: dt * 512 + 512],
                                 start=(dt == 0), stop=(dt == 7), inc=(dt == 7))
                        gu_pe = pe.n
                        if q < 2:
                            p.wait("scalar", pe, gu_pe)
                            p.act(acs, sgb[:, f2 * 512: f2 * 512 + 512],
                                  P[gb][:, :], AF.Silu, inc=True)
                            sg_act = acs.n
                            gu_rd[par] = ("acs", acs.n)
                        else:
                            fu = f2 - 8
                            p.wait("vector", pe, gu_pe)
                            p.wait("vector", pe, gbc_pe)
                            p.wait("vector", acs, sg_act)
                            p.vop(dve, "tensor_mul", tmpc[:, :], P[gb][:, :],
                                  sgb[:, fu * 512: fu * 512 + 512])
                            p.vop(dve, "tensor_mul",
                                  spre[:, fu * 512: fu * 512 + 512],
                                  tmpc[:, :], P[5][:, :], inc=True)
                            gu_rd[par] = ("dve", dve.n)
                    sgu_free[slot] = pe.n
                    if q + 2 < 4:
                        sgudma(q + 2)
                sp_dve = dve.n
                for dt in range(8):
                    slot = dt % 2
                    bk = dt % 4
                    p.waitt("tensor", sdd_ld[dt])
                    if dt == 0:
                        p.wait("tensor", dve, sp_dve)
                    p.wait("tensor", dve, routed_dve[bk])
                    for ft in range(8):
                        p.mm(pe, P[bk][:, :],
                             sddb[slot][:, ft * 128: ft * 128 + 128],
                             spre[:, ft * 512: ft * 512 + 512],
                             start=(ft == 0), stop=(ft == 7), inc=(ft == 7))
                    sdd_free[slot] = pe.n
                    if dt + 2 < 8:
                        sdddma(dt + 2)
                    d_pe = pe.n
                    p.wait("vector", pe, d_pe)
                    ysl = y1f[:, dt * T + c * 512: dt * T + c * 512 + 512]
                    p.vop(dve, "tensor_add", ysl, P[bk][:, :], ysl, inc=True)
                    routed_dve[bk] = dve.n
                z2_dve[c] = dve.n
                y1bf_pe = pe.n
            if "d_hg" in ddump:
                p.wait("sync", dve, dve.n)
                p.dma("sync", st, ddump["d_hg"][:, :], hgrp[:, :])
            if "d_comb" in ddump:
                p.wait("sync", acs, acs.n)
                for e in range(NE):
                    p.dma("sync", st, ddump["d_comb"][e:e + 1, :], comb[e][0:1, :])
            if "d_ctm" in ddump:
                p.wait("sync", dve, dve.n)
                p.dma("sync", st, ddump["d_ctm"][:, :], ctm[:, :])
                p.dma("sync", st, ddump["d_probs"][:, :], probs[:, :])
                p.dma("sync", st, ddump["d_mt1"][:, :], mt1[:, :])
            if "d_spre" in ddump:
                p.wait("sync", dve, dve.n)
                p.dma("sync", st, ddump["d_spre"][:, :], spre[:, :])
            if "d_accs" in ddump:
                p.dma("sync", st, ddump["d_accs"][:, :], accs[:, :])
            if "d_grow" in ddump:
                p.dma("sync", st, ddump["d_grow"][:, :], grow[0:1, :])
            if "d_z2" in ddump:
                p.wait("sync", dve, dve.n)
                p.dma("sync", st, ddump["d_z2"][:, :], y1f[:, :])
            p.run()

        with ExitStack() as dctx:
            ztm = [dctx.enter_context(nc.sbuf_tensor(f"ztm{i}", [128, 1024], F32))
                   for i in range(2)]
            y2s = [dctx.enter_context(nc.sbuf_tensor(f"y2s{i}", [128, 1024], F32))
                   for i in range(2)]
            sqs = dctx.enter_context(nc.sbuf_tensor("sqs", [128, 1024], BF16))
            dS = dctx.enter_context(nc.sbuf_tensor("dS", [128, 1], F32))
            dS2 = dctx.enter_context(nc.sbuf_tensor("dS2", [128, 1], F32))
            dm = dctx.enter_context(nc.sbuf_tensor("dm", [128, 1], F32))
            dv2 = dctx.enter_context(nc.sbuf_tensor("dv2", [128, 1], F32))
            dr = dctx.enter_context(nc.sbuf_tensor("dr", [128, 1], F32))

            tile_st = [0, 0]
            tr_free = [0, 0]
            for tt in range(NTT):
                sl = tt % 2
                for dt in range(8):
                    bki = dt // 4
                    if tt >= 1 and dt % 4 == 0:
                        p.wait("tensor", acs, tr_free[bki])
                    p.trans(pe, P[4 + bki][:, (dt % 4) * 128:(dt % 4) * 128 + 128],
                            y1f[:, dt * T + tt * 128: dt * T + tt * 128 + 128],
                            idf[:, :], inc=(dt % 4 == 3))
                tr_pe = pe.n
                p.wait("scalar", pe, tr_pe)
                p.wait("scalar", st, tile_st[sl])
                p.scopy(acs, ztm[sl][:, 0:512], P[4][:, :])
                p.scopy(acs, ztm[sl][:, 512:1024], P[5][:, :], inc=True)
                tr_free[0] = acs.n
                tr_free[1] = acs.n
                zt_act = acs.n
                p.wait("vector", acs, zt_act)
                p.vop(dve, "reduce_sum", dS[:, :], ztm[sl][:, :], axis=AX.X,
                      inc=True)
                p.wait("scalar", dve, dve.n)
                p.act(acs, sqs[:, :], ztm[sl][:, :], AF.Square, accum=dS2[:, :],
                      inc=True)
                sq_act = acs.n
                p.wait("vector", acs, sq_act)
                p.vop(dve, "tensor_scalar", dm[:, :], dS[:, :], 1.0 / D, None,
                      ALU.mult)
                p.vop(dve, "tensor_scalar", dv2[:, :], dS2[:, :], 1.0 / D, None,
                      ALU.mult, inc=True)
                p.wait("vector", dve, dve.n)
                p.vop(dve, "tensor_mul", dr[:, :], dm[:, :], dm[:, :], inc=True)
                p.wait("vector", dve, dve.n)
                p.vop(dve, "tensor_sub", dv2[:, :], dv2[:, :], dr[:, :], inc=True)
                p.wait("scalar", dve, dve.n)
                p.act(acs, dr[:, :], dv2[:, :], AF.Ln, bias=epsb[:, :], scale=1.0,
                      inc=True)
                p.wait("scalar", acs, acs.n)
                p.act(acs, dr[:, :], dr[:, :], AF.Exp, scale=-0.5, inc=True)
                r_act = acs.n
                p.wait("vector", acs, r_act)
                p.vop(dve, "tensor_scalar", y2s[sl][:, :], ztm[sl][:, :],
                      dm[:, :], dr[:, :], ALU.subtract, ALU.mult, inc=True)
                y2_dve = dve.n
                p.wait("sync", dve, y2_dve)
                p.dma("sync", st, out[tt * 128:(tt + 1) * 128, :], y2s[sl][:, :])
                tile_st[sl] = st.n
            p.wait("sync", st, st.n)
            p.run()
    return nc


_cache = {}


def _host_prep(L0_attn_Wq, L0_attn_Wk, L0_attn_Wv, L0_attn_Wo, L0_ffn_router_w,
               L0_ffn_w1, L0_ffn_w2, L0_ffn_shared_gate_up, L0_ffn_shared_down,
               L0_ffn_shared_gate_w):
    f = np.float32
    perm_e = np.array([64 * h + 2 * j for h in range(H) for j in range(32)])
    perm_o = perm_e + 1
    Wq = np.asarray(L0_attn_Wq, f)
    Wk = np.asarray(L0_attn_Wk, f)
    d = {
        "wqet": np.ascontiguousarray(Wq[perm_e].T),
        "wqot": np.ascontiguousarray(Wq[perm_o].T),
        "wket": np.ascontiguousarray(Wk[perm_e].T),
        "wkot": np.ascontiguousarray(Wk[perm_o].T),
        "wvt": np.ascontiguousarray(np.asarray(L0_attn_Wv, f).T),
        "wotq": np.ascontiguousarray(
            np.asarray(L0_attn_Wo, f).T.reshape(D, 4, 256).transpose(1, 0, 2)),
        "rwp": np.ascontiguousarray(
            np.asarray(L0_ffn_router_w, f).reshape(8, 128, 8)
            .transpose(1, 0, 2).reshape(128, 64)),
        "w1h": np.ascontiguousarray(
            np.asarray(L0_ffn_w1, f).reshape(NE, D, 2, 512).transpose(0, 2, 1, 3)),
        "w2d": np.ascontiguousarray(
            np.asarray(L0_ffn_w2, f).reshape(NE * D, 8, 128).transpose(1, 0, 2)),
        "sguq": np.ascontiguousarray(
            np.asarray(L0_ffn_shared_gate_up, f).reshape(D, 4, 512)
            .transpose(1, 0, 2)),
        "sdd": np.ascontiguousarray(
            np.asarray(L0_ffn_shared_down, f).reshape(D, 8, 128)
            .transpose(1, 0, 2)),
        "sgwp": np.ascontiguousarray(
            np.asarray(L0_ffn_shared_gate_w, f).reshape(8, 128).T),
        "idfd": np.eye(128, dtype=f),
        "idbd": np.eye(128, dtype=f).astype(BF),
    }
    j = np.arange(32, dtype=np.float64)
    invf = 10000.0 ** (-j / 32.0)
    t = np.arange(T, dtype=np.float64)
    ang = t[None, :] * invf[:, None]
    d["cosd"] = np.ascontiguousarray(np.tile(np.cos(ang), (4, 1))).astype(BF)
    d["sind"] = np.ascontiguousarray(np.tile(np.sin(ang), (4, 1))).astype(BF)
    pp, jj = np.mgrid[0:128, 0:896]
    d["maskd"] = (jj >= pp + 384).astype(f).astype(BF)
    return d


def kernel(L0_attn_x, L0_attn_Wq, L0_attn_Wk, L0_attn_Wv, L0_attn_Wo,
           L0_ffn_router_w, L0_ffn_w1, L0_ffn_w2, L0_ffn_shared_gate_up,
           L0_ffn_shared_down, L0_ffn_shared_gate_w, _dumps=()):
    shared = _host_prep(L0_attn_Wq, L0_attn_Wk, L0_attn_Wv, L0_attn_Wo,
                        L0_ffn_router_w, L0_ffn_w1, L0_ffn_w2,
                        L0_ffn_shared_gate_up, L0_ffn_shared_down,
                        L0_ffn_shared_gate_w)
    x = np.asarray(L0_attn_x, np.float32)
    key = tuple(sorted(_dumps))
    if key not in _cache:
        _cache[key] = build_program(dumps=_dumps)
    nc = _cache[key]
    in_maps = []
    for cidx in range(8):
        m = dict(shared)
        m["xT"] = np.ascontiguousarray(x[cidx].T)
        in_maps.append(m)
    res = run_bass_kernel_spmd(nc, in_maps, core_ids=list(range(8)))
    outs = np.stack([np.asarray(res.results[i]["out"]) for i in range(8)], axis=0)
    if _dumps:
        return outs, res.results
    return outs


# version 24
# speedup vs baseline: 1.1219x; 1.0144x over previous
import numpy as np
import ml_dtypes
import concourse.bass as bass
import concourse.mybir as mybir
from concourse.bass_utils import run_bass_kernel_spmd
from contextlib import ExitStack

F32 = mybir.dt.float32
BF16 = mybir.dt.bfloat16
AF = mybir.ActivationFunctionType
ALU = mybir.AluOpType
AX = mybir.AxisListType

D = 1024
T = 2048
H = 16
NE = 8
NC = 4
NTT = 16
BF = ml_dtypes.bfloat16


class Sem:
    def __init__(self, ctx, nc, name):
        self.h = ctx.enter_context(nc.semaphore(name))
        self.n = 0


class Prog:

    def __init__(self, nc):
        self.nc = nc
        self.ops = {k: [] for k in ("tensor", "vector", "scalar", "gpsimd", "sync")}

    def add(self, eng, fn):
        self.ops[eng].append(fn)

    def wait(self, eng, sem, val):
        if val > 0:
            self.add(eng, lambda e, s=sem.h, v=val: e.wait_ge(s, v))

    def run(self):
        def mk(lst):
            def body(eng):
                for f in lst:
                    f(eng)
            return body
        with self.nc.Block() as blk:
            for nm, deco in (("tensor", blk.tensor), ("vector", blk.vector),
                             ("scalar", blk.scalar), ("gpsimd", blk.gpsimd),
                             ("sync", blk.sync)):
                if self.ops[nm]:
                    deco(mk(self.ops[nm]))
        self.ops = {k: [] for k in self.ops}

    def mm(self, pe, out, l, r, start, stop, tp=None, inc=False):
        if inc:
            pe.n += 1

        def f(e, out=out, l=l, r=r, start=start, stop=stop, tp=tp, inc=inc, s=pe.h):
            ins = e.matmul(out, l, r, start=start, stop=stop, tile_position=tp)
            if inc:
                ins.then_inc(s, 1)
        self.add("tensor", f)

    def trans(self, pe, out, in_, ident, inc=False):
        if inc:
            pe.n += 1

        def f(e, out=out, in_=in_, ident=ident, inc=inc, s=pe.h):
            ins = e.transpose(out, in_, ident)
            if inc:
                ins.then_inc(s, 1)
        self.add("tensor", f)

    def act(self, sm, out, in_, func, bias=0.0, scale=1.0, accum=None, inc=False):
        if inc:
            sm.n += 1

        def f(e, out=out, in_=in_, func=func, bias=bias, scale=scale, accum=accum,
              inc=inc, s=sm.h):
            ins = e.activation(out, in_, func, bias=bias, scale=scale,
                               accum_out=accum)
            if inc:
                ins.then_inc(s, 1)
        self.add("scalar", f)

    def scopy(self, sm, out, in_, inc=False):
        if inc:
            sm.n += 1

        def f(e, out=out, in_=in_, inc=inc, s=sm.h):
            ins = e.copy(out, in_)
            if inc:
                ins.then_inc(s, 1)
        self.add("scalar", f)

    def vop(self, dv, name, *args, inc=False, **kw):
        if inc:
            dv.n += 1

        def f(e, name=name, args=args, kw=kw, inc=inc, s=dv.h):
            ins = getattr(e, name)(*args, **kw)
            if inc:
                ins.then_inc(s, 1)
        self.add("vector", f)

    def dma(self, eng, sem, out, in_):
        if sem is None:
            s = self.dpool[self.dpi % len(self.dpool)]
            self.dpi += 1
            s.n += 16

            def f(e, out=out, in_=in_, sh=s.h):
                e.dma_start(out, in_).then_inc(sh, 16)
            self.add(eng, f)
            return (s, s.n)
        sem.n += 16

        def f(e, out=out, in_=in_, s=sem.h):
            e.dma_start(out, in_).then_inc(s, 16)
        self.add(eng, f)

    def waitt(self, eng, tkt):
        if tkt is None:
            return
        if isinstance(tkt, list):
            for t in tkt:
                self.waitt(eng, t)
            return
        self.wait(eng, tkt[0], tkt[1])


def fm_src(dram, rows, cols, off_elems=0, ccnt=None):
    return bass.AP(dram, off_elems,
                   [[cols, 128], [128 * cols, rows // 128], [1, ccnt or cols]])


def build_program(dumps=()):
    nc = bass.Bass("TRN2", target_bir_lowering=False)

    def din(name, shape, dt=F32):
        return nc.dram_tensor(name, list(shape), dt, kind="ExternalInput")

    xT = din("xT", (D, T))
    wqet = din("wqet", (D, 512))
    wqot = din("wqot", (D, 512))
    wket = din("wket", (D, 512))
    wkot = din("wkot", (D, 512))
    wvt = din("wvt", (D, D))
    wotq = din("wotq", (4, D, 256))
    rwp = din("rwp", (128, 64))
    w1h = din("w1h", (NE, 2, D, 512))
    w2d = din("w2d", (8, NE * D, 128))
    sguq = din("sguq", (4, D, 512))
    sdd = din("sdd", (8, D, 128))
    sgwp = din("sgwp", (128, 8))
    cosd = din("cosd", (128, T), BF16)
    sind = din("sind", (128, T), BF16)
    maskd = din("maskd", (128, 896), BF16)
    idfd = din("idfd", (128, 128))
    idbd = din("idbd", (128, 128), BF16)

    wotc = nc.dram_tensor("wotc", [4, 128, 8 * 256], BF16)
    w1c = nc.dram_tensor("w1c", [NE * 2, 128, 8 * 512], BF16)
    w2c = nc.dram_tensor("w2c", [16, 128, 32 * 128], BF16)
    sguc = nc.dram_tensor("sguc", [4, 128, 8 * 512], BF16)
    sddc = nc.dram_tensor("sddc", [8, 128, 8 * 128], BF16)
    out = nc.dram_tensor("out", [T, D], F32, kind="ExternalOutput")
    ddump = {}
    for nm, shp, ddt in (("d_rq", [128, 8 * T], BF16), ("d_rk", [128, 8 * T], BF16),
                         ("d_vaug", [128, 16 * 1040], BF16),
                         ("d_aout", [128, 8 * 512], BF16),
                         ("d_y1", [128, 8 * T], F32),
                         ("d_hg", [128, 32 * 512], BF16),
                         ("d_comb", [8, 512], BF16),
                         ("d_ctm", [128, 8], F32),
                         ("d_probs", [128, 8], F32),
                         ("d_mt1", [128, 8], F32),
                         ("d_spre", [128, 8 * 512], BF16),
                         ("d_accs", [128, 8 * 512], F32),
                         ("d_grow", [1, 512], F32),
                         ("d_z2", [128, 8 * T], F32)):
        if nm in dumps:
            ddump[nm] = nc.dram_tensor(nm, shp, ddt, kind="ExternalOutput")

    ctx = ExitStack()
    with ctx:
        ld = Sem(ctx, nc, "ld")
        ld2 = Sem(ctx, nc, "ld2")
        ld3 = Sem(ctx, nc, "ld3")
        wcs = Sem(ctx, nc, "wcs")
        pe = Sem(ctx, nc, "pe")
        acs = Sem(ctx, nc, "acs")
        dve = Sem(ctx, nc, "dve")
        st = Sem(ctx, nc, "st")

        P01 = ctx.enter_context(nc.psum_tensor("P01", [128, 1024], F32))
        P23 = ctx.enter_context(nc.psum_tensor("P23", [128, 1024], F32))
        P4_7 = [ctx.enter_context(nc.psum_tensor(f"P{i}", [128, 512], F32))
                for i in range(4, 8)]
        P = [P01[:, 0:512], P01[:, 512:1024], P23[:, 0:512], P23[:, 512:1024],
             P4_7[0][:, :], P4_7[1][:, :], P4_7[2][:, :], P4_7[3][:, :]]
        PG = [P01, P23]

        onesf = ctx.enter_context(nc.sbuf_tensor("onesf", [128, 128], F32))
        onesb = ctx.enter_context(nc.sbuf_tensor("onesb", [128, 128], BF16))
        idf = ctx.enter_context(nc.sbuf_tensor("idf", [128, 128], F32))
        idb = ctx.enter_context(nc.sbuf_tensor("idb", [128, 128], BF16))
        epsb = ctx.enter_context(nc.sbuf_tensor("epsb", [128, 1], F32))
        neg30 = ctx.enter_context(nc.sbuf_tensor("neg30", [128, 1], F32))

        p = Prog(nc)
        p.dpool = [Sem(ctx, nc, f"dp{i}") for i in range(32)]
        p.dpi = 0

        with ExitStack() as abctx:
            rq = abctx.enter_context(nc.sbuf_tensor("rq", [128, 8 * T], BF16))
            rk = abctx.enter_context(nc.sbuf_tensor("rk", [128, 8 * T], BF16))
            vaug = abctx.enter_context(nc.sbuf_tensor("vaug", [128, 16 * 1040], BF16))

            with ExitStack() as a2:
                xTbf = a2.enter_context(nc.sbuf_tensor("xTbf", [128, 8 * T], BF16))
                cosb = a2.enter_context(nc.sbuf_tensor("cosb", [128, T], BF16))
                sinb = a2.enter_context(nc.sbuf_tensor("sinb", [128, T], BF16))
                wqk = [a2.enter_context(nc.sbuf_tensor(f"wqk{i}", [128, 8 * 512], BF16))
                       for i in range(2)]
                wvb = a2.enter_context(nc.sbuf_tensor("wvb", [128, 8 * 1024], BF16))
                tmp1 = a2.enter_context(nc.sbuf_tensor("tmpa", [128, 512], BF16))
                tmp2 = a2.enter_context(nc.sbuf_tensor("tmpb", [128, 512], BF16))
                ebuf = a2.enter_context(nc.sbuf_tensor("ebuf", [128, 2 * 512], BF16))
                obuf = a2.enter_context(nc.sbuf_tensor("obuf", [128, 2 * 512], BF16))

                wqk2 = [a2.enter_context(
                    nc.sbuf_tensor(f"wqk2{i}", [128, 8 * 512], BF16))
                    for i in range(2)]
                ld_qeo = [p.dma("gpsimd", None, wqk[0][:, :], fm_src(wqet, D, 512)),
                          p.dma("gpsimd", None, wqk[1][:, :], fm_src(wqot, D, 512))]
                ld_xc = [p.dma("gpsimd", None,
                               bass.AP(xTbf.ap().tensor, cc * 512,
                                       [[8 * T, 128], [T, 8], [1, 512]]),
                               bass.AP(xT, cc * 512,
                                       [[T, 128], [128 * T, 8], [1, 512]]))
                         for cc in range(NC)]
                ld_k = [p.dma("gpsimd", None, wqk2[0][:, :], fm_src(wket, D, 512)),
                        p.dma("gpsimd", None, wqk2[1][:, :], fm_src(wkot, D, 512))]
                ld_v = p.dma("gpsimd", None, wvb[:, :], fm_src(wvt, D, D))
                ld_cs = [p.dma("sync", None, cosb[:, :], fm_src(cosd, 128, T)),
                         p.dma("sync", None, sinb[:, :], fm_src(sind, 128, T))]
                p.add("gpsimd", lambda g: g.memset(onesf[:, :], 1.0))
                p.add("gpsimd", lambda g: g.memset(onesb[:, :], 1.0))
                p.add("gpsimd", lambda g: g.memset(epsb[:, :], 1e-5))
                p.add("gpsimd", lambda g: g.memset(neg30[:, :], -1e30))
                for tt in range(NTT):
                    p.vop(dve, "memset",
                          vaug[:, tt * 1040 + 64:(tt + 1) * 1040:65], 1.0)

                it_dve_done = []
                it_evac_act = []
                itg = 0
                for wi, rT in enumerate((rq, rk)):
                    ld_pair = ld_qeo if wi == 0 else ld_k
                    wq = wqk if wi == 0 else wqk2
                    for m in range(4):
                        for c in range(NC):
                            G = PG[itg % 2]
                            if itg >= 2:
                                p.wait("tensor", acs, it_evac_act[itg - 2])
                            if m == 0 and c == 0:
                                p.waitt("tensor", ld_pair)
                            if wi == 0 and m == 0:
                                p.waitt("tensor", ld_xc[c])
                            if itg == 0:
                                p.waitt("vector", ld_cs)
                            for half in range(2):
                                for i in range(8):
                                    p.mm(pe, G[:, half * 512:half * 512 + 512],
                                         wq[half][:, i * 512 + m * 128:
                                                  i * 512 + m * 128 + 128],
                                         xTbf[:, i * T + c * 512:
                                              i * T + c * 512 + 512],
                                         start=(i == 0), stop=(i == 7),
                                         inc=(i == 7 and half == 1))
                            mm_done = pe.n
                            eb = ebuf[:, (itg % 2) * 512:(itg % 2) * 512 + 512]
                            ob = obuf[:, (itg % 2) * 512:(itg % 2) * 512 + 512]
                            p.wait("scalar", pe, mm_done)
                            if itg >= 2:
                                p.wait("scalar", dve, it_dve_done[itg - 2])
                            p.scopy(acs, eb, G[:, 0:512])
                            p.scopy(acs, ob, G[:, 512:1024], inc=True)
                            it_evac_act.append(acs.n)
                            cs = (slice(None),
                                  slice(c * 512, c * 512 + 512))
                            c0 = slice((2 * m) * T + c * 512,
                                       (2 * m) * T + c * 512 + 512)
                            c1 = slice((2 * m + 1) * T + c * 512,
                                       (2 * m + 1) * T + c * 512 + 512)
                            p.wait("vector", acs, acs.n)
                            p.vop(dve, "tensor_mul", tmp1[:, :], eb, cosb[cs])
                            p.vop(dve, "tensor_mul", tmp2[:, :], ob, sinb[cs])
                            p.vop(dve, "tensor_sub", rT[0:64, c0],
                                  tmp1[0:64, :], tmp2[0:64, :])
                            p.vop(dve, "tensor_sub", rT[0:64, c1],
                                  tmp1[64:128, :], tmp2[64:128, :])
                            p.vop(dve, "tensor_mul", tmp1[:, :], eb, sinb[cs])
                            p.vop(dve, "tensor_mul", tmp2[:, :], ob, cosb[cs])
                            p.vop(dve, "tensor_add", rT[64:128, c0],
                                  tmp1[0:64, :], tmp2[0:64, :])
                            p.vop(dve, "tensor_add", rT[64:128, c1],
                                  tmp1[64:128, :], tmp2[64:128, :],
                                  inc=True)
                            it_dve_done.append(dve.n)
                            itg += 1

                v_act = []
                vi = 0
                for tt in range(NTT):
                    for oc in range(2):
                        bk = 4 + (vi % 2)
                        if vi >= 2:
                            p.wait("tensor", acs, v_act[vi - 2])
                        if vi == 0:
                            p.waitt("tensor", ld_v)
                        for i in range(8):
                            p.mm(pe, P[bk][:, :],
                                 xTbf[:, i * T + tt * 128: i * T + tt * 128 + 128],
                                 wvb[:, i * 1024 + oc * 512:
                                     i * 1024 + oc * 512 + 512],
                                 start=(i == 0), stop=(i == 7), inc=(i == 7))
                        p.wait("scalar", pe, pe.n)
                        dst = bass.AP(vaug.ap().tensor, tt * 1040 + oc * 520,
                                      [[16 * 1040, 128], [65, 8], [1, 64]])
                        src = P[bk][:, 0:512].rearrange("p (h d) -> p h d", h=8)
                        p.scopy(acs, dst, src, inc=True)
                        v_act.append(acs.n)
                        vi += 1
                if "d_rq" in ddump or "d_rk" in ddump:
                    p.wait("sync", dve, dve.n)
                    for nm, tsr in (("d_rq", rq), ("d_rk", rk)):
                        if nm in ddump:
                            p.dma("sync", st, ddump[nm][:, :], tsr[:, :])
                if "d_vaug" in ddump:
                    p.wait("sync", acs, acs.n)
                    p.dma("sync", st, ddump["d_vaug"][:, :], vaug[:, :])
                p.run()
            a_pe, a_act, a_dve = pe.n, acs.n, dve.n

            y1f = ctx.enter_context(
                nc.sbuf_tensor("y1f", [128, 8 * T], F32, side="right"))

            with ExitStack() as bctx:
                wob = bctx.enter_context(nc.sbuf_tensor("wob", [128, 8 * 256], BF16))
                aout = bctx.enter_context(nc.sbuf_tensor("aout", [128, 8 * 512], BF16))
                exps = bctx.enter_context(nc.sbuf_tensor("exps", [128, 4 * 512], BF16))
                rrow = bctx.enter_context(nc.sbuf_tensor("rrow", [128, 512], F32))
                nbs0 = bctx.enter_context(nc.sbuf_tensor("nbs0", [128, 512], F32))
                maskb = bctx.enter_context(nc.sbuf_tensor("maskb", [128, 896], BF16))
                xf32 = bctx.enter_context(nc.sbuf_tensor("xf32", [128, 2 * 512], F32))
                z1bf = bctx.enter_context(nc.sbuf_tensor("z1bf", [128, 2 * 512], BF16))
                zsqb = bctx.enter_context(nc.sbuf_tensor("zsqb", [128, 2 * 512], BF16))
                stat = [bctx.enter_context(nc.sbuf_tensor(f"stat{i}", [128, 512], F32))
                        for i in range(3)]
                stat.append(stat[1])
                wstage = [bctx.enter_context(
                    nc.sbuf_tensor(f"wst{i}", [128, 1024], BF16)) for i in range(2)]
                rqz = bctx.enter_context(
                    nc.sbuf_tensor("rqz", [128, 2 * 1024], BF16))

                ld_msk = [p.dma("sync", None, maskb[:, :], maskd[:, :]),
                          p.dma("sync", None, idf[:, :], idfd[:, :]),
                          p.dma("sync", None, idb[:, :], idbd[:, :])]
                wo_ld = {}
                wot_st = {}
                cast_acts = {}
                xs_dve = [a_dve] * 2
                xs_ld = {}

                def x_slice_src(c, odt):
                    return bass.AP(xT, odt * 128 * T + c * 512,
                                   [[T, 128], [1, 512]])

                pieces = []
                for oh in range(4):
                    for qi in range(2):
                        pieces.append((bass.AP(wotq,
                                               oh * D * 256 + qi * 4 * 128 * 256,
                                               [[256, 128], [128 * 256, 4],
                                                [1, 256]]),
                                       wotc[oh, :, qi * 1024:(qi + 1) * 1024],
                                       "wot", oh))
                for key in range(16):
                    for qi in range(4):
                        pieces.append((bass.AP(w1h, key * D * 512 + qi * 2 * 128 * 512,
                                               [[512, 128], [128 * 512, 2], [1, 512]]),
                                       w1c[key, :, qi * 1024:(qi + 1) * 1024],
                                       "w1", key))
                for key in range(16):
                    eg_, dt_ = key // 8, key % 8
                    off0 = dt_ * (NE * D) * 128 + eg_ * 4096 * 128
                    for qi in range(4):
                        pieces.append((bass.AP(w2d, off0 + qi * 8 * 128 * 128,
                                               [[128, 128], [128 * 128, 8], [1, 128]]),
                                       w2c[key, :, qi * 1024:(qi + 1) * 1024],
                                       "w2", key))
                for key in range(4):
                    for qi in range(4):
                        pieces.append((bass.AP(sguq, key * D * 512 + qi * 2 * 128 * 512,
                                               [[512, 128], [128 * 512, 2], [1, 512]]),
                                       sguc[key, :, qi * 1024:(qi + 1) * 1024],
                                       "sgu", key))
                for key in range(8):
                    pieces.append((bass.AP(sdd, key * D * 128,
                                           [[128, 128], [128 * 128, 8], [1, 128]]),
                                   sddc[key, :, :], "sdd", key))
                pf_st = {}
                pf_slot = [None, None]
                pf_i = [0]

                def emit_pieces(n):
                    while n > 0 and pf_i[0] < len(pieces):
                        srcap, dstap, strm, key = pieces[pf_i[0]]
                        sl = pf_i[0] % 2
                        p.waitt("gpsimd", pf_slot[sl])
                        t_ld = p.dma("gpsimd", None,
                                     wstage[sl][:, 0:dstap.shape[-1]], srcap)
                        p.waitt("sync", t_ld)
                        t_st = p.dma("sync", None, dstap,
                                     wstage[sl][:, 0:dstap.shape[-1]])
                        pf_slot[sl] = t_st
                        pf_st.setdefault((strm, key), []).append(t_st)
                        pf_i[0] += 1
                        n -= 1
                exp_act = {}
                exp_rd = {}
                msk_dve = {}
                wo_free = [a_dve, a_dve]
                av_free = {0: a_dve, 1: a_dve, 2: a_dve}
                nbs_rd = [a_dve, a_dve]
                bcast_pe = [a_pe, a_pe]
                p7_dve = a_dve
                ln_dve = a_dve
                xz_dve = [a_dve] * NC
                wob_rd_pe = 0
                stats_pe = a_pe
                stats_pe_slot = [a_pe, a_pe]
                stats_q = []
                stats_pe_slot = [a_pe, a_pe]
                stats_q = []
                ip = 0
                gg = 0
                pending_norm = []
                pending_norm2 = []
                mr_dve_prev = a_dve
                gpend = [None]
                flush_av_last = None
                p.vop(dve, "memset", rqz[:, :], 0.0, inc=True)
                rqz_dve = {}
                rqz_free_pe = [a_pe, a_pe]

                def build_rqz(j):
                    if j >= 32:
                        return
                    cj, prj = j // 8, j % 8
                    sl = j % 2
                    p.wait("vector", pe, rqz_free_pe[sl])
                    base = prj * T + cj * 512
                    for hi, (r0a, r0b) in enumerate(((0, 64), (32, 96))):
                        o = sl * 1024 + hi * 512
                        p.vop(dve, "tensor_copy", rqz[r0a:r0a + 32, o:o + 512],
                              rq[r0a:r0a + 32, base:base + 512])
                        p.vop(dve, "tensor_copy", rqz[r0b:r0b + 32, o:o + 512],
                              rq[r0b:r0b + 32, base:base + 512],
                              inc=(hi == 1))
                    rqz_dve[j] = dve.n
                for c in range(NC):
                    nmt = 4 * c + 4
                    for odt in range(2):
                        p.wait("scalar", dve, xs_dve[odt])
                        xs_ld[(c, odt)] = p.dma(
                            "scalar", None, xf32[:, odt * 512:(odt + 1) * 512],
                            x_slice_src(c, odt))
                    if c > 0:
                        p.wait("tensor", dve, mr_dve_prev)

                    def emit_norm_s1(ent):
                        h0n, bksn, dd = ent
                        recs = [0, 0]
                        for hi in range(2):
                            avb = P4_7[bksn[hi]]
                            rr = rrow[64 + 32 * hi:65 + 32 * hi, :]
                            p.wait("scalar", pe, dd["av_done"][hi])
                            p.wait("scalar", pe, bcast_pe[hi])
                            p.act(acs, rr, avb[64:65, :], AF.Ln, inc=True)
                            p.wait("scalar", acs, acs.n)
                            p.act(acs, rr, rr, AF.Exp, scale=-1.0, inc=True)
                            recs[hi] = acs.n
                        pending_norm2.append((h0n, bksn, recs))

                    def emit_norm_s2(ent2):
                        nonlocal p7_dve
                        h0n, bksn, recs = ent2
                        un = h0n // 2
                        for hi in range(2):
                            rr = rrow[64 + 32 * hi:65 + 32 * hi, :]
                            p.wait("tensor", acs, recs[hi])
                            p.wait("tensor", dve, p7_dve)
                            p.mm(pe, P[7][64 * hi:64 * hi + 64, :],
                                 onesf[64 + 32 * hi:65 + 32 * hi, 0:64],
                                 rr, start=True, stop=True,
                                 tp=(64 + 32 * hi, 64 * hi), inc=True)
                            bcast_pe[hi] = pe.n
                        for hi in range(2):
                            avb = P4_7[bksn[hi]]
                            p.wait("vector", pe, bcast_pe[hi])
                            p.vop(dve, "tensor_copy",
                                  nbs0[hi * 64:hi * 64 + 64, :],
                                  P[7][64 * hi:64 * hi + 64, :], inc=True)
                            p.vop(dve, "tensor_mul",
                                  aout[hi * 64:hi * 64 + 64,
                                       un * 512:un * 512 + 512],
                                  avb[0:64, :],
                                  nbs0[hi * 64:hi * 64 + 64, :], inc=True)
                            av_free[bksn[hi]] = dve.n
                            nbs_rd[hi] = dve.n
                        p7_dve = dve.n
                    for pp in range(4):
                        for sub in range(2):
                            h0 = 4 * pp + 2 * sub
                            av_done = [0, 0]
                            bks = [(2 * ip) % 3, (2 * ip + 1) % 3]
                            def flush_av(pend):
                                nonlocal flush_av_last
                                pgg, pmt, pdiag, ph0, pbks, avd, pnmt = pend
                                for hi in range(2):
                                    h = ph0 + hi
                                    if pdiag:
                                        p.wait("tensor", dve, msk_dve[pgg])
                                    else:
                                        p.wait("tensor", acs, exp_act[pgg])
                                    avb = P4_7[pbks[hi]]
                                    if pmt == 0:
                                        p.wait("tensor", dve, av_free[pbks[hi]])
                                    slh = exps[:, (pgg % 2) * 1024 + hi * 512:
                                               (pgg % 2) * 1024 + hi * 512 + 512]
                                    p.mm(pe, avb[0:65, :],
                                         vaug[:, pmt * 1040 + 65 * h:
                                              pmt * 1040 + 65 * h + 65],
                                         slh, start=(pmt == 0),
                                         stop=(pmt == pnmt - 1),
                                         inc=(pmt == pnmt - 1))
                                    if pmt == pnmt - 1:
                                        avd[hi] = pe.n
                                exp_rd[pgg] = pe.n
                            flush_av_last = flush_av
                            if ip == 0:
                                build_rqz(0)
                            build_rqz(ip + 1)
                            pr = h0 // 2
                            for mt in range(nmt):
                                G = PG[gg % 2]
                                if gg >= 2:
                                    p.wait("tensor", acs, exp_act[gg - 2])
                                if mt == 0:
                                    p.wait("tensor", dve, rqz_dve[ip])
                                lo = pr * T + mt * 128
                                zsl = (ip % 2) * 1024
                                for hi in range(2):
                                    half = G[:, hi * 512:(hi + 1) * 512]
                                    p.mm(pe, half, rk[:, lo:lo + 128],
                                         rqz[:, zsl + hi * 512:
                                             zsl + hi * 512 + 512],
                                         start=True, stop=True,
                                         inc=(hi == 1))
                                sc_pe = pe.n
                                p.wait("scalar", pe, sc_pe)
                                if gg >= 2:
                                    p.wait("scalar", pe, exp_rd[gg - 2])
                                sl2 = exps[:, (gg % 2) * 1024:(gg % 2) * 1024 + 1024]
                                p.act(acs, sl2, G[:, :], AF.Exp, scale=0.125,
                                      inc=True)
                                exp_act[gg] = acs.n
                                diag = mt >= 4 * c
                                if diag:
                                    kd = mt - 4 * c
                                    p.wait("vector", acs, exp_act[gg])
                                    if gg == 0:
                                        p.waitt("vector", ld_msk)
                                    for hi in range(2):
                                        slh = exps[:, (gg % 2) * 1024 + hi * 512:
                                                   (gg % 2) * 1024 + hi * 512 + 512]
                                        p.vop(dve, "tensor_mul", slh, slh,
                                              maskb[:, 384 - 128 * kd:
                                                    384 - 128 * kd + 512],
                                              inc=(hi == 1))
                                    msk_dve[gg] = dve.n
                                if mt == 1:
                                    while pending_norm2:
                                        emit_norm_s2(pending_norm2.pop(0))
                                if gpend[0] is not None:
                                    flush_av(gpend[0])
                                    gpend[0] = None
                                if mt == 0:
                                    while pending_norm:
                                        emit_norm_s1(pending_norm.pop(0))
                                gpend[0] = (gg, mt, diag, h0, bks, av_done, nmt)
                                gg += 1
                            pending_norm.append((h0, bks, dict(av_done=av_done)))
                            emit_pieces(8 if ip == 0 else (3 if c == 0 else 6))
                            rqz_free_pe[ip % 2] = sc_pe
                            ip += 1
                    if gpend[0] is not None:
                        flush_av_last(gpend[0])
                        gpend[0] = None
                    while pending_norm:
                        emit_norm_s1(pending_norm.pop(0))
                    while pending_norm2:
                        emit_norm_s2(pending_norm2.pop(0))
                    aout_dve, aout_act = dve.n, acs.n

                    for oh in range(4):
                        p.wait("scalar", pe, wob_rd_pe)
                        if c == 0:
                            p.waitt("scalar", pf_st[("wot", oh)])
                        wo_ld[(c, oh)] = p.dma("scalar", None, wob[:, :],
                                               wotc[oh, :, :])
                        for odt in range(2 * oh, 2 * oh + 2):
                            bk = odt % 2
                            if odt == 0:
                                p.wait("tensor", dve, aout_dve)
                                p.wait("tensor", acs, aout_act)
                            if odt % 2 == 0:
                                p.waitt("tensor", wo_ld[(c, oh)])
                            if odt >= 2:
                                p.wait("tensor", dve, wo_free[bk])
                            for i in range(8):
                                p.mm(pe, P[bk][:, :],
                                     wob[:, i * 256 + (odt % 2) * 128:
                                         i * 256 + (odt % 2) * 128 + 128],
                                     aout[:, i * 512: i * 512 + 512],
                                     start=(i == 0), stop=(i == 7), inc=(i == 7))
                            wo_pe = pe.n
                            ysl = y1f[:, odt * T + c * 512: odt * T + c * 512 + 512]
                            p.wait("vector", pe, wo_pe)
                            p.waitt("vector", xs_ld[(c, odt)])
                            sx = odt % 2
                            p.vop(dve, "tensor_add", ysl, P[bk][:, :],
                                  xf32[:, sx * 512: sx * 512 + 512], inc=True)
                            wo_free[bk] = dve.n
                            xs_dve[sx] = dve.n
                            if odt < 6:
                                p.wait("scalar", dve, xs_dve[sx])
                                xs_ld[(c, odt + 2)] = p.dma(
                                    "scalar", None,
                                    xf32[:, sx * 512:(sx + 1) * 512],
                                    x_slice_src(c, odt + 2))
                            zb = z1bf[:, (odt % 2) * 512:(odt % 2) * 512 + 512]
                            qb = zsqb[:, (odt % 2) * 512:(odt % 2) * 512 + 512]
                            p.wait("scalar", dve, wo_free[bk])
                            p.wait("scalar", pe, stats_pe_slot[odt % 2])
                            p.scopy(acs, zb, ysl)
                            p.act(acs, qb, zb, AF.Square, inc=True)
                            cast_acts[odt] = acs.n
                            if stats_q:
                                (podt, pzb, pqb, pca) = stats_q.pop(0)
                                p.wait("tensor", acs, pca)
                                p.mm(pe, P[2][0:1, :], onesb[:, 0:1], pzb,
                                     start=(podt == 0), stop=(podt == 7))
                                p.mm(pe, P[3][0:1, :], onesb[:, 0:1], pqb,
                                     start=(podt == 0), stop=(podt == 7),
                                     inc=True)
                                stats_pe_slot[podt % 2] = pe.n
                            stats_q.append((odt, zb, qb, acs.n))
                        wob_rd_pe = pe.n
                    while stats_q:
                        (podt, pzb, pqb, pca) = stats_q.pop(0)
                        p.wait("tensor", acs, pca)
                        p.mm(pe, P[2][0:1, :], onesb[:, 0:1], pzb,
                             start=(podt == 0), stop=(podt == 7))
                        p.mm(pe, P[3][0:1, :], onesb[:, 0:1], pqb,
                             start=(podt == 0), stop=(podt == 7), inc=True)
                        stats_pe_slot[podt % 2] = pe.n
                    stats_pe = pe.n
                    while stats_q:
                        (podt, pzb, pqb, pca) = stats_q.pop(0)
                        p.wait("tensor", acs, pca)
                        p.mm(pe, P[2][0:1, :], onesb[:, 0:1], pzb,
                             start=(podt == 0), stop=(podt == 7))
                        p.mm(pe, P[3][0:1, :], onesb[:, 0:1], pqb,
                             start=(podt == 0), stop=(podt == 7), inc=True)
                        stats_pe_slot[podt % 2] = pe.n
                    stats_pe = pe.n
                    p.wait("vector", pe, stats_pe)
                    p.vop(dve, "tensor_scalar", stat[0][0:1, :], P[2][0:1, :],
                          1.0 / D, None, ALU.mult)
                    p.vop(dve, "tensor_scalar", stat[1][0:1, :], P[3][0:1, :],
                          1.0 / D, None, ALU.mult, inc=True)
                    p.wait("vector", dve, dve.n)
                    p.vop(dve, "tensor_mul", stat[2][0:1, :], stat[0][0:1, :],
                          stat[0][0:1, :], inc=True)
                    p.wait("vector", dve, dve.n)
                    p.vop(dve, "tensor_sub", stat[1][0:1, :], stat[1][0:1, :],
                          stat[2][0:1, :], inc=True)
                    var_dve = dve.n
                    p.wait("scalar", dve, var_dve)
                    p.act(acs, stat[2][0:1, :], stat[1][0:1, :], AF.Ln,
                          bias=epsb[0:1, :], scale=1.0, inc=True)
                    p.wait("scalar", acs, acs.n)
                    p.act(acs, stat[2][0:1, :], stat[2][0:1, :], AF.Exp,
                          scale=-0.5, inc=True)
                    rstd_act = acs.n
                    p.wait("vector", acs, rstd_act)
                    p.vop(dve, "tensor_mul", stat[3][0:1, :], stat[0][0:1, :],
                          stat[2][0:1, :], inc=True)
                    p.wait("vector", dve, dve.n)
                    mr_dve = dve.n
                    p.wait("tensor", dve, mr_dve)
                    p.wait("tensor", dve, av_free[0])
                    p.wait("tensor", dve, av_free[1])
                    p.mm(pe, P[4][:, :], onesf[0:1, 0:128], stat[2][0:1, :],
                         start=True, stop=True)
                    p.mm(pe, P[5][:, :], onesf[0:1, 0:128], stat[3][0:1, :],
                         start=True, stop=True, inc=True)
                    bc2_pe = pe.n
                    for odt in range(8):
                        ysl = y1f[:, odt * T + c * 512: odt * T + c * 512 + 512]
                        p.wait("vector", pe, bc2_pe)
                        p.vop(dve, "tensor_mul", ysl, ysl, P[4][:, :])
                        p.vop(dve, "tensor_sub", ysl, ysl, P[5][:, :],
                              inc=(odt == 7))
                    ln_dve = dve.n
                    av_free[0] = ln_dve
                    av_free[1] = ln_dve
                    mr_dve_prev = mr_dve
                    xz_dve[c] = dve.n
                emit_pieces(len(pieces))
                pf_state = (pf_st,)
                if "d_aout" in ddump:
                    p.wait("sync", dve, dve.n)
                    p.dma("sync", st, ddump["d_aout"][:, :], aout[:, :])
                if "d_y1" in ddump:
                    p.wait("sync", dve, dve.n)
                    p.dma("sync", st, ddump["d_y1"][:, :], y1f[:, :])
                p.run()
        b_dve, b_act, b_pe = dve.n, acs.n, pe.n

        with ExitStack() as cctx:
            y1bf = cctx.enter_context(nc.sbuf_tensor("y1bf", [128, 8 * 512], BF16))
            hgrp = cctx.enter_context(nc.sbuf_tensor("hgrp", [128, 32 * 512], BF16))
            accs = cctx.enter_context(nc.sbuf_tensor("accs", [128, 8 * 512], F32))
            w1b = [cctx.enter_context(nc.sbuf_tensor(f"w1b{i}", [128, 8 * 512], BF16))
                   for i in range(2)]
            w2b = [cctx.enter_context(nc.sbuf_tensor(f"w2b{i}", [128, 32 * 128], BF16))
                   for i in range(2)]
            sgub = [cctx.enter_context(nc.sbuf_tensor(f"sgub{i}", [128, 8 * 512], BF16))
                    for i in range(2)]
            sddb = [cctx.enter_context(nc.sbuf_tensor(f"sddb{i}", [128, 8 * 128], BF16))
                    for i in range(2)]
            sgb = cctx.enter_context(nc.sbuf_tensor("sgb", [128, 8 * 512], BF16))
            spre = cctx.enter_context(nc.sbuf_tensor("spre", [128, 8 * 512], BF16))
            hsil = cctx.enter_context(nc.sbuf_tensor("hsil", [128, 2 * 512], BF16))
            rwb = cctx.enter_context(nc.sbuf_tensor("rwb", [128, 64], BF16))
            sgwb = cctx.enter_context(nc.sbuf_tensor("sgwb", [128, 8], BF16))
            comb = [cctx.enter_context(nc.sbuf_tensor(f"comb{i}", [128, 512], BF16))
                    for i in range(8)]
            cbcS = cctx.enter_context(nc.sbuf_tensor("cbcS", [128, 2 * 512], BF16))
            probs = cctx.enter_context(nc.sbuf_tensor("probs", [128, 8], F32))
            ctm = cctx.enter_context(nc.sbuf_tensor("ctm", [128, 8], F32))
            mt1 = cctx.enter_context(nc.sbuf_tensor("mt1", [128, 8], F32))
            lt1 = cctx.enter_context(nc.sbuf_tensor("lt1", [128, 8], F32))
            rsc = [cctx.enter_context(nc.sbuf_tensor(f"rsc{i}", [128, 1], F32))
                   for i in range(5)]
            grow = cctx.enter_context(nc.sbuf_tensor("grow", [128, 512], F32))
            tmpc = cctx.enter_context(nc.sbuf_tensor("tmpc", [128, 512], BF16))

            ld_rw = [p.dma("gpsimd", None, rwb[:, :], rwp[:, :]),
                     p.dma("gpsimd", None, sgwb[:, :], sgwp[:, :])]

            w1_free = [b_pe] * 2
            w2_free = [b_pe] * 2
            sgu_free = [b_pe] * 2
            sdd_free = [b_pe] * 2
            w1_st, w2_st, sgu_st, sdd_st = {}, {}, {}, {}
            w1_stslot = [None] * 2
            w2_stslot = [None] * 2
            sgu_stslot = [None] * 2
            sdd_stslot = [None] * 2
            y1bf_pe = b_pe
            routed_dve = [b_dve] * 4
            hs_act = [b_act] * 2
            hsil_rd = [b_dve] * 2
            cbc_rd = [b_dve] * 2
            gu_rd = [("dve", b_dve), ("dve", b_dve)]
            z2_dve = [0] * NC
            gbc_pe_prev = b_pe
            hgrp_w2_pe = b_pe

            for c in range(NC):
                pe_start = pe.n
                if c > 0:
                    p.wait("tensor", dve, z2_dve[c - 1])
                    p.wait("vector", pe, pe_start)
                    p.wait("scalar", dve, z2_dve[c - 1])
                p.wait("scalar", pe, y1bf_pe)
                for dt in range(8):
                    p.scopy(acs, y1bf[:, dt * 512: dt * 512 + 512],
                            y1f[:, dt * T + c * 512: dt * T + c * 512 + 512],
                            inc=(dt == 7))
                y1bf_act = acs.n
                p.wait("tensor", acs, y1bf_act)
                if c == 0:
                    p.waitt("tensor", ld_rw)
                p.wait("tensor", dve, routed_dve[3])
                for ts in range(4):
                    for dt in range(8):
                        p.mm(pe, P[7][:, ts * 8: ts * 8 + 8],
                             y1bf[:, dt * 512 + ts * 128: dt * 512 + ts * 128 + 128],
                             rwb[:, dt * 8: dt * 8 + 8],
                             start=(dt == 0), stop=(dt == 7), inc=(dt == 7))
                rt_pe = pe.n
                t_pe = 0
                for ts in range(4):
                    psl = P[7][:, ts * 8: ts * 8 + 8]
                    p.wait("vector", pe, rt_pe)
                    if ts > 0:
                        p.wait("vector", pe, t_pe)
                    p.vop(dve, "reduce_max", rsc[0][:, :], psl, axis=AX.X,
                          negate=True)
                    p.vop(dve, "reduce_max", rsc[1][:, :], psl, axis=AX.X,
                          inc=True)
                    p.wait("vector", dve, dve.n)
                    p.vop(dve, "tensor_single_scalar", mt1[:, :], psl,
                          rsc[1][:, :], ALU.is_ge, inc=True)
                    m1_dve = dve.n
                    p.wait("vector", dve, m1_dve)
                    p.wait("scalar", dve, m1_dve)
                    p.act(acs, probs[:, :], psl, AF.Exp, bias=rsc[0][:, :],
                          scale=1.0, accum=rsc[2][:, :], inc=True)
                    e_act = acs.n
                    p.wait("vector", acs, e_act)
                    p.vop(dve, "reciprocal", rsc[3][:, :], rsc[2][:, :], inc=True)
                    p.wait("vector", dve, dve.n)
                    p.vop(dve, "scalar_tensor_tensor", lt1[:, :], mt1[:, :],
                          neg30[:, 0:1], psl, ALU.mult, ALU.add, inc=True)
                    p.wait("vector", dve, dve.n)
                    p.vop(dve, "reduce_max", rsc[4][:, :], lt1[:, :], axis=AX.X,
                          inc=True)
                    p.wait("vector", dve, dve.n)
                    p.vop(dve, "tensor_single_scalar", mt1[:, :], psl,
                          rsc[4][:, :], ALU.is_ge, inc=True)
                    p.wait("vector", dve, dve.n)
                    p.vop(dve, "tensor_scalar", probs[:, :], probs[:, :],
                          rsc[3][:, :], None, ALU.mult, inc=True)
                    p.wait("vector", dve, dve.n)
                    p.vop(dve, "tensor_mul", ctm[:, :], probs[:, :], mt1[:, :],
                          inc=True)
                    c_dve = dve.n
                    for eb in range(2):
                        p.wait("tensor", dve, c_dve)
                        p.wait("tensor", acs, acs.n)
                        for e4 in range(4):
                            e = eb * 4 + e4
                            p.trans(pe, P[6][0:1, e4 * 128: e4 * 128 + 128],
                                    ctm[:, e:e + 1], idf[:, :], inc=(e4 == 3))
                        t_pe = pe.n
                        p.wait("scalar", pe, t_pe)
                        for e4 in range(4):
                            e = eb * 4 + e4
                            p.scopy(acs, comb[e][0:1, ts * 128: ts * 128 + 128],
                                    P[6][0:1, e4 * 128: e4 * 128 + 128],
                                    inc=(e4 == 3))
                comb_act = acs.n
                cbc_rd[0] = max(cbc_rd[0], dve.n)
                sgu_ld = {}
                sdd_ld = {}
                w1_lds2 = {}

                def w1load(s, c=c):
                    slot = s % 2
                    p.wait("scalar", pe, w1_free[slot])
                    if c == 0:
                        p.waitt("scalar", pf_st[("w1", s)])
                    w1_lds2[s] = p.dma("scalar", None, w1b[slot][:, :],
                                       w1c[s, :, :])
                w1load(0)
                w1load(1)

                def sgudma(q, c=c):
                    slot = q % 2
                    p.wait("scalar", pe, sgu_free[slot])
                    if c == 0:
                        p.waitt("scalar", pf_st[("sgu", q)])
                    sgu_ld[q] = p.dma("scalar", None, sgub[slot][:, :],
                                      sguc[q, :, :])

                def sdddma(dt, c=c):
                    slot = dt % 2
                    p.wait("sync", pe, sdd_free[slot])
                    if c == 0:
                        p.waitt("sync", pf_st[("sdd", dt)])
                    sdd_ld[dt] = p.dma("sync", None, sddb[slot][:, :],
                                       sddc[dt, :, :])

                for eg in range(2):
                    w2_ld = {}

                    def w2dma(dt, eg=eg, c=c):
                        slot = dt % 2
                        key = eg * 8 + dt
                        p.wait("sync", pe, w2_free[slot])
                        if c == 0:
                            p.waitt("sync", pf_st[("w2", key)])
                        w2_ld[dt] = p.dma("sync", None, w2b[slot][:, :],
                                          w2c[key, :, :])
                    w2dma(0)
                    w2dma(1)
                    if eg == 1:
                        sgudma(0)
                        sgudma(1)
                        sdddma(0)
                        sdddma(1)
                        p.wait("vector", pe, hgrp_w2_pe)
                    for ei in range(4):
                        e = 4 * eg + ei
                        cb = 6 + (e % 2)
                        p.wait("tensor", acs, comb_act)
                        p.wait("tensor", dve, cbc_rd[e % 2])
                        p.mm(pe, P[cb][:, :], onesb[0:1, 0:128],
                             comb[e][0:1, :], start=True, stop=True, inc=True)
                        cbc_mm = pe.n
                        p.wait("vector", pe, cbc_mm)
                        p.vop(dve, "tensor_copy",
                              cbcS[:, (e % 2) * 512:(e % 2) * 512 + 512],
                              P[cb][:, :], inc=True)
                        cbc_rd[e % 2] = dve.n
                        for fh in range(2):
                            slot = (2 * e + fh) % 2
                            p.waitt("tensor", w1_lds2[2 * e + fh])
                            for ft in range(4):
                                f = fh * 4 + ft
                                hb = 4 + (f % 2)
                                p.wait("tensor", acs, hs_act[f % 2])
                                for dt in range(8):
                                    p.mm(pe, P[hb][:, :],
                                         w1b[slot][:, dt * 512 + ft * 128:
                                                   dt * 512 + ft * 128 + 128],
                                         y1bf[:, dt * 512: dt * 512 + 512],
                                         start=(dt == 0), stop=(dt == 7),
                                         inc=(dt == 7))
                                h_pe = pe.n
                                p.wait("scalar", pe, h_pe)
                                p.wait("scalar", dve, hsil_rd[f % 2])
                                hs = hsil[:, (f % 2) * 512:(f % 2) * 512 + 512]
                                p.act(acs, hs, P[hb][:, :], AF.Silu, inc=True)
                                hs_act[f % 2] = acs.n
                                p.wait("vector", acs, hs_act[f % 2])
                                p.vop(dve, "tensor_mul",
                                      hgrp[:, (ei * 8 + f) * 512:
                                           (ei * 8 + f) * 512 + 512],
                                      hs,
                                      cbcS[:, (e % 2) * 512:
                                           (e % 2) * 512 + 512], inc=True)
                                hsil_rd[f % 2] = dve.n
                            w1_free[slot] = pe.n
                            if 2 * e + fh + 2 < 16:
                                w1load(2 * e + fh + 2)
                    hg_dve = dve.n
                    for dt in range(8):
                        slot = dt % 2
                        bk = dt % 4
                        p.waitt("tensor", w2_ld[dt])
                        if dt == 0:
                            p.wait("tensor", dve, hg_dve)
                        p.wait("tensor", dve, routed_dve[bk])
                        for kt in range(32):
                            p.mm(pe, P[bk][:, :],
                                 w2b[slot][:, kt * 128: kt * 128 + 128],
                                 hgrp[:, kt * 512: kt * 512 + 512],
                                 start=(kt == 0), stop=(kt == 31), inc=(kt == 31))
                        w2_free[slot] = pe.n
                        if dt + 2 < 8:
                            w2dma(dt + 2)
                        r_pe = pe.n
                        p.wait("vector", pe, r_pe)
                        asl = accs[:, dt * 512: dt * 512 + 512]
                        ysl = y1f[:, dt * T + c * 512: dt * T + c * 512 + 512]
                        if eg == 0:
                            p.vop(dve, "tensor_add", asl, P[bk][:, :], ysl,
                                  inc=True)
                        else:
                            p.vop(dve, "tensor_add", ysl, P[bk][:, :], asl,
                                  inc=True)
                        routed_dve[bk] = dve.n
                    hgrp_w2_pe = pe.n

                p.wait("tensor", dve, routed_dve[0])
                p.wait("tensor", acs, hs_act[0])
                p.wait("tensor", pe, gbc_pe_prev)
                for dt in range(8):
                    p.mm(pe, P[4][0:1, :], sgwb[:, dt:dt + 1],
                         y1bf[:, dt * 512: dt * 512 + 512],
                         start=(dt == 0), stop=(dt == 7), inc=(dt == 7))
                gw_pe = pe.n
                p.wait("scalar", pe, gw_pe)
                p.scopy(acs, grow[0:1, :], P[4][0:1, :], inc=True)
                p.wait("scalar", acs, acs.n)
                p.act(acs, grow[0:1, :], grow[0:1, :], AF.Sigmoid, inc=True)
                g_act = acs.n
                p.wait("tensor", acs, g_act)
                p.wait("tensor", acs, hs_act[1])
                p.mm(pe, P[5][:, :], onesf[0:1, 0:128], grow[0:1, :],
                     start=True, stop=True, inc=True)
                gbc_pe = pe.n
                gbc_pe_prev = gbc_pe
                sg_act = acs.n
                for q in range(4):
                    slot = q % 2
                    p.waitt("tensor", sgu_ld[q])
                    for ft2 in range(4):
                        f2 = q * 4 + ft2
                        par = f2 % 2
                        gb = 4 if par == 0 else 6
                        if f2 == 0:
                            p.wait("tensor", acs, g_act)
                        if f2 == 1:
                            p.wait("tensor", dve, cbc_rd[0])
                        kind, val = gu_rd[par]
                        if kind == "acs":
                            p.wait("tensor", acs, val)
                        else:
                            p.wait("tensor", dve, val)
                        for dt in range(8):
                            p.mm(pe, P[gb][:, :],
                                 sgub[slot][:, dt * 512 + ft2 * 128:
                                            dt * 512 + ft2 * 128 + 128],
                                 y1bf[:, dt * 512: dt * 512 + 512],
                                 start=(dt == 0), stop=(dt == 7), inc=(dt == 7))
                        gu_pe = pe.n
                        if q < 2:
                            p.wait("scalar", pe, gu_pe)
                            p.act(acs, sgb[:, f2 * 512: f2 * 512 + 512],
                                  P[gb][:, :], AF.Silu, inc=True)
                            sg_act = acs.n
                            gu_rd[par] = ("acs", acs.n)
                        else:
                            fu = f2 - 8
                            p.wait("vector", pe, gu_pe)
                            p.wait("vector", pe, gbc_pe)
                            p.wait("vector", acs, sg_act)
                            p.vop(dve, "tensor_mul", tmpc[:, :], P[gb][:, :],
                                  sgb[:, fu * 512: fu * 512 + 512])
                            p.vop(dve, "tensor_mul",
                                  spre[:, fu * 512: fu * 512 + 512],
                                  tmpc[:, :], P[5][:, :], inc=True)
                            gu_rd[par] = ("dve", dve.n)
                    sgu_free[slot] = pe.n
                    if q + 2 < 4:
                        sgudma(q + 2)
                sp_dve = dve.n
                for dt in range(8):
                    slot = dt % 2
                    bk = dt % 4
                    p.waitt("tensor", sdd_ld[dt])
                    if dt == 0:
                        p.wait("tensor", dve, sp_dve)
                    p.wait("tensor", dve, routed_dve[bk])
                    for ft in range(8):
                        p.mm(pe, P[bk][:, :],
                             sddb[slot][:, ft * 128: ft * 128 + 128],
                             spre[:, ft * 512: ft * 512 + 512],
                             start=(ft == 0), stop=(ft == 7), inc=(ft == 7))
                    sdd_free[slot] = pe.n
                    if dt + 2 < 8:
                        sdddma(dt + 2)
                    d_pe = pe.n
                    p.wait("vector", pe, d_pe)
                    ysl = y1f[:, dt * T + c * 512: dt * T + c * 512 + 512]
                    p.vop(dve, "tensor_add", ysl, P[bk][:, :], ysl, inc=True)
                    routed_dve[bk] = dve.n
                z2_dve[c] = dve.n
                y1bf_pe = pe.n
            if "d_hg" in ddump:
                p.wait("sync", dve, dve.n)
                p.dma("sync", st, ddump["d_hg"][:, :], hgrp[:, :])
            if "d_comb" in ddump:
                p.wait("sync", acs, acs.n)
                for e in range(NE):
                    p.dma("sync", st, ddump["d_comb"][e:e + 1, :], comb[e][0:1, :])
            if "d_ctm" in ddump:
                p.wait("sync", dve, dve.n)
                p.dma("sync", st, ddump["d_ctm"][:, :], ctm[:, :])
                p.dma("sync", st, ddump["d_probs"][:, :], probs[:, :])
                p.dma("sync", st, ddump["d_mt1"][:, :], mt1[:, :])
            if "d_spre" in ddump:
                p.wait("sync", dve, dve.n)
                p.dma("sync", st, ddump["d_spre"][:, :], spre[:, :])
            if "d_accs" in ddump:
                p.dma("sync", st, ddump["d_accs"][:, :], accs[:, :])
            if "d_grow" in ddump:
                p.dma("sync", st, ddump["d_grow"][:, :], grow[0:1, :])
            if "d_z2" in ddump:
                p.wait("sync", dve, dve.n)
                p.dma("sync", st, ddump["d_z2"][:, :], y1f[:, :])
            p.run()

        with ExitStack() as dctx:
            ztm = [dctx.enter_context(nc.sbuf_tensor(f"ztm{i}", [128, 1024], F32))
                   for i in range(2)]
            y2s = [dctx.enter_context(nc.sbuf_tensor(f"y2s{i}", [128, 1024], F32))
                   for i in range(2)]
            sqs = dctx.enter_context(nc.sbuf_tensor("sqs", [128, 1024], BF16))
            dS = dctx.enter_context(nc.sbuf_tensor("dS", [128, 1], F32))
            dS2 = dctx.enter_context(nc.sbuf_tensor("dS2", [128, 1], F32))
            dm = dctx.enter_context(nc.sbuf_tensor("dm", [128, 1], F32))
            dv2 = dctx.enter_context(nc.sbuf_tensor("dv2", [128, 1], F32))
            dr = dctx.enter_context(nc.sbuf_tensor("dr", [128, 1], F32))

            tile_st = [0, 0]
            tr_free = [0, 0]
            for tt in range(NTT):
                sl = tt % 2
                for dt in range(8):
                    bki = dt // 4
                    if tt >= 1 and dt % 4 == 0:
                        p.wait("tensor", acs, tr_free[bki])
                    p.trans(pe, P[4 + bki][:, (dt % 4) * 128:(dt % 4) * 128 + 128],
                            y1f[:, dt * T + tt * 128: dt * T + tt * 128 + 128],
                            idf[:, :], inc=(dt % 4 == 3))
                tr_pe = pe.n
                p.wait("scalar", pe, tr_pe)
                p.wait("scalar", st, tile_st[sl])
                p.scopy(acs, ztm[sl][:, 0:512], P[4][:, :])
                p.scopy(acs, ztm[sl][:, 512:1024], P[5][:, :], inc=True)
                tr_free[0] = acs.n
                tr_free[1] = acs.n
                zt_act = acs.n
                p.wait("vector", acs, zt_act)
                p.vop(dve, "reduce_sum", dS[:, :], ztm[sl][:, :], axis=AX.X,
                      inc=True)
                p.wait("scalar", dve, dve.n)
                p.act(acs, sqs[:, :], ztm[sl][:, :], AF.Square, accum=dS2[:, :],
                      inc=True)
                sq_act = acs.n
                p.wait("vector", acs, sq_act)
                p.vop(dve, "tensor_scalar", dm[:, :], dS[:, :], 1.0 / D, None,
                      ALU.mult)
                p.vop(dve, "tensor_scalar", dv2[:, :], dS2[:, :], 1.0 / D, None,
                      ALU.mult, inc=True)
                p.wait("vector", dve, dve.n)
                p.vop(dve, "tensor_mul", dr[:, :], dm[:, :], dm[:, :], inc=True)
                p.wait("vector", dve, dve.n)
                p.vop(dve, "tensor_sub", dv2[:, :], dv2[:, :], dr[:, :], inc=True)
                p.wait("scalar", dve, dve.n)
                p.act(acs, dr[:, :], dv2[:, :], AF.Ln, bias=epsb[:, :], scale=1.0,
                      inc=True)
                p.wait("scalar", acs, acs.n)
                p.act(acs, dr[:, :], dr[:, :], AF.Exp, scale=-0.5, inc=True)
                r_act = acs.n
                p.wait("vector", acs, r_act)
                p.vop(dve, "tensor_scalar", y2s[sl][:, :], ztm[sl][:, :],
                      dm[:, :], dr[:, :], ALU.subtract, ALU.mult, inc=True)
                y2_dve = dve.n
                p.wait("sync", dve, y2_dve)
                p.dma("sync", st, out[tt * 128:(tt + 1) * 128, :], y2s[sl][:, :])
                tile_st[sl] = st.n
            p.wait("sync", st, st.n)
            p.run()
    return nc


_cache = {}


def _host_prep(L0_attn_Wq, L0_attn_Wk, L0_attn_Wv, L0_attn_Wo, L0_ffn_router_w,
               L0_ffn_w1, L0_ffn_w2, L0_ffn_shared_gate_up, L0_ffn_shared_down,
               L0_ffn_shared_gate_w):
    f = np.float32
    perm_e = np.array([64 * h + 2 * j for h in range(H) for j in range(32)])
    perm_o = perm_e + 1
    Wq = np.asarray(L0_attn_Wq, f)
    Wk = np.asarray(L0_attn_Wk, f)
    d = {
        "wqet": np.ascontiguousarray(Wq[perm_e].T),
        "wqot": np.ascontiguousarray(Wq[perm_o].T),
        "wket": np.ascontiguousarray(Wk[perm_e].T),
        "wkot": np.ascontiguousarray(Wk[perm_o].T),
        "wvt": np.ascontiguousarray(np.asarray(L0_attn_Wv, f).T),
        "wotq": np.ascontiguousarray(
            np.asarray(L0_attn_Wo, f).T.reshape(D, 4, 256).transpose(1, 0, 2)),
        "rwp": np.ascontiguousarray(
            np.asarray(L0_ffn_router_w, f).reshape(8, 128, 8)
            .transpose(1, 0, 2).reshape(128, 64)),
        "w1h": np.ascontiguousarray(
            np.asarray(L0_ffn_w1, f).reshape(NE, D, 2, 512).transpose(0, 2, 1, 3)),
        "w2d": np.ascontiguousarray(
            np.asarray(L0_ffn_w2, f).reshape(NE * D, 8, 128).transpose(1, 0, 2)),
        "sguq": np.ascontiguousarray(
            np.asarray(L0_ffn_shared_gate_up, f).reshape(D, 4, 512)
            .transpose(1, 0, 2)),
        "sdd": np.ascontiguousarray(
            np.asarray(L0_ffn_shared_down, f).reshape(D, 8, 128)
            .transpose(1, 0, 2)),
        "sgwp": np.ascontiguousarray(
            np.asarray(L0_ffn_shared_gate_w, f).reshape(8, 128).T),
        "idfd": np.eye(128, dtype=f),
        "idbd": np.eye(128, dtype=f).astype(BF),
    }
    j = np.arange(32, dtype=np.float64)
    invf = 10000.0 ** (-j / 32.0)
    t = np.arange(T, dtype=np.float64)
    ang = t[None, :] * invf[:, None]
    d["cosd"] = np.ascontiguousarray(np.tile(np.cos(ang), (4, 1))).astype(BF)
    d["sind"] = np.ascontiguousarray(np.tile(np.sin(ang), (4, 1))).astype(BF)
    pp, jj = np.mgrid[0:128, 0:896]
    d["maskd"] = (jj >= pp + 384).astype(f).astype(BF)
    return d


def kernel(L0_attn_x, L0_attn_Wq, L0_attn_Wk, L0_attn_Wv, L0_attn_Wo,
           L0_ffn_router_w, L0_ffn_w1, L0_ffn_w2, L0_ffn_shared_gate_up,
           L0_ffn_shared_down, L0_ffn_shared_gate_w, _dumps=()):
    shared = _host_prep(L0_attn_Wq, L0_attn_Wk, L0_attn_Wv, L0_attn_Wo,
                        L0_ffn_router_w, L0_ffn_w1, L0_ffn_w2,
                        L0_ffn_shared_gate_up, L0_ffn_shared_down,
                        L0_ffn_shared_gate_w)
    x = np.asarray(L0_attn_x, np.float32)
    key = tuple(sorted(_dumps))
    if key not in _cache:
        _cache[key] = build_program(dumps=_dumps)
    nc = _cache[key]
    in_maps = []
    for cidx in range(8):
        m = dict(shared)
        m["xT"] = np.ascontiguousarray(x[cidx].T)
        in_maps.append(m)
    res = run_bass_kernel_spmd(nc, in_maps, core_ids=list(range(8)))
    outs = np.stack([np.asarray(res.results[i]["out"]) for i in range(8)], axis=0)
    if _dumps:
        return outs, res.results
    return outs
